# version 07
# baseline (speedup 1.0000x reference)
import numpy as np
import concourse.bass as bass
import concourse.mybir as mybir
from concourse.bass_utils import run_bass_kernel_spmd
from concourse.bass_core_rust import IndirectOffsetOnAxis

F32 = mybir.dt.float32
U32 = mybir.dt.uint32
AF = mybir.ActivationFunctionType
ALU = mybir.AluOpType
AX = mybir.AxisListType

B, T, D, H, DK, E = 2, 8192, 1024, 4, 256, 8
N = B * T
CAP = 2458
CAPP = 2560
TT = 512
NTT = T // TT
SUB = TT // 128
CH = 128
NCH = TT // CH
NGRP = N // 128
MYTOK = 2048
BIG = 1.0e30
NEG = -1.0e30

SKIP_SC = False
SCAN_BAR = 0
TRACE_SC = False
TRUNC = None
DUMPS = set()

RG = [[0, 1, 2, 3, 4, 5, 6, 7]]



_NOWRAP = {"wait_ge", "dma_start", "indirect_dma_start", "collective_compute",
           "nop", "engine_nop", "drain", "sem_inc", "register", "alloc_register",
           "reg_mov", "free_register", "wait_op"}


class _EngProxy:

    def __init__(self, eng):
        self._e = eng

    def __getattr__(self, n):
        a = getattr(self._e, n)
        if not callable(a) or n.startswith("_") or n in _NOWRAP:
            return a

        def call(*args, **kw):
            r = a(*args, **kw)
            try:
                out = args[0]
                apl = out.ap
                nfree = 1
                for st_, ct_ in apl[1:]:
                    nfree *= ct_
                if nfree < 128:
                    e2 = self._e
                    for _ in range(2):
                        if hasattr(e2, "engine_nop"):
                            e2.engine_nop()
                        else:
                            e2.nop()
            except Exception:
                pass
            return r
        return call


class Prog:
    ENGS = ("t", "v", "s", "g", "gd", "q")

    def __init__(self):
        self.steps = []
        self.tagval = {}
        self.cnt = {e: 0 for e in self.ENGS}
        self.default_waits = []
        self._last_gd = None
        self._gd_uid = 0

    def on(self, eng, fn, waits=(), tag=None, n_dma=0):
        waits = list(waits) + list(self.default_waits)
        if eng == "g" and n_dma:
            eng = "gd"
            if tag is None:
                self._gd_uid += 1
                tag = f"_gd{self._gd_uid}"
            if self._last_gd is not None:
                waits = waits + [self._last_gd]
            self._last_gd = tag
        self.steps.append([eng, fn, waits, tag, n_dma])

    def barrier(self, name):
        self.steps.append(["BAR", None, [], name, 0])

    def resolve(self):
        last_step = {e: None for e in self.ENGS}
        for st in self.steps:
            eng, fn, waits, tag, n_dma = st
            if eng == "BAR":
                snap = {}
                for e in self.ENGS:
                    ls = last_step[e]
                    if ls is not None and ls[3] is None and ls[4] == 0:
                        ls[3] = f"_auto_{id(ls)}"
                        self.cnt[e] += 1
                        self.tagval[ls[3]] = (e, self.cnt[e], False)
                    snap[e] = self.cnt[e]
                self.tagval[tag] = snap
                continue
            if n_dma:
                self.cnt[eng] += 16 * n_dma
                if tag:
                    self.tagval[tag] = (eng, self.cnt[eng], True)
            elif tag:
                self.cnt[eng] += 1
                self.tagval[tag] = (eng, self.cnt[eng], False)
            last_step[eng] = st

    def emit(self, eng_name, eng_obj, sems):
        waited = {e: 0 for e in self.ENGS}
        streams = ("g", "gd") if eng_name == "g" else (eng_name,)
        for st in self.steps:
            eng, fn, waits, tag, n_dma = st
            if eng == "BAR":
                bar = self.tagval[tag]
                continue
            if eng not in streams:
                continue
            for w in waits:
                tv = self.tagval[w]
                if isinstance(tv, dict):
                    for e2 in self.ENGS:
                        val = tv[e2]
                        if val > waited[e2] and val > 0:
                            eng_obj.wait_ge(sems[e2], val)
                            waited[e2] = val
                else:
                    e2, val, isdma = tv
                    if e2 == eng_name and not isdma:
                        continue
                    if val > waited[e2]:
                        eng_obj.wait_ge(sems[e2], val)
                        waited[e2] = val
            insts = fn(_EngProxy(eng_obj) if eng_name in ("v", "s", "g") else eng_obj)
            if n_dma:
                assert isinstance(insts, (list, tuple)) and len(insts) == n_dma, \
                    f"step {tag}: expected {n_dma} dmas, fn returned {insts}"
                for i in insts:
                    i.then_inc(sems[eng], 16)
            elif tag:
                assert insts is not None, f"step {tag}: fn returned None but tagged"
                last = insts[-1] if isinstance(insts, (list, tuple)) else insts
                last.then_inc(sems[eng], 1)


def build(core_consts):
    nc = bass.Bass("TRN2", target_bir_lowering=False, detect_race_conditions=False)

    def inp(name, shape, dtype=F32):
        return nc.dram_tensor(name, shape, dtype, kind="ExternalInput").ap()

    x0 = inp("x0", [T, D])
    x1 = inp("x1", [T, D])
    xmine = inp("xmine", [T, D])
    xmy = inp("xmy", [MYTOK, D])
    wq_t = inp("wq_t", [D, DK])
    wk_t = inp("wk_t", [D, DK])
    wv_t = inp("wv_t", [D, DK])
    wbz = inp("wbz", [D, 9])
    convw = inp("convw", [3 * DK, 4])
    firw = inp("firw", [DK, 56])
    gw1t = inp("gw1t", [1048, 2048])
    gb1 = inp("gb1", [2048, 1])
    gw2t = inp("gw2t", [2048, 24])
    gb2 = inp("gb2", [1, 24])
    alpha24 = inp("alpha24", [1, 24])
    gltemp = inp("gltemp", [1, 4])
    onw = inp("onw", [1, DK])
    wot = inp("wot", [D, D])
    brr = inp("brr", [1, 8])
    w1e = inp("w1e", [D, 4096])
    w2e = inp("w2e", [4096, D])
    b1e = inp("b1e", [4096, 1])
    b2e = inp("b2e", [D, 1])
    c_ident = inp("c_ident", [128, 128])
    c_masklow = inp("c_masklow", [128, 128])
    c_maskupd = inp("c_maskupd", [128, 128])
    c_trils = inp("c_trils", [128, 128])
    c_ones_col = inp("c_ones_col", [128, 1])
    c_ones_row = inp("c_ones_row", [1, 128])
    c_iota8 = inp("c_iota8", [128, 8])
    c_iota256 = inp("c_iota256", [128, 256])
    c_gtok = inp("c_gtok", [128, 256])
    c_myexp = inp("c_myexp", [1, 1])
    c_dump = inp("c_dump", [128, 256])
    c_myrows = inp("c_myrows", [128, 16], U32)
    c_stat_rows = inp("c_stat_rows", [128, 64], U32)
    c_probs_rows = inp("c_probs_rows", [128, 64], U32)
    c_o_rows = inp("c_o_rows", [128, 64], U32)

    out_slice = nc.dram_tensor("out_slice", [MYTOK, D], F32, kind="ExternalOutput").ap()

    def idram(name, shape, dtype=F32):
        kind = "ExternalOutput" if name in DUMPS else "Internal"
        return nc.dram_tensor(name, shape, dtype, kind=kind).ap()

    xn_tm_d = idram("xn_tm_d", [N + 1, D])
    br_d = idram("br_d", [6, T, DK])
    disp_d = idram("disp_d", [CAPP + 32768, 1], U32)
    xg_fm_d = idram("xg_fm_d", [D, CAPP])
    rts_d = idram("rts_d", [N, 4])
    outfm_d = idram("outfm_d", [D, CAPP])
    zdump_d = idram("zdump_d", [128, 128 * 8])
    trace_d = idram("trace_d", [64, 128, 1792])
    routdump_d = idram("routdump_d", [24, 128, 256])

    stats_in = idram("stats_in", [T, 6])
    stats_all = nc.dram_tensor("stats_all", [8, T, 6], F32, addr_space="Shared").ap()
    probs_in = idram("probs_in", [MYTOK, 24])
    probs_all = nc.dram_tensor("probs_all", [8, MYTOK, 24], F32, addr_space="Shared").ap()
    o_in = idram("o_in", [T, DK])
    o_all = nc.dram_tensor("o_all", [4, 8, MYTOK, DK], F32, addr_space="Shared").ap()
    ffn_in = idram("ffn_in", [CAPP, D])
    ffn_all = nc.dram_tensor("ffn_all", [5, 8, 512, D], F32, addr_space="Shared").ap()

    prog = Prog()
    build_phases(nc, prog, locals())

    prog.resolve()

    with (
        nc.semaphore("semt") as semt,
        nc.semaphore("semv") as semv,
        nc.semaphore("sems_") as sems_,
        nc.semaphore("semg") as semg,
        nc.semaphore("semgd") as semgd,
        nc.semaphore("semq") as semq,
        nc.Block() as block,
    ):
        sems = {"t": semt, "v": semv, "s": sems_, "g": semg, "gd": semgd, "q": semq}

        @block.tensor
        def _(t):
            prog.emit("t", t, sems)

        @block.vector
        def _(v):
            prog.emit("v", v, sems)

        @block.scalar
        def _(s):
            prog.emit("s", s, sems)

        @block.gpsimd
        def _(g):
            prog.emit("g", g, sems)

        @block.sync
        def _(q):
            prog.emit("q", q, sems)

    return nc



def build_phases(nc, prog, io):
    ns = dict(io)

    def A(name):
        return ns[name]

    sb = {}

    def alloc(name, shape, dtype=F32):
        sb[name] = nc.alloc_sbuf_tensor(name, shape, dtype).ap()
        return sb[name]

    ident = alloc("ident", [128, 128])
    masklow = alloc("masklow", [128, 128])
    maskupd = alloc("maskupd", [128, 128])
    trils = alloc("trils", [128, 128])
    ones_col = alloc("ones_col", [128, 1])
    ones_row = alloc("ones_row", [1, 128])
    iota8 = alloc("iota8", [128, 8])
    iota256 = alloc("iota256", [128, 256])
    gtok = alloc("gtok", [128, 256])
    myexp = alloc("myexp", [1, 1])
    wq_sb = alloc("wq_sb", [128, 8, 256])
    wk_sb = alloc("wk_sb", [128, 8, 256])
    wv_sb = alloc("wv_sb", [128, 8, 256])
    wbz_sb = alloc("wbz_sb", [128, 8, 9])
    convw_sb = alloc("convw_sb", [128, 6, 4])
    firw_sb = alloc("firw_sb", [128, 2, 56])
    gw2t_sb = alloc("gw2t_sb", [128, 16, 24])
    gb1_sb = alloc("gb1_sb", [128, 16])
    gb2_row = alloc("gb2_row", [1, 24])
    alpha_row = alloc("alpha_row", [1, 24])
    gltemp_row = alloc("gltemp_row", [1, 4])
    brr_row = alloc("brr_row", [1, 8])
    onw_row = alloc("onw_row", [1, 256])
    onw_b = alloc("onw_b", [128, 256])
    z_sb = alloc("z_sb", [128, 128, 8])
    beta_all = alloc("beta_all", [128, 64])
    stats_sb = alloc("stats_sb", [128, 64, 6])
    S_sb = alloc("S_sb", [128, 2, 256])
    trash = alloc("trash", [128, 1024])
    b2b = alloc("b2b", [128, 24])
    alphab = alloc("alphab", [128, 24])
    itempb = alloc("itempb", [128, 24])
    brrb = alloc("brrb", [128, 8])
    tmp4 = alloc("tmp4", [1, 4])
    cbias = alloc("cbias", [128, 3])
    myexpb = alloc("myexpb", [128, 1])
    th_row = alloc("th_row", [1, 8])
    lo_row = alloc("lo_row", [1, 8])
    hi_row = alloc("hi_row", [1, 8])
    cnt_row = alloc("cnt_row", [1, 8])
    slb_row = alloc("slb_row", [1, 8])
    disp_pre = alloc("disp_pre", [128, 20], U32)
    posu = alloc("posu", [128, 256], U32)
    sgidx = alloc("sgidx", [128, 20], U32)
    myrows_u = alloc("myrows_u", [128, 16], U32)
    statrows_u = alloc("statrows_u", [128, 64], U32)
    probrows_u = alloc("probrows_u", [128, 64], U32)
    orows_u = alloc("orows_u", [128, 64], U32)
    gidx0_u = alloc("gidx0_u", [128, 16], U32)
    gidx1_u = alloc("gidx1_u", [128, 16], U32)
    gtoku = alloc("gtoku", [128, 256], U32)
    dumpt = alloc("dumpt", [128, 256])
    zer256 = alloc("zer256", [128, 256])

    SCRW = 33000
    scr_t = nc.alloc_sbuf_tensor("scratch", [128, SCRW], F32)
    scr = scr_t.ap()

    def view(off, shape):
        n = 1
        for d_ in shape[1:]:
            n *= d_
        assert off + n <= SCRW, f"scratch overflow {off}+{n}"
        a = scr[:, off:off + n]
        if len(shape) > 2:
            lab = "abcdefg"[: len(shape) - 1]
            pat = f"p ({' '.join(lab)}) -> p {' '.join(lab)}"
            kw = {lab[i]: shape[i + 1] for i in range(1, len(shape) - 1)}
            a = a.rearrange(pat, **kw)
        return a

    pb = [nc.alloc_psum_tensor(f"pb{i}", [128, 512], F32).ap() for i in range(8)]

    bank_last = {i: "setup_done" for i in range(8)}

    def mmstep(bk, fn, waits=(), pfx="mm"):
        t_ = tag(pfx)
        prog.on("t", fn, waits=list(waits) + [bank_last[bk]], tag=t_)
        return t_

    def evstep(bk, fn, eng="v", waits=(), pfx="ev"):
        t_ = tag(pfx)
        prog.on(eng, fn, waits=waits, tag=t_)
        bank_last[bk] = t_
        return t_

    def stop_if(name):
        if TRUNC != name:
            return False
        if "zdump_d" in DUMPS:
            prog.on("q", lambda q: [q.dma_start(
                out=A("zdump_d"), in_=z_sb[:].rearrange("p a b -> p (a b)"))],
                waits=[name], n_dma=1, tag="zd")
        if "routdump_d" in DUMPS:
            rl = [("items_g", items_g), ("items_e", items_e), ("kept", keptI),
                  ("slot", slotI), ("thi", thiI), ("gidx", r1), ("gg", ggi0),
                  ("pos", posf)]
            for ri, (nm, ap_) in enumerate(rl):
                prog.on("q", lambda q, ri=ri, ap_=ap_: [q.dma_start(
                    out=A("routdump_d")[ri], in_=ap_)], waits=[name], n_dma=1,
                    tag=f"rd{ri}")
        prog.on("v", lambda v: v.memset(trash[:], 0.0), waits=[name], tag="fz")
        prog.on("q", lambda q: [q.dma_start(
            out=A("out_slice")[0:128, :], in_=trash[:, 0:1024])],
            waits=["fz"], n_dma=1, tag="fzw")
        prog.barrier("end")
        prog.on("q", lambda q: None, waits=["end"])
        return True


    uid = [0]

    def tag(pfx):
        uid[0] += 1
        return f"{pfx}_{uid[0]}"

    def bcast_row(row_ap, width, out_sb, bank, wname, waits=()):
        t1 = mmstep(bank, lambda t, r=row_ap, b=bank, w=width: t.matmul(
            pb[b][:, :w], ones_row, r, start=True, stop=True), waits=waits, pfx="bc")
        return evstep(bank, lambda v, o=out_sb, b=bank, w=width: v.tensor_copy(
            o[:, :w], pb[b][:, :w]), "v", [t1], "bce")

    def ld(dst, src):
        return lambda q: [q.dma_start(out=dst, in_=src)]

    setup_loads = [
        (ident, A("c_ident")), (masklow, A("c_masklow")), (maskupd, A("c_maskupd")),
        (trils, A("c_trils")), (ones_col, A("c_ones_col")), (ones_row, A("c_ones_row")),
        (iota8, A("c_iota8")), (iota256, A("c_iota256")), (gtok, A("c_gtok")),
        (myexp, A("c_myexp")),
        (wq_sb, A("wq_t").rearrange("(kc p) f -> p kc f", p=128)),
        (wk_sb, A("wk_t").rearrange("(kc p) f -> p kc f", p=128)),
        (wv_sb, A("wv_t").rearrange("(kc p) f -> p kc f", p=128)),
        (wbz_sb, A("wbz").rearrange("(kc p) f -> p kc f", p=128)),
        (convw_sb, A("convw").rearrange("(fc p) k -> p fc k", p=128)),
        (firw_sb, A("firw").rearrange("(dc p) k -> p dc k", p=128)),
        (gw2t_sb, A("gw2t").rearrange("(kc p) f -> p kc f", p=128)),

        (gb2_row, A("gb2")), (alpha_row, A("alpha24")), (gltemp_row, A("gltemp")),
        (onw_row, A("onw")),
        (brr_row, A("brr")),
        (dumpt, A("c_dump")),
        (myrows_u, A("c_myrows")), (statrows_u, A("c_stat_rows")),
        (probrows_u, A("c_probs_rows")), (orows_u, A("c_o_rows")),
    ]
    for i, (dst, src) in enumerate(setup_loads):
        prog.on("q", ld(dst[:] if hasattr(dst, "tensor") else dst, src),
                tag=f"sl_{i}", n_dma=1)
    prog.on("q", lambda q: [q.dma_start(
        out=gb1_sb[:], in_=A("gb1").rearrange("(kc p) one -> p (kc one)", p=128),
        allow_slow_non_contiguous=True)], tag="sl_gb1", n_dma=1)
    prog.barrier("setup_done")

    bcast_row(onw_row[0:1, :], 256, onw_b, 6, "onw", waits=["setup_done"])
    bcast_row(gb2_row[0:1, :], 24, b2b, 6, "b2b")
    bcast_row(alpha_row[0:1, :], 24, alphab, 6, "alpha")
    bcast_row(brr_row[0:1, :], 8, brrb, 6, "brr")
    bcast_row(myexp[0:1, 0:1], 1, myexpb, 6, "mye")
    prog.on("v", lambda v: v.memset(trash[:], 0.0), tag="zt")
    prog.on("q", lambda q: [q.dma_start(out=A("xn_tm_d")[N:N + 1, :],
                                        in_=trash[0:1, :])], waits=["zt"], n_dma=1, tag="zdum")
    prog.on("v", lambda v: v.memset(S_sb[:], 0.0), tag="sinit")
    prog.on("v", lambda v: v.memset(zer256[:], 0.0), tag="z256")

    def _cb(v):
        v.memset(cbias[:, 0:1], 1.0e-5)
        v.memset(cbias[:, 1:2], 1.0e-12)
        return v.memset(cbias[:, 2:3], 1.0)
    prog.on("v", _cb, tag="cbias")
    prog.on("s", lambda s: s.activation(tmp4[0:1, :], gltemp_row[0:1, :], AF.Exp),
            waits=["setup_done"], tag="t4a")
    prog.on("s", lambda s: s.activation(tmp4[0:1, :], tmp4[0:1, :], AF.Ln, bias=cbias[0:1, 2:3]),
            waits=["t4a", "cbias"], tag="t4b")
    prog.on("v", lambda v: v.tensor_scalar_add(tmp4[0:1, :], tmp4[0:1, :], 1.0e-4),
            waits=["t4b"], tag="t4c")
    prog.on("v", lambda v: v.reciprocal(tmp4[0:1, :], tmp4[0:1, :]), waits=["t4c"], tag="t4d")
    itemp_view = bass.AP(tmp4.tensor, tmp4.offset, [tmp4.ap[0], [1, 4], [0, 6]])
    bcast_row(itemp_view, 24, itempb, 7, "itemp", waits=["t4d"])
    prog.barrier("p0")
    prog.default_waits = ["p0"]
    if stop_if("p0"):
        return

    x_t = view(0, [128, SUB, 1024])
    xn_fm = view(4096, [128, 8, TT])
    mu_t = view(8192, [128, SUB])
    s2_t = view(8208, [128, SUB])
    sd_t = view(8224, [128, SUB])

    def emit_ln(xin_ap, it, pfx, ld_waits=()):
        tl = tag(pfx + "ld")
        prog.on("q", lambda q, a=xin_ap: [q.dma_start(
            out=x_t[:], in_=a.rearrange("(s p) d -> p s d", p=128))],
            waits=ld_waits, n_dma=1, tag=tl)
        t1 = tag(pfx + "mu")
        prog.on("v", lambda v: v.tensor_reduce(mu_t[:], x_t[:], AX.X, ALU.add),
                waits=[tl], tag=t1)
        sq = []
        for si in range(SUB):
            ts_ = tag(pfx + "sq")
            sq.append(ts_)
            prog.on("s", lambda s, si=si: s.activation(
                trash[:, 0:1024], x_t[:, si, :], AF.Square,
                accum_out=s2_t[:, si:si + 1]), waits=[tl], tag=ts_)
        t2 = tag(pfx + "st")

        def stats(v):
            v.tensor_scalar_mul(mu_t[:], mu_t[:], 1.0 / 1024.0)
            v.tensor_scalar_mul(s2_t[:], s2_t[:], 1.0 / 1024.0)
            v.tensor_tensor(sd_t[:], mu_t[:], mu_t[:], ALU.mult)
            return v.tensor_tensor(s2_t[:], s2_t[:], sd_t[:], ALU.subtract)
        prog.on("v", stats, waits=[t1] + sq, tag=t2)
        t3 = tag(pfx + "sd")
        prog.on("s", lambda s: s.activation(sd_t[:], s2_t[:], AF.Sqrt, bias=cbias[:, 0:1]),
                waits=[t2], tag=t3)
        t4 = tag(pfx + "rs")
        prog.on("v", lambda v: v.reciprocal(sd_t[:], sd_t[:]), waits=[t3], tag=t4)
        t5 = tag(pfx + "xn")

        def xn(v):
            r = None
            for si in range(SUB):
                r = v.tensor_scalar(x_t[:, si, :], x_t[:, si, :],
                                    mu_t[:, si:si + 1], sd_t[:, si:si + 1],
                                    ALU.subtract, ALU.mult)
            return r
        prog.on("v", xn, waits=[t4], tag=t5)
        return t5

    def emit_tr_fm(it, pfx, xn_tag):
        evs = []
        for kc in range(8):
            bk = 6 + (kc % 2)

            def mms(t, kc=kc, bk=bk):
                r = None
                for si in range(SUB):
                    r = t.transpose(pb[bk][:, si * 128:(si + 1) * 128],
                                    x_t[:, si, kc * 128:(kc + 1) * 128], ident)
                return r
            tm = mmstep(bk, mms, waits=[xn_tag], pfx=pfx + "trm")
            eng = "v" if kc % 2 == 0 else "s"
            if eng == "v":
                te = evstep(bk, lambda v, kc=kc, bk=bk: v.tensor_copy(
                    xn_fm[:, kc, :], pb[bk][:, :]), "v", [tm], pfx + "tre")
            else:
                te = evstep(bk, lambda s, kc=kc, bk=bk: s.copy(
                    xn_fm[:, kc, :], pb[bk][:, :]), "s", [tm], pfx + "tre")
            evs.append(te)
        return evs

    def emit_z(gi0, pfx, fm_tags):
        outs = []
        for si in range(SUB):
            def mms(t, si=si):
                r = None
                for kc in range(8):
                    r = t.matmul(pb[5][:, si * 8:si * 8 + 8],
                                 xn_fm[:, kc, si * 128:(si + 1) * 128],
                                 wbz_sb[:, kc, 0:8],
                                 start=(kc == 0), stop=(kc == 7))
                return r
            outs.append(mmstep(5, mms, waits=fm_tags, pfx=pfx + "zmm"))
        tz = evstep(5, lambda v, gi0=gi0: v.tensor_copy(
            z_sb[:, gi0:gi0 + SUB, :],
            pb[5][:, 0:SUB * 8].rearrange("p (a b) -> p a b", b=8)),
            "v", outs, pfx + "zev")
        return [tz]

    for bi in range(2):
        src = A("x0") if bi == 0 else A("x1")
        for it in range(NTT):
            pfx = f"al{bi}_{it}_"
            rows = src[it * TT:(it + 1) * TT, :]
            xn_tag = emit_ln(rows, it, pfx)
            prog.on("q", lambda q, bi=bi, it=it: [q.dma_start(
                out=A("xn_tm_d").rearrange("n d -> n d")[bi * T + it * TT: bi * T + (it + 1) * TT, :]
                    .rearrange("(s p) d -> p s d", p=128),
                in_=x_t[:])], waits=[xn_tag], n_dma=1, tag=tag(pfx + "xnw"))
            fm_tags = emit_tr_fm(it, pfx, xn_tag)
            emit_z(bi * 64 + it * SUB, pfx, fm_tags[-2:])
            bt = f"tb_al{bi}_{it}"
            prog.barrier(bt)
            prog.default_waits = [bt]
    prog.barrier("alite")
    prog.default_waits = ["alite"]
    if stop_if("alite"):
        return

    proj = view(8240, [128, 6, 515])
    vfir = view(11330, [128, 2, 542])
    fir_fm = view(12414, [128, 2, 512])
    brA = view(13438, [128, 4, 256])
    brB = view(14462, [128, 4, 256])
    v_tm = view(15486, [128, 4, 256])
    q_tm = view(16510, [128, 4, 256])
    k_tm = view(17534, [128, 4, 256])
    kn_tm = view(18558, [128, 4, 256])
    rhs_uw = view(19582, [128, 4, 512])
    m_a = view(21630, [128, 4, 128])
    m_b = view(22142, [128, 4, 128])
    n_a = view(22654, [128, 4, 128])
    n_b = view(23166, [128, 4, 128])
    invP = view(23678, [128, 4, 128])
    attnT = view(24190, [128, 4, 128])
    uw_all = view(24702, [128, 4, 512])
    wfm = view(26750, [128, 4, 256])
    gsc = view(27774, [128, 128])
    u_fin = view(27902, [128, 256])
    ssq_q = view(28158, [128, 4])
    ssq_k = view(28162, [128, 4])
    rn_q = view(28166, [128, 4])
    rn_k = view(28170, [128, 4])
    rs_q = view(28174, [128, 4])
    rs_k = view(28178, [128, 4])
    ccA = view(28200, [128, 512])
    ccB = view(28712, [128, 512])

    identb4 = bass.AP(ident.tensor, ident.offset,
                      [ident.ap[0], [0, 4], [1, 128]])


    FIROFF = {3: 0, 7: 3, 15: 10, 31: 25}

    def conv_taps(dst_ap, src_buf, fch, taps, woff, halo, pfx, waits):
        wsb, wslot = (convw_sb, fch) if taps == 4 else (firw_sb, fch)
        last = None
        tprev = tag(pfx + "c0")

        def t0(v):
            return v.tensor_scalar_mul(ccA[:], src_buf[:, fch, 0 + (halo - (taps - 1)):0 + (halo - (taps - 1)) + 512] if False else src_buf[:, fch, (halo - (taps - 1)):(halo - (taps - 1)) + 512],
                                       wsb[:, wslot, woff:woff + 1])
        prog.on("v", t0, waits=waits, tag=tprev)
        cur, nxt = ccA, ccB
        for j in range(1, taps):
            tj = tag(pfx + f"c{j}")
            dst = dst_ap if j == taps - 1 else nxt

            def tap(v, j=j, cur=cur, dst=dst):
                return v.scalar_tensor_tensor(
                    dst, src_buf[:, fch, (halo - (taps - 1) + j):(halo - (taps - 1) + j) + 512],
                    wsb[:, wslot, woff + j:woff + j + 1], cur, ALU.mult, ALU.add)
            prog.on("v", tap, waits=[tprev], tag=tj)
            tprev = tj
            cur, nxt = nxt, cur
        return tprev

    def transpose4(src_fn, dst_ap, bk, pfx, waits, eng="v"):
        tm = tag(pfx + "t4")

        def mms(t):
            r = None
            for si in range(SUB):
                r = t.transpose(pb[bk][:, si * 128:(si + 1) * 128], src_fn(si), ident)
            return r
        mmt = mmstep(bk, mms, waits=waits, pfx=pfx + "t4")
        if eng == "v":
            ev = evstep(bk, lambda v: v.tensor_copy(dst_ap, pb[bk][:]), "v", [mmt], pfx + "t4e")
        else:
            ev = evstep(bk, lambda s: s.copy(dst_ap, pb[bk][:]), "s", [mmt], pfx + "t4e")
        return ev

    prev_S = ("sinit",)
    for it in range(NTT):
        pfx = f"af{it}_"
        rows = A("xmine")[it * TT:(it + 1) * TT, :]
        xn_tag = emit_ln(rows, it, pfx)
        fm_tags = emit_tr_fm(it, pfx, xn_tag)
        fm_done = fm_tags[-2:]
        bmm = mmstep(6, lambda t: [t.matmul(pb[6][:, si:si + 1],
                                            xn_fm[:, kc, si * 128:(si + 1) * 128],
                                            wbz_sb[:, kc, 8:9],
                                            start=(kc == 0), stop=(kc == 7))
                                   for si in range(SUB) for kc in range(8)][-1],
                     waits=fm_done, pfx=pfx + "bmm")
        bev = evstep(6, lambda s, it=it: s.activation(
            beta_all[:, it * SUB:(it + 1) * SUB], pb[6][:, 0:SUB], AF.Sigmoid),
            "s", [bmm], pfx + "bev")
        pe = []
        for f in range(6):
            wsb = (wq_sb, wq_sb, wk_sb, wk_sb, wv_sb, wv_sb)[f]
            dc = f % 2
            bk = f % 6
            pm = mmstep(bk, lambda t, wsb=wsb, dc=dc, bk=bk: [
                t.matmul(pb[bk][:], wsb[:, kc, dc * 128:(dc + 1) * 128],
                         xn_fm[:, kc, :], start=(kc == 0), stop=(kc == 7))
                for kc in range(8)][-1], waits=fm_done, pfx=pfx + f"pj{f}")
            pe0 = []
            if it == 0:
                zi = tag(pfx + f"hz{f}")

                def zf(v, f=f):
                    r = v.memset(proj[:, f, 0:3], 0.0)
                    if f >= 4:
                        r = v.memset(vfir[:, f - 4, 0:30], 0.0)
                    return r
                prog.on("v", zf, tag=zi)
                pe0 = [zi]
            ev = evstep(bk, (lambda v, f=f, bk=bk: v.tensor_copy(proj[:, f, 3:515], pb[bk][:]))
                        if f % 2 == 0 else
                        (lambda s, f=f, bk=bk: s.copy(proj[:, f, 3:515], pb[bk][:])),
                        "v" if f % 2 == 0 else "s", [pm] + pe0, pfx + f"pje{f}")
            pe.append(ev)
        prev_sl = []
        qk_silu = []
        v_silu = []
        for f in range(6):
            cv = conv_taps(ccB[:], proj, f, 4, 0, 3, pfx + f"cv{f}_", [pe[f]] + prev_sl)
            hs = tag(pfx + f"hs{f}")
            prog.on("v", lambda v, f=f: v.tensor_copy(proj[:, f, 0:3], proj[:, f, 512:515]),
                    waits=[cv], tag=hs)
            sl = tag(pfx + f"sl{f}")
            if f < 4:
                prog.on("s", lambda s, f=f: s.activation(proj[:, f, 3:515], ccB[:], AF.Silu),
                        waits=[cv, hs], tag=sl)
                qk_silu.append(sl)
            else:
                prog.on("s", lambda s, f=f: s.activation(vfir[:, f - 4, 30:542], ccB[:], AF.Silu),
                        waits=[cv, hs], tag=sl)
                v_silu.append(sl)
            prev_sl = [sl]
        vt = []
        for dc in range(2):
            vt.append(transpose4(lambda si, dc=dc: vfir[:, dc, 30 + si * 128:30 + (si + 1) * 128],
                                 v_tm[:, :, dc * 128:(dc + 1) * 128], 7, pfx + f"vt{dc}",
                                 v_silu, "v"))
        st_v = []
        for si in range(SUB):
            t_ = tag(pfx + f"stv{si}")
            prog.on("s", lambda s, si=si, it=it: s.activation(
                trash[:, 0:256], v_tm[:, si, :], AF.Abs, scale=1.0 / 256.0,
                accum_out=stats_sb[:, it * SUB + si, 4:5]), waits=vt, tag=t_)
            st_v.append(t_)
        prog.on("q", lambda q, it=it: [q.dma_start(
            out=A("br_d")[4, it * TT:(it + 1) * TT, :].rearrange("(s p) d -> p s d", p=128),
            in_=v_tm[:])], waits=vt, n_dma=1, tag=tag(pfx + "vspill"))
        qt = []
        for dc in range(2):
            qt.append(transpose4(lambda si, dc=dc: proj[:, dc, 3 + si * 128:3 + (si + 1) * 128],
                                 q_tm[:, :, dc * 128:(dc + 1) * 128], 7, pfx + f"qt{dc}",
                                 qk_silu, "v"))
            qt.append(transpose4(lambda si, dc=dc: proj[:, 2 + dc, 3 + si * 128:3 + (si + 1) * 128],
                                 k_tm[:, :, dc * 128:(dc + 1) * 128], 7, pfx + f"kt{dc}",
                                 qk_silu, "s"))
        sq_tags = []
        for si in range(SUB):
            t1 = tag(pfx + f"sqq{si}")
            prog.on("s", lambda s, si=si: s.activation(
                trash[:, 0:256], q_tm[:, si, :], AF.Square,
                accum_out=ssq_q[:, si:si + 1]), waits=qt, tag=t1)
            t2 = tag(pfx + f"sqk{si}")
            prog.on("s", lambda s, si=si: s.activation(
                trash[:, 256:512], k_tm[:, si, :], AF.Square,
                accum_out=ssq_k[:, si:si + 1]), waits=qt, tag=t2)
            sq_tags += [t1, t2]
        trn = tag(pfx + "rn")

        def rnf(s):
            s.activation(rs_q[:], ssq_q[:], AF.Sqrt, bias=cbias[:, 1:2])
            return s.activation(rs_k[:], ssq_k[:], AF.Sqrt, bias=cbias[:, 1:2])
        prog.on("s", rnf, waits=sq_tags, tag=trn)
        trn1 = tag(pfx + "rn1")

        def rnr(v):
            v.reciprocal(rs_q[:], rs_q[:])
            return v.reciprocal(rs_k[:], rs_k[:])
        prog.on("v", rnr, waits=[trn], tag=trn1)
        trn2 = tag(pfx + "rn2")

        def rnc(s):
            s.activation(rn_q[:], rs_q[:], AF.Copy)
            return s.activation(rn_k[:], rs_k[:], AF.Copy)
        prog.on("s", rnc, waits=[trn1], tag=trn2)
        prep_rhs = []
        for si in range(SUB):
            t1 = tag(pfx + f"kn{si}")
            prog.on("v", lambda v, si=si: v.tensor_scalar_mul(
                kn_tm[:, si, :], k_tm[:, si, :], rn_k[:, si:si + 1]), waits=[trn2], tag=t1)
            t2 = tag(pfx + f"rh{si}")

            def rh(v, si=si, it=it):
                v.tensor_scalar_mul(rhs_uw[:, si, 0:256], v_tm[:, si, :],
                                    beta_all[:, it * SUB + si:it * SUB + si + 1])
                return v.tensor_scalar_mul(rhs_uw[:, si, 256:512], kn_tm[:, si, :],
                                           beta_all[:, it * SUB + si:it * SUB + si + 1])
            prog.on("v", rh, waits=[t1, bev] + vt, tag=t2)
            prep_rhs.append(t2)
        br_bufs = [brA, brB]
        fir_sp = []
        for bi_, kk in enumerate((3, 7, 15, 31)):
            bb = br_bufs[bi_ % 2]
            buf_free = [fir_sp[bi_ - 2]] if bi_ >= 2 else []
            fe = []
            for dc in range(2):
                cv = conv_taps(fir_fm[:, dc, :], vfir, dc, kk, FIROFF[kk], 30,
                               pfx + f"f{kk}_{dc}_", v_silu + fir_sp[-1:])
                fe.append(transpose4(lambda si, dc=dc: fir_fm[:, dc, si * 128:(si + 1) * 128],
                                     bb[:, :, dc * 128:(dc + 1) * 128], 7,
                                     pfx + f"ft{kk}{dc}", [cv] + buf_free, "v"))
            for si in range(SUB):
                prog.on("s", lambda s, si=si, it=it, bi_=bi_, bb=bb: s.activation(
                    trash[:, 0:256], bb[:, si, :], AF.Abs, scale=1.0 / 256.0,
                    accum_out=stats_sb[:, it * SUB + si, bi_:bi_ + 1]), waits=fe,
                    tag=tag(pfx + f"stf{kk}{si}"))
            sp = tag(pfx + f"fsp{kk}")
            prog.on("q", lambda q, it=it, bi_=bi_, bb=bb: [q.dma_start(
                out=A("br_d")[bi_, it * TT:(it + 1) * TT, :].rearrange("(s p) d -> p s d", p=128),
                in_=bb[:])], waits=fe, n_dma=1, tag=sp)
            fir_sp.append(sp)
        prog.on("v", lambda v: v.tensor_copy(vfir[:, :, 0:30], vfir[:, :, 512:542]),
                waits=fir_sp[-1:], tag=tag(pfx + "vhalo"))
        uw_tags = []
        wfm_tags = []
        at_tags = []
        n_tags = []
        m_tags = []
        for ci in range(NCH):
            cpfx = pfx + f"c{ci}_"
            if SCAN_BAR >= 2:
                prog.barrier(cpfx + "PBAR")
                prog.default_waits = [cpfx + "PBAR"]
            gm = mmstep(0, lambda t, ci=ci: [t.matmul(
                pb[0][:, 0:128], proj[:, 2 + dc, 3 + ci * 128:3 + (ci + 1) * 128],
                proj[:, 2 + dc, 3 + ci * 128:3 + (ci + 1) * 128],
                start=(dc == 0), stop=(dc == 1)) for dc in range(2)][-1],
                waits=qk_silu, pfx=cpfx + "G")
            g1 = evstep(0, lambda v, ci=ci: v.tensor_scalar_mul(
                gsc[:], pb[0][:, 0:128], rn_k[:, ci:ci + 1]), "v", [gm, trn2], cpfx + "g1")
            gt = mmstep(1, lambda t: t.transpose(pb[1][:, 0:128], gsc[:], ident),
                        waits=[g1], pfx=cpfx + "gt")
            gn = evstep(1, lambda v, ci=ci: v.tensor_scalar_mul(
                gsc[:], pb[1][:, 0:128], rn_k[:, ci:ci + 1]), "v", [gt], cpfx + "gn")
            tm_ = tag(cpfx + "M")
            prog.on("v", lambda v, ci=ci, it=it: v.scalar_tensor_tensor(
                m_a[:, ci, :], gsc[:], beta_all[:, it * SUB + ci:it * SUB + ci + 1],
                masklow[:], ALU.mult, ALU.mult), waits=[gn, bev], tag=tm_)
            m_tags.append(tm_)
            nt = mmstep(1, lambda t, ci=ci: t.transpose(pb[1][:, 0:128], m_a[:, ci, :], ident),
                        waits=[tm_], pfx=cpfx + "Nt")
            nev = evstep(1, lambda v, ci=ci: v.tensor_copy(n_a[:, ci, :], pb[1][:, 0:128]),
                         "v", [nt], cpfx + "Ne")
            n_tags.append(nev)
            am = mmstep(2, lambda t, ci=ci: [t.matmul(
                pb[2][:, 0:128], proj[:, 2 + dc, 3 + ci * 128:3 + (ci + 1) * 128],
                proj[:, 0 + dc, 3 + ci * 128:3 + (ci + 1) * 128],
                start=(dc == 0), stop=(dc == 1)) for dc in range(2)][-1],
                waits=qk_silu, pfx=cpfx + "at")
            ae = evstep(2, lambda v, ci=ci: v.scalar_tensor_tensor(
                attnT[:, ci, :], pb[2][:, 0:128], rn_k[:, ci:ci + 1], maskupd[:],
                ALU.mult, ALU.mult), "v", [am, trn2], cpfx + "ae")
            at_tags.append(ae)
        pi0 = tag(pfx + "pi0")
        prog.on("v", lambda v: v.tensor_tensor(
            invP[:], n_a[:], identb4, ALU.add), waits=n_tags, tag=pi0)
        mcur, ncur = m_a, n_a
        mnext, nnext = m_b, n_b
        pprev = pi0
        mprev, nprev = m_tags[-1], n_tags[-1]
        for lvl in range(1, 7):
            if SCAN_BAR >= 2:
                prog.barrier(pfx + f"LBAR{lvl}")
                prog.default_waits = [pfx + f"LBAR{lvl}"]
            mm_t = mmstep(3, lambda t, mcur=mcur, ncur=ncur: [t.matmul(
                pb[3][:, ci * 128:(ci + 1) * 128], ncur[:, ci, :], mcur[:, ci, :],
                start=True, stop=True) for ci in range(NCH)][-1],
                waits=[mprev, nprev], pfx=pfx + f"pw{lvl}m")
            me = evstep(3, lambda v, mnext=mnext: v.tensor_copy(mnext[:], pb[3][:]),
                        "v", [mm_t], pfx + f"pw{lvl}me")
            if lvl < 6:
                nn_t = mmstep(4, lambda t, mcur=mcur, ncur=ncur: [t.matmul(
                    pb[4][:, ci * 128:(ci + 1) * 128], mcur[:, ci, :], ncur[:, ci, :],
                    start=True, stop=True) for ci in range(NCH)][-1],
                    waits=[mprev, nprev], pfx=pfx + f"pw{lvl}n")
                ne = evstep(4, lambda s, nnext=nnext: s.copy(nnext[:], pb[4][:]),
                            "s", [nn_t], pfx + f"pw{lvl}ne")
            else:
                ne = me
            pu = mmstep(5, lambda t, mnext=mnext: [t.matmul(
                pb[5][:, ci * 128:(ci + 1) * 128], mnext[:, ci, :], invP[:, ci, :],
                start=True, stop=True) for ci in range(NCH)][-1],
                waits=[me, pprev], pfx=pfx + f"pu{lvl}")
            pa = evstep(5, lambda v: v.tensor_tensor(invP[:], invP[:], pb[5][:], ALU.add),
                        "v", [pu], pfx + f"pa{lvl}")
            pprev = pa
            mprev, nprev = me, ne
            mcur, ncur = mnext, nnext
            mnext, nnext = (m_a, n_a) if lvl % 2 == 1 else (m_b, n_b)
        for ci in range(NCH):
            cpfx = pfx + f"s{ci}_"
            if SCAN_BAR >= 2:
                prog.barrier(cpfx + "SBAR")
                prog.default_waits = [cpfx + "SBAR"]
            um = mmstep(0, lambda t, ci=ci: t.matmul(
                pb[0][:], invP[:, ci, :], rhs_uw[:, ci, :], start=True, stop=True),
                waits=[pprev, prep_rhs[ci]], pfx=cpfx + "uw")
            ue = evstep(0, lambda v, ci=ci: v.tensor_copy(uw_all[:, ci, :], pb[0][:]),
                        "v", [um], cpfx + "uwe")
            uw_tags.append(ue)
            wm = mmstep(1, lambda t, ci=ci: [t.matmul(
                pb[1][:, dc * 128:(dc + 1) * 128],
                rhs_uw[:, ci, 256 + dc * 128:256 + (dc + 1) * 128],
                invP[:, ci, :], start=True, stop=True) for dc in range(2)][-1],
                waits=[pprev, prep_rhs[ci]], pfx=cpfx + "wf")
            we = evstep(1, lambda s, ci=ci: s.copy(wfm[:, ci, :], pb[1][:, 0:256]),
                        "s", [wm], cpfx + "wfe")
            wfm_tags.append(we)
        for ci in range(NCH):
            cpfx = pfx + f"sc{ci}_"
            if SCAN_BAR >= 1:
                prog.barrier(cpfx + "BAR")
                prog.default_waits = [cpfx + "BAR"]
            uc = mmstep(2, lambda t, ci=ci: [t.matmul(
                pb[2][:, 0:256], wfm[:, ci, dc * 128:(dc + 1) * 128],
                S_sb[:, dc, :], start=(dc == 0), stop=(dc == 1)) for dc in range(2)][-1],
                waits=[wfm_tags[ci]] + list(prev_S), pfx=cpfx + "uc")
            uf = evstep(2, lambda v, ci=ci: v.tensor_tensor(
                u_fin[:], uw_all[:, ci, 0:256], pb[2][:, 0:256], ALU.subtract),
                "v", [uc, uw_tags[ci]], cpfx + "uf")
            om = mmstep(3, lambda t, ci=ci: [t.matmul(
                pb[3][:, 0:256], proj[:, 0 + dc, 3 + ci * 128:3 + (ci + 1) * 128],
                S_sb[:, dc, :], start=(dc == 0), stop=False) for dc in range(2)][-1] and
                t.matmul(pb[3][:, 0:256], attnT[:, ci, :], u_fin[:],
                         start=False, stop=True),
                waits=list(prev_S) + [at_tags[ci], uf], pfx=cpfx + "om")
            oe = evstep(3, lambda v, ci=ci: v.tensor_scalar_mul(
                brA[:, ci, :], pb[3][:, 0:256], rn_q[:, ci:ci + 1]), "v",
                [om, fir_sp[2]], cpfx + "oe")
            sm0 = mmstep(4, lambda t, ci=ci: t.matmul(
                pb[4][:, 0:256], kn_tm[:, ci, 0:128], u_fin[:], start=True, stop=True),
                waits=[uf], pfx=cpfx + "sm0")
            sm1 = mmstep(5, lambda t, ci=ci: t.matmul(
                pb[5][:, 0:256], kn_tm[:, ci, 128:256], u_fin[:], start=True, stop=True),
                waits=[uf], pfx=cpfx + "sm1")
            sa0 = evstep(4, lambda v: v.tensor_tensor(
                S_sb[:, 0, :], S_sb[:, 0, :], pb[4][:, 0:256], ALU.add), "v", [sm0], cpfx + "sa0")
            sa1 = evstep(5, lambda v: v.tensor_tensor(
                S_sb[:, 1, :], S_sb[:, 1, :], pb[5][:, 0:256], ALU.add), "v", [sm1], cpfx + "sa1")
            prev_S = (sa0, sa1)
            if TRACE_SC:
                gci = it * NCH + ci
                def trc(q, ci=ci, gci=gci):
                    return [q.dma_start(out=A("trace_d")[gci, :, 0:512],
                                        in_=S_sb[:].rearrange("p a b -> p (a b)")),
                            q.dma_start(out=A("trace_d")[gci, :, 512:768], in_=u_fin[:]),
                            q.dma_start(out=A("trace_d")[gci, :, 768:1280],
                                        in_=uw_all[:, ci, :]),
                            q.dma_start(out=A("trace_d")[gci, :, 1280:1536],
                                        in_=wfm[:, ci, :]),
                            q.dma_start(out=A("trace_d")[gci, :, 1536:1664],
                                        in_=invP[:, ci, :]),
                            q.dma_start(out=A("trace_d")[gci, :, 1664:1792],
                                        in_=attnT[:, ci, :])]
                prog.on("q", trc, waits=[sa1, oe], n_dma=6, tag=tag(cpfx + "trc"))
                prog.barrier(cpfx + "TBAR")
                prog.default_waits = [cpfx + "TBAR"]
            prog.on("s", lambda s, ci=ci, it=it: s.activation(
                trash[:, 0:256], brA[:, ci, :], AF.Abs, scale=1.0 / 256.0,
                accum_out=stats_sb[:, it * SUB + ci, 5:6]), waits=[oe],
                tag=tag(cpfx + "std"))
        prog.on("q", lambda q, it=it: [q.dma_start(
            out=A("br_d")[5, it * TT:(it + 1) * TT, :].rearrange("(s p) d -> p s d", p=128),
            in_=brA[:])], waits=list(prev_S), n_dma=1, tag=tag(pfx + "dsp"))
        bt = f"tb_af{it}"
        prog.barrier(bt)
        prog.default_waits = [bt]
    prog.barrier("afull")
    prog.default_waits = ["afull"]
    if stop_if("afull"):
        return
    prog.on("q", lambda q: [q.dma_start(
        out=A("stats_in").rearrange("(g p) f -> p g f", p=128),
        in_=stats_sb[:])], n_dma=1, tag="statw")

    items_g = view(0, [128, 256])
    items_e = view(256, [128, 256])
    eqs = view(512, [128, 8, 256])
    gme = view(2560, [128, 8, 256])
    work = view(4608, [128, 8, 256])
    work2 = view(6656, [128, 8, 256])
    scn = view(8704, [128, 8, 256])
    g1t = view(10752, [128, 128])
    id1t = view(10880, [128, 128])
    g2t = view(11008, [128, 128])
    id2t = view(11136, [128, 128])
    ebuf = view(11264, [128, 128, 8])
    ebuf2 = view(12288, [128, 128, 8])
    zred = view(13312, [128, 128])
    zred2 = view(13440, [128, 128])
    keptI = view(13568, [128, 256])
    slotI = view(13824, [128, 256])
    thiI = view(14080, [128, 256])
    gtI = view(14336, [128, 256])
    eqthI = view(14592, [128, 256])
    r1 = view(14848, [128, 256])
    r2 = view(15104, [128, 256])
    r3 = view(15360, [128, 256])
    posf = view(15616, [128, 256])
    ggi0 = view(15872, [128, 256])
    thb = view(16128, [128, 8])
    hib = view(16136, [128, 8])
    cpe = view(16144, [128, 8])
    offs = view(16152, [128, 8])
    tots = view(16160, [128, 8])
    slbb = view(16168, [128, 8])
    thsb = view(16176, [128, 8])
    rts_sb = view(16192, [128, 4, 128])
    posD = view(16704, [128, 1])

    def b8(a):
        return bass.AP(a.tensor, a.offset, [a.ap[0], a.ap[1], [0, 256]])

    def be(a):
        return bass.AP(a.tensor, a.offset, [a.ap[0], [0, 8], a.ap[1]])

    def swp(a3):
        return bass.AP(a3.tensor, a3.offset, [a3.ap[0], [1, 256], [256, 8]])

    def bg(a, n):
        return bass.AP(a.tensor, a.offset, [a.ap[0], [0, n], a.ap[1]])

    def vstep(fn, waits=(), pfx="r"):
        t_ = tag(pfx)
        prog.on("v", fn, waits=waits, tag=t_)
        return t_

    iota8b = bass.AP(iota8.tensor, iota8.offset, [iota8.ap[0], [0, 128], [1, 8]])
    t_ = vstep(lambda v: v.tensor_tensor(
        z_sb[:], z_sb[:], bg(brrb, 128), ALU.add), ["alite"], "rz")
    t_ = vstep(lambda v: v.tensor_reduce(zred[:], z_sb[:], AX.X, ALU.max), [t_], "rmax")
    t_ = vstep(lambda v: v.tensor_tensor(
        ebuf[:], z_sb[:], bass.AP(zred.tensor, zred.offset,
                                  [zred.ap[0], [1, 128], [0, 8]]), ALU.subtract), [t_], "rsub")
    t_ex = tag("rexp")
    prog.on("s", lambda s: s.activation(ebuf[:], ebuf[:], AF.Exp), waits=[t_], tag=t_ex)
    t_ = vstep(lambda v: v.tensor_reduce(zred2[:], ebuf[:], AX.X, ALU.add), [t_ex], "rsum")
    t_ = vstep(lambda v: v.reciprocal(zred2[:], zred2[:]), [t_], "rrec")
    tprob = vstep(lambda v: v.tensor_tensor(
        ebuf[:], ebuf[:], bass.AP(zred2.tensor, zred2.offset,
                                  [zred2.ap[0], [1, 128], [0, 8]]), ALU.mult), [t_], "rdiv")
    t_ = vstep(lambda v: v.tensor_reduce(g1t[:], ebuf[:], AX.X, ALU.max), [tprob], "tg1")
    teq = vstep(lambda v: v.tensor_tensor(
        ebuf2[:], ebuf[:], bass.AP(g1t.tensor, g1t.offset,
                                   [g1t.ap[0], [1, 128], [0, 8]]), ALU.is_equal), [t_], "teq")
    t_ = vstep(lambda v: v.tensor_scalar(ebuf2[:], ebuf2[:], -BIG, BIG, ALU.mult, ALU.add),
               [teq], "tc1")
    t_ = vstep(lambda v: v.tensor_tensor(ebuf2[:], ebuf2[:], iota8b, ALU.add), [t_], "tc2")
    tid1 = vstep(lambda v: v.tensor_reduce(id1t[:], ebuf2[:], AX.X, ALU.min), [t_], "tid1")
    tm1 = vstep(lambda v: v.tensor_tensor(
        ebuf2[:], iota8b, bass.AP(id1t.tensor, id1t.offset,
                                  [id1t.ap[0], [1, 128], [0, 8]]), ALU.is_equal), [tid1], "tm1")
    tp2 = vstep(lambda v: v.scalar_tensor_tensor(
        ebuf2[:], ebuf2[:], -BIG, ebuf[:], ALU.mult, ALU.add), [tm1], "tp2")
    t_ = vstep(lambda v: v.tensor_reduce(g2t[:], ebuf2[:], AX.X, ALU.max), [tp2], "tg2")
    teq2 = vstep(lambda v: v.tensor_tensor(
        ebuf2[:], ebuf2[:], bass.AP(g2t.tensor, g2t.offset,
                                    [g2t.ap[0], [1, 128], [0, 8]]), ALU.is_equal), [t_], "teq2")
    t_ = vstep(lambda v: v.tensor_scalar(ebuf2[:], ebuf2[:], -BIG, BIG, ALU.mult, ALU.add),
               [teq2], "tc3")
    t_ = vstep(lambda v: v.tensor_tensor(ebuf2[:], ebuf2[:], iota8b, ALU.add), [t_], "tc4")
    tid2 = vstep(lambda v: v.tensor_reduce(id2t[:], ebuf2[:], AX.X, ALU.min), [t_], "tid2")
    itev = []
    for ix, (src, ch) in enumerate(((g1t, 0), (g2t, 1), (id1t, 0), (id2t, 1))):
        mmt = mmstep(0, lambda t, src=src: t.transpose(pb[0][:, 0:128], src[:], ident),
                     waits=[tid1, tid2, tprob], pfx=f"itr{ix}")
        dst = items_g if ix < 2 else items_e
        ev = evstep(0, lambda v, dst=dst, ch=ch: v.tensor_copy(
            bass.AP(dst.tensor, dst.offset + ch, [dst.ap[0], [2, 128]]),
            pb[0][:, 0:128]), "v", [mmt], f"itre{ix}")
        itev.append(ev)
    teqs = vstep(lambda v: v.tensor_tensor(
        eqs[:], be(items_e), bass.AP(iota8.tensor, iota8.offset,
                                     [iota8.ap[0], [1, 8], [0, 256]]), ALU.is_equal),
        itev, "eqs")
    t_a = vstep(lambda v: v.tensor_tensor(gme[:], eqs[:], be(items_g), ALU.mult),
                [teqs], "gmea")
    t_b = vstep(lambda v: v.tensor_scalar(work[:], eqs[:], BIG, -BIG, ALU.mult, ALU.add),
                [teqs], "gmeb")
    tgme = vstep(lambda v: v.tensor_tensor(gme[:], gme[:], work[:], ALU.add),
                 [t_a, t_b], "gme")
    tbi = vstep(lambda v: (v.memset(lo_row[0:1, :], 0.0), v.memset(hi_row[0:1, :], 1.0),
                           v.memset(th_row[0:1, :], 0.5))[-1], ["alite"], "binit")
    prev = [tgme, tbi]
    for bi_ in range(45):
        bpfx = f"bis{bi_}_"
        bm = mmstep(6, lambda t: t.matmul(pb[6][:, 0:8], ones_row, th_row[0:1, :],
                                          start=True, stop=True), waits=prev, pfx=bpfx + "b")
        be_ = evstep(6, lambda v: v.tensor_copy(thb[:], pb[6][:, 0:8]), "v", [bm], bpfx + "be")
        ti = vstep(lambda v: v.tensor_tensor(work[:], gme[:], b8(thb), ALU.is_gt),
                   [be_], bpfx + "i")
        tr_ = vstep(lambda v: v.tensor_reduce(cpe[:], work[:], AX.X, ALU.add), [ti], bpfx + "r")
        cm = mmstep(7, lambda t: t.matmul(pb[7][0:1, 0:8], ones_col, cpe[:],
                                          start=True, stop=True), waits=[tr_], pfx=bpfx + "c")
        ce = evstep(7, lambda v: v.tensor_copy(cnt_row[0:1, :], pb[7][0:1, 0:8]),
                    "v", [cm], bpfx + "ce")

        def upd(v):
            v.tensor_scalar(cnt_row[0:1, :], cnt_row[0:1, :], float(CAP), None, ALU.is_ge)
            v.tensor_tensor(thb[0:1, 0:8], th_row[0:1, :], lo_row[0:1, :], ALU.subtract)
            v.tensor_tensor(thb[0:1, 0:8], thb[0:1, 0:8], cnt_row[0:1, :], ALU.mult)
            v.tensor_tensor(lo_row[0:1, :], lo_row[0:1, :], thb[0:1, 0:8], ALU.add)
            v.tensor_scalar(cnt_row[0:1, :], cnt_row[0:1, :], -1.0, 1.0, ALU.mult, ALU.add)
            v.tensor_tensor(thb[0:1, 0:8], th_row[0:1, :], hi_row[0:1, :], ALU.subtract)
            v.tensor_tensor(thb[0:1, 0:8], thb[0:1, 0:8], cnt_row[0:1, :], ALU.mult)
            v.tensor_tensor(hi_row[0:1, :], hi_row[0:1, :], thb[0:1, 0:8], ALU.add)
            v.tensor_tensor(th_row[0:1, :], lo_row[0:1, :], hi_row[0:1, :], ALU.add)
            return v.tensor_scalar_mul(th_row[0:1, :], th_row[0:1, :], 0.5)
        prev = [vstep(upd, [ce], bpfx + "u")]
    bm = mmstep(6, lambda t: t.matmul(pb[6][:, 0:8], ones_row, hi_row[0:1, :],
                                      start=True, stop=True), waits=prev, pfx="exb")
    bhi = evstep(6, lambda v: v.tensor_copy(hib[:], pb[6][:, 0:8]), "v", [bm], "exbe")
    t_ = vstep(lambda v: v.tensor_tensor(work[:], gme[:], b8(hib), ALU.is_le), [bhi], "exle")
    t_a = vstep(lambda v: v.tensor_tensor(work2[:], gme[:], work[:], ALU.mult), [t_], "exm")
    t_b = vstep(lambda v: v.tensor_scalar(work[:], work[:], BIG, -BIG, ALU.mult, ALU.add),
                [t_a], "exo")
    t_ = vstep(lambda v: v.tensor_tensor(work2[:], work2[:], work[:], ALU.add), [t_b], "exc")
    t_ = vstep(lambda v: v.tensor_reduce(cpe[:], work2[:], AX.X, ALU.max), [t_], "exr")
    xm = mmstep(7, lambda t: t.transpose(pb[7][0:8, 0:128], cpe[:], ident), waits=[t_], pfx="ext")
    xe = evstep(7, lambda v: v.tensor_copy(zred[0:8, 0:128], pb[7][0:8, 0:128]), "v", [xm], "exte")
    t_ = vstep(lambda v: v.tensor_reduce(zred2[0:8, 0:1], zred[0:8, 0:128], AX.X, ALU.max),
               [xe], "exr2")
    xm2 = mmstep(6, lambda t: t.transpose(pb[6][0:1, 0:8], zred2[0:8, 0:1], ident[0:8, 0:8]),
                 waits=[t_], pfx="ext2")
    tth = evstep(6, lambda v: v.tensor_copy(th_row[0:1, :], pb[6][0:1, 0:8]), "v", [xm2], "exte2")
    bm = mmstep(6, lambda t: t.matmul(pb[6][:, 0:8], ones_row, th_row[0:1, :],
                                      start=True, stop=True), waits=[tth], pfx="thb")
    tthb = evstep(6, lambda v: v.tensor_copy(thsb[:], pb[6][:, 0:8]), "v", [bm], "thbe")
    t_ = vstep(lambda v: v.tensor_tensor(work[:], eqs[:], b8(thsb), ALU.mult), [tthb], "thm")
    tthi = vstep(lambda v: v.tensor_reduce(thiI[:], swp(work), AX.X, ALU.add), [t_], "thi")
    tgt = vstep(lambda v: v.tensor_tensor(gtI[:], items_g[:], thiI[:], ALU.is_gt), [tthi], "gt")
    teqt = vstep(lambda v: v.tensor_tensor(eqthI[:], items_g[:], thiI[:], ALU.is_equal),
                 [tthi], "eqt")
    t_ = vstep(lambda v: v.tensor_tensor(work[:], eqs[:], be(gtI), ALU.mult), [tgt], "slm")
    t_ = vstep(lambda v: v.tensor_reduce(cpe[:], work[:], AX.X, ALU.add), [t_], "slr")
    cm = mmstep(7, lambda t: t.matmul(pb[7][0:1, 0:8], ones_col, cpe[:], start=True, stop=True),
                waits=[t_], pfx="slc")
    ce = evstep(7, lambda v: v.tensor_copy(cnt_row[0:1, :], pb[7][0:1, 0:8]), "v", [cm], "slce")
    tslb = vstep(lambda v: v.tensor_scalar(slb_row[0:1, :], cnt_row[0:1, :], -1.0, float(CAP),
                                           ALU.mult, ALU.add), [ce], "slb")

    def prefix_pass(src_tag, indI, dst, pfxn):
        sc = []
        for e_ in range(8):
            sc.append(vstep(lambda v, e_=e_: v.tensor_tensor_scan(
                scn[:, e_, :], work[:, e_, :], zer256[:], 0.0, ALU.add, ALU.add),
                [src_tag], pfxn + f"s{e_}"))
        t1 = vstep(lambda v: v.tensor_reduce(tots[:], indI[:], AX.X, ALU.add),
                   [src_tag], pfxn + "tot")
        om_ = mmstep(6, lambda t: t.matmul(pb[6][:, 0:8], trils[:], tots[:],
                                           start=True, stop=True), waits=[t1], pfx=pfxn + "off")
        oe_ = evstep(6, lambda s: s.copy(offs[:], pb[6][:, 0:8]), "s", [om_], pfxn + "offe")
        t2 = vstep(lambda v: v.tensor_tensor(scn[:], scn[:], b8(offs), ALU.add),
                   [oe_], pfxn + "full")
        t3 = vstep(lambda v: v.tensor_tensor(work2[:], scn[:], eqs[:], ALU.mult), [t2], pfxn + "sel")
        return vstep(lambda v: v.tensor_reduce(dst[:], swp(work2), AX.X, ALU.add),
                     [t3], pfxn + "red")

    t_ = vstep(lambda v: v.tensor_tensor(work[:], eqs[:], be(eqthI), ALU.mult), [teqt], "tiem")
    trk = prefix_pass(t_, work, r1, "trk")
    bm = mmstep(6, lambda t: t.matmul(pb[6][:, 0:8], ones_row, slb_row[0:1, :],
                                      start=True, stop=True), waits=[tslb], pfx="slbb")
    tslbb = evstep(6, lambda v: v.tensor_copy(slbb[:], pb[6][:, 0:8]), "v", [bm], "slbbe")
    t_ = vstep(lambda v: v.tensor_tensor(work[:], eqs[:], b8(slbb), ALU.mult), [tslbb], "slim")
    tsli = vstep(lambda v: v.tensor_reduce(r2[:], swp(work), AX.X, ALU.add), [t_], "sli")
    t_ = vstep(lambda v: v.tensor_tensor(r3[:], r1[:], r2[:], ALU.is_le), [trk, tsli], "tkle")
    t_ = vstep(lambda v: v.tensor_tensor(r3[:], r3[:], eqthI[:], ALU.mult), [t_], "tk2")
    tkept = vstep(lambda v: v.tensor_tensor(keptI[:], gtI[:], r3[:], ALU.max), [t_], "kept")
    t_ = vstep(lambda v: v.tensor_tensor(work[:], eqs[:], be(keptI), ALU.mult), [tkept], "klm")
    tslot = prefix_pass(t_, work, slotI, "slt")
    tslot = vstep(lambda v: v.tensor_tensor(slotI[:], slotI[:], keptI[:], ALU.subtract),
                  [tslot], "slx")
    def s512f(v):
        v.tensor_scalar(r1[:], slotI[:], 512.0, None, ALU.is_ge)
        for thr in (1024.0, 1536.0, 2048.0):
            v.tensor_scalar(r2[:], slotI[:], thr, None, ALU.is_ge)
            v.tensor_tensor(r1[:], r1[:], r2[:], ALU.add)
        return v.tensor_scalar_mul(r1[:], r1[:], 3584.0)
    t_ = vstep(s512f, [tslot], "fm1")
    t_ = vstep(lambda v: v.tensor_tensor(r1[:], r1[:], slotI[:], ALU.add), [t_], "fm4")
    t_ = vstep(lambda v: v.scalar_tensor_tensor(r1[:], items_e[:], 512.0, r1[:],
                                                ALU.mult, ALU.add), [t_], "fm5")
    tgidx = vstep(lambda v: v.tensor_tensor(r1[:], r1[:], keptI[:], ALU.mult), [t_], "fm6")
    tgg = vstep(lambda v: v.tensor_tensor(ggi0[:], items_g[:], keptI[:], ALU.mult),
                [tkept], "fgg")
    rts_ev = []
    for ix, (src, ch) in enumerate(((r1, 0), (r1, 1), (ggi0, 0), (ggi0, 1))):
        sv = bass.AP(src.tensor, src.offset + ch, [src.ap[0], [2, 128]])
        mmt = mmstep(1, lambda t, sv=sv, ix=ix: t.transpose(
            pb[1][:, ix * 128:(ix + 1) * 128], sv, ident), waits=[tgidx, tgg], pfx=f"rts{ix}")
        rts_ev.append(mmt)
    rte = evstep(1, lambda v: v.tensor_copy(
        bass.AP(rts_sb.tensor, rts_sb.offset, [rts_sb.ap[0], [1, 4], [4, 128]]),
        pb[1][:].rearrange("p (a b) -> p a b", b=128)), "v", rts_ev, "rtse")
    rtw = tag("rtsw")
    prog.on("q", lambda q: [q.dma_start(
        out=A("rts_d").rearrange("(g p) f -> p g f", p=128),
        in_=bass.AP(rts_sb.tensor, rts_sb.offset, [rts_sb.ap[0], [4, 128], [1, 4]]))],
        waits=[rte], n_dma=1, tag=rtw)
    t_ = vstep(lambda v: v.tensor_scalar(r2[:], items_e[:], myexpb[:, 0:1], None, ALU.is_equal),
               itev + ["p0"], "dse")
    tsel = vstep(lambda v: v.tensor_tensor(r2[:], r2[:], keptI[:], ALU.mult), [t_, tkept], "dsel")
    t_a = vstep(lambda v: v.tensor_tensor(r3[:], r2[:], slotI[:], ALU.mult), [tsel, tslot], "dp1")
    t_b = vstep(lambda v: v.tensor_scalar(posf[:], r2[:], -1.0, 1.0, ALU.mult, ALU.add),
                [tsel], "dp2")
    t_c = vstep(lambda v: v.tensor_tensor(posf[:], posf[:], dumpt[:], ALU.mult), [t_b], "dp3")
    tpos = vstep(lambda v: v.tensor_tensor(posf[:], posf[:], r3[:], ALU.add), [t_a, t_c], "dp4")
    tposu = vstep(lambda v: v.tensor_copy(posu[:], posf[:]), [tpos], "dpu")
    tgtoku = vstep(lambda v: v.tensor_copy(gtoku[:], gtok[:]), ["p0"], "dgt")
    t_ = vstep(lambda v: v.memset(zred[:, 0:20], float(N)), ["alite"], "dpf1")
    t_ = vstep(lambda v: v.tensor_copy(disp_pre[:], zred[:, 0:20]), [t_], "dpf2")
    dpf = tag("dpf3")
    prog.on("q", lambda q: [q.dma_start(
        out=A("disp_d")[0:CAPP, :].rearrange("(g p) one -> p (g one)", p=128),
        in_=disp_pre[:], allow_slow_non_contiguous=True)],
        waits=[t_], n_dma=1, tag=dpf)
    sc_tags = []
    for j in range(256 if not SKIP_SC else 0):
        st_ = tag(f"dsc{j}")
        prog.on("g", lambda g, j=j: [g.indirect_dma_start(
            out=A("disp_d"), out_offset=IndirectOffsetOnAxis(posu[:, j:j + 1], 0),
            in_=gtoku[:, j:j + 1], in_offset=None)],
            waits=[tposu, tgtoku, dpf] if j == 0 else [], n_dma=1, tag=st_)
        sc_tags.append(st_)
    prog.barrier("routing")
    prog.default_waits = ["routing"]
    if stop_if("routing"):
        return

    cstat = tag("cstat")
    prog.on("g", lambda g: g.collective_compute(
        "AllGather", ALU.bypass, replica_groups=RG,
        ins=[A("stats_in")], outs=[A("stats_all")]), waits=["statw"], tag=cstat)
    gw1sb = view(8240, [128, 9, 1024])
    h1B = view(17456, [128, 8, 512])
    sstat = view(21552, [128, 64, 6])
    stats_fm = view(21936, [128, 512])
    ltile = view(22448, [128, 4, 24])
    lred = view(22544, [128, 16])
    lred2 = view(22560, [128, 16])
    sft = view(22576, [128, 24])
    sg_tags = []
    for i in range(64):
        t_ = tag(f"sg{i}")
        prog.on("g", lambda g, i=i: [g.indirect_dma_start(
            out=sstat[:, i, :], out_offset=None,
            in_=A("stats_all").rearrange("c t f -> (c t) f"),
            in_offset=IndirectOffsetOnAxis(statrows_u[:, i:i + 1], 0))],
            waits=[cstat] if i == 0 else [], n_dma=1, tag=t_)
        sg_tags.append(t_)
    pq_tags = []
    for qt in range(4):
        pfx = f"gb{qt}_"
        rows = A("xmy")[qt * TT:(qt + 1) * TT, :]
        xn_tag = emit_ln(rows, qt, pfx)
        fm_tags = emit_tr_fm(qt, pfx, xn_tag)
        fm_done = fm_tags[-2:]
        sf_tags = []
        for g4 in range(4):
            gg = qt * 4 + g4
            sv = bass.AP(sstat.tensor, sstat.offset + gg * 6,
                         [sstat.ap[0], [96, 4], [1, 6]])
            tc_ = vstep(lambda v, sv=sv: v.tensor_copy(
                sft[:].rearrange("p (h w) -> p h w", w=6), sv),
                sg_tags[-1:] + sf_tags[-1:], pfx + f"sfc{g4}")
            mmt = mmstep(5, lambda t: t.transpose(pb[5][0:24, 0:128], sft[:], ident),
                         waits=[tc_], pfx=pfx + f"sf{g4}")
            sf_tags.append(evstep(5, lambda v, g4=g4: v.tensor_copy(
                stats_fm[0:24, g4 * 128:(g4 + 1) * 128], pb[5][0:24, 0:128]),
                "v", [mmt], pfx + f"sfe{g4}"))
        h1_done = []
        for Mh in range(2):
            l1 = tag(pfx + f"gw{Mh}a")
            prog.on("q", lambda q, Mh=Mh: [q.dma_start(
                out=gw1sb[:, 0:8, :],
                in_=A("gw1t")[0:1024, Mh * 1024:(Mh + 1) * 1024]
                .rearrange("(kc p) f -> p kc f", p=128))], waits=h1_done[-1:], n_dma=1, tag=l1)
            l2 = tag(pfx + f"gw{Mh}b")
            prog.on("q", lambda q, Mh=Mh: [q.dma_start(
                out=gw1sb[0:24, 8, :],
                in_=A("gw1t")[1024:1048, Mh * 1024:(Mh + 1) * 1024])], n_dma=1, tag=l2)
            for Mc in range(8):
                bk = Mc % 4
                col = Mh * 8 + Mc

                def h1mm(t, Mc=Mc, bk=bk):
                    for kc in range(8):
                        t.matmul(pb[bk][:], gw1sb[:, kc, Mc * 128:(Mc + 1) * 128],
                                 xn_fm[:, kc, :], start=(kc == 0), stop=False)
                    return t.matmul(pb[bk][:], gw1sb[0:24, 8, Mc * 128:(Mc + 1) * 128],
                                    stats_fm[0:24, :], start=False, stop=True)
                hm = mmstep(bk, h1mm, waits=[l1, l2] + fm_done + sf_tags, pfx=pfx + f"h1m{col}")
                he = evstep(bk, lambda s, Mc=Mc, bk=bk, col=col: s.activation(
                    h1B[:, Mc, :], pb[bk][:], AF.Gelu, bias=gb1_sb[:, col:col + 1]),
                    "s", [hm], pfx + f"h1e{col}")
                lt_ = mmstep(6, lambda t, Mc=Mc, Mh=Mh, col=col: [t.matmul(
                    pb[6][:, g4 * 24:g4 * 24 + 24], h1B[:, Mc, g4 * 128:(g4 + 1) * 128],
                    gw2t_sb[:, col, :], start=(col == 0), stop=(col == 15))
                    for g4 in range(4)][-1], waits=[he], pfx=pfx + f"lg{col}")
                h1_done.append(lt_)
        lev = evstep(6, lambda v: v.tensor_copy(
            ltile[:].rearrange("p a b -> p (a b)"), pb[6][:, 0:96]), "v",
            h1_done[-1:], pfx + "lev")
        def epi(v, qt=qt):
            v.tensor_tensor(ltile[:], ltile[:], bg(b2b, 4), ALU.add)
            for g4 in range(4):
                gg = qt * 4 + g4
                sv = bass.AP(sstat.tensor, sstat.offset + gg * 6,
                             [sstat.ap[0], [96, 4], [1, 6]])
                v.tensor_tensor(trash[:, 0:24].rearrange("p (h w) -> p h w", w=6), sv,
                                alphab[:].rearrange("p (h w) -> p h w", w=6), ALU.mult)
                v.tensor_tensor(ltile[:, g4, :], ltile[:, g4, :], trash[:, 0:24], ALU.add)
            return v.tensor_tensor(ltile[:], ltile[:], bg(itempb, 4), ALU.mult)
        te = vstep(epi, [lev] + sg_tags[-1:], pfx + "epi")
        lt4 = ltile[:].rearrange("p a (h w) -> p (a h) w", w=6)
        t_ = vstep(lambda v: v.tensor_reduce(lred[:], lt4, AX.X, ALU.max), [te], pfx + "smx")
        t_ = vstep(lambda v: v.tensor_tensor(
            lt4, lt4, bass.AP(lred.tensor, lred.offset, [lred.ap[0], [1, 16], [0, 6]]),
            ALU.subtract), [t_], pfx + "sms")
        t_x = tag(pfx + "sme")
        prog.on("s", lambda s: s.activation(ltile[:], ltile[:], AF.Exp), waits=[t_], tag=t_x)
        t_ = vstep(lambda v: v.tensor_reduce(lred2[:], lt4, AX.X, ALU.add), [t_x], pfx + "smr")
        t_ = vstep(lambda v: v.reciprocal(lred2[:], lred2[:]), [t_], pfx + "smc")
        t_ = vstep(lambda v: v.tensor_tensor(
            lt4, lt4, bass.AP(lred2.tensor, lred2.offset, [lred2.ap[0], [1, 16], [0, 6]]),
            ALU.mult), [t_], pfx + "smd")
        t_ = vstep(lambda v: v.tensor_scalar_mul(ltile[:], ltile[:], 0.95), [t_], pfx + "smf")
        t_ = vstep(lambda v: v.tensor_scalar_add(
            bass.AP(ltile.tensor, ltile.offset + 5, [ltile.ap[0], [24, 4], [6, 4]])
            .rearrange("p a b -> p (a b)"), bass.AP(
                ltile.tensor, ltile.offset + 5, [ltile.ap[0], [24, 4], [6, 4]])
            .rearrange("p a b -> p (a b)"), 0.05), [t_], pfx + "smg")
        pw = tag(pfx + "pw")
        prog.on("q", lambda q, qt=qt: [q.dma_start(
            out=A("probs_in")[qt * TT:(qt + 1) * TT, :].rearrange("(s p) f -> p s f", p=128),
            in_=ltile[:])], waits=[t_], n_dma=1, tag=pw)
        pq_tags.append(pw)
        prog.barrier(f"qb{qt}")
        prog.default_waits = [f"qb{qt}"]
    cprob = tag("cprob")
    prog.on("g", lambda g: g.collective_compute(
        "AllGather", ALU.bypass, replica_groups=RG,
        ins=[A("probs_in")], outs=[A("probs_all")]), waits=pq_tags, tag=cprob)
    prog.barrier("phB")
    prog.default_waits = ["phB"]
    if stop_if("phB"):
        return

    sprob = view(0, [128, 64, 6])
    brl = [view(512 + b_ * 2048, [128, 8, 256]) for b_ in range(6)]
    o_acc = view(12800, [128, 8, 256])
    cssq = view(14848, [128, 8])
    pg_tags = []
    for i in range(64):
        t_ = tag(f"pg{i}")
        prog.on("g", lambda g, i=i: [g.indirect_dma_start(
            out=sprob[:, i, :], out_offset=None,
            in_=A("probs_all").rearrange("c t (h w) -> (c t h) w", w=6),
            in_offset=IndirectOffsetOnAxis(probrows_u[:, i:i + 1], 0))],
            waits=[cprob] if i == 0 else [], n_dma=1, tag=t_)
        pg_tags.append(t_)
    for cc in range(8):
        pfx = f"cc{cc}_"
        bl_tags = []
        for b_ in range(6):
            t_ = tag(pfx + f"bl{b_}")
            prog.on("q", lambda q, b_=b_, cc=cc: [q.dma_start(
                out=brl[b_][:],
                in_=A("br_d")[b_, cc * 1024:(cc + 1) * 1024, :]
                .rearrange("(s p) d -> p s d", p=128))], n_dma=1, tag=t_)
            bl_tags.append(t_)
        rms_tags = []
        for g8 in range(8):
            gg = cc * 8 + g8

            def comb(v, g8=g8, gg=gg):
                r = v.tensor_scalar_mul(o_acc[:, g8, :], brl[0][:, g8, :],
                                        sprob[:, gg, 0:1])
                for b_ in range(1, 6):
                    r = v.scalar_tensor_tensor(o_acc[:, g8, :], brl[b_][:, g8, :],
                                               sprob[:, gg, b_:b_ + 1], o_acc[:, g8, :],
                                               ALU.mult, ALU.add)
                return r
            t_ = vstep(comb, bl_tags + pg_tags[-1:], pfx + f"cb{g8}")
            t2 = tag(pfx + f"rq{g8}")
            prog.on("s", lambda s, g8=g8: s.activation(
                trash[:, 0:256], o_acc[:, g8, :], AF.Square, scale=1.0 / 16.0,
                accum_out=cssq[:, g8:g8 + 1]), waits=[t_], tag=t2)
            rms_tags.append(t2)
        t_ = tag(pfx + "rsd")
        prog.on("s", lambda s: s.activation(cssq[:], cssq[:], AF.Sqrt, bias=cbias[:, 0:1]),
                waits=rms_tags, tag=t_)
        t_ = vstep(lambda v: v.reciprocal(cssq[:], cssq[:]), [t_], pfx + "rrc")
        onorm = []
        for g8 in range(8):
            onorm.append(vstep(lambda v, g8=g8: v.scalar_tensor_tensor(
                o_acc[:, g8, :], o_acc[:, g8, :], cssq[:, g8:g8 + 1], onw_b[:],
                ALU.mult, ALU.mult), [t_], pfx + f"on{g8}"))
        ow = tag(pfx + "ow")
        prog.on("q", lambda q, cc=cc: [q.dma_start(
            out=A("o_in")[cc * 1024:(cc + 1) * 1024, :].rearrange("(s p) d -> p s d", p=128),
            in_=o_acc[:])], waits=onorm, n_dma=1, tag=ow)
        if cc % 2 == 1:
            ct = f"co{cc // 2}"
            prog.on("g", lambda g, cc=cc: g.collective_compute(
                "AllGather", ALU.bypass, replica_groups=RG,
                ins=[A("o_in")[(cc - 1) * 1024:(cc + 1) * 1024, :]],
                outs=[A("o_all")[cc // 2]]), waits=[ow], tag=ct)
        prog.barrier(pfx + "end")
        prog.default_waits = [pfx + "end"]
    prog.barrier("phC")
    prog.default_waits = ["phC"]
    if stop_if("phC"):
        return

    xg_g = view(0, [128, 5, 1024])
    xgT = view(5120, [128, 8, 640])
    w1q = view(10240, [128, 8, 1024])
    w2q = view(18432, [128, 8, 1024])
    h1q = view(26624, [128, 8, 512])
    xgt_t = view(0, [128, 8, 512])
    outbuf = view(4096, [128, 8, 512])
    fout = view(30720, [128, 4, 512])
    b1sb = view(32768, [128, 32])
    b2sb = view(32800, [128, 8])
    t_ = tag("dld")
    prog.on("q", lambda q: [q.dma_start(
        out=sgidx[:], in_=A("disp_d")[0:CAPP, :].rearrange("(g p) one -> p (g one)", p=128),
        allow_slow_non_contiguous=True)],
        waits=sc_tags[-1:] or [dpf], n_dma=1, tag=t_)
    dld = t_
    prog.on("q", lambda q: [q.dma_start(
        out=b1sb[:], in_=A("b1e").rearrange("(c p) one -> p (c one)", p=128),
        allow_slow_non_contiguous=True)], n_dma=1, tag="db1")
    prog.on("q", lambda q: [q.dma_start(
        out=b2sb[:], in_=A("b2e").rearrange("(c p) one -> p (c one)", p=128),
        allow_slow_non_contiguous=True)], n_dma=1, tag="db2")
    stg_tags = []
    for r_ in range(4):
        pfx = f"dg{r_}_"
        ga = []
        for i5 in range(5):
            t_ = tag(pfx + f"g{i5}")
            prog.on("g", lambda g, r_=r_, i5=i5: [g.indirect_dma_start(
                out=xg_g[:, i5, :], out_offset=None,
                in_=A("xn_tm_d"),
                in_offset=IndirectOffsetOnAxis(sgidx[:, r_ * 5 + i5:r_ * 5 + i5 + 1], 0))],
                waits=[dld, "zdum"] + stg_tags[-1:], n_dma=1, tag=t_)
            ga.append(t_)
        ev_all = []
        for kc in range(8):
            bk = 6 + kc % 2
            mm1 = mmstep(bk, lambda t, kc=kc, bk=bk: [t.transpose(
                pb[bk][:, i5 * 128:(i5 + 1) * 128],
                xg_g[:, i5, kc * 128:(kc + 1) * 128], ident) for i5 in range(4)][-1],
                waits=ga, pfx=pfx + f"t{kc}a")
            e1 = evstep(bk, lambda v, kc=kc, bk=bk: v.tensor_copy(
                xgT[:, kc, 0:512], pb[bk][:]), "v", [mm1], pfx + f"te{kc}a")
            mm2 = mmstep(bk, lambda t, kc=kc, bk=bk: t.transpose(
                pb[bk][:, 0:128], xg_g[:, 4, kc * 128:(kc + 1) * 128], ident),
                waits=ga, pfx=pfx + f"t{kc}b")
            e2 = evstep(bk, lambda s, kc=kc, bk=bk: s.copy(
                xgT[:, kc, 512:640], pb[bk][:, 0:128]), "s", [mm2], pfx + f"te{kc}b")
            ev_all += [e1, e2]
        t_ = tag(pfx + "w")
        prog.on("q", lambda q, r_=r_: [q.dma_start(
            out=A("xg_fm_d").rearrange("(kc p) s -> p kc s", p=128)[:, :, r_ * 640:(r_ + 1) * 640],
            in_=xgT[:])], waits=ev_all, n_dma=1, tag=t_)
        stg_tags.append(t_)
    prog.barrier("dstage")
    prog.default_waits = ["dstage"]
    if stop_if("dstage"):
        return
    out_dma = [None] * 5
    ffn_done = []
    for qh in range(4):
        pfx = f"dq{qh}_"
        lw1 = tag(pfx + "w1")
        prog.on("q", lambda q, qh=qh: [q.dma_start(
            out=w1q[:], in_=A("w1e").rearrange("(kc p) f -> p kc f", p=128)
            [:, :, qh * 1024:(qh + 1) * 1024])], waits=ffn_done[-1:], n_dma=1, tag=lw1)
        lw2 = tag(pfx + "w2")
        prog.on("q", lambda q, qh=qh: [q.dma_start(
            out=w2q[:], in_=A("w2e")[qh * 1024:(qh + 1) * 1024, :]
            .rearrange("(kh p) f -> p kh f", p=128))], n_dma=1, tag=lw2)
        for tk in range(5):
            tpfx = pfx + f"t{tk}_"
            lx = tag(tpfx + "x")
            prog.on("q", lambda q, tk=tk: [q.dma_start(
                out=xgt_t[:], in_=A("xg_fm_d").rearrange("(kc p) s -> p kc s", p=128)
                [:, :, tk * 512:(tk + 1) * 512])], waits=ffn_done[-1:], n_dma=1, tag=lx)
            h1_tags = []
            for Mh in range(8):
                bk = Mh % 2
                hm = mmstep(bk, lambda t, Mh=Mh, bk=bk: [t.matmul(
                    pb[bk][:], w1q[:, kc, Mh * 128:(Mh + 1) * 128], xgt_t[:, kc, :],
                    start=(kc == 0), stop=(kc == 7)) for kc in range(8)][-1],
                    waits=[lw1, lx], pfx=tpfx + f"h{Mh}")
                he = evstep(bk, lambda s, Mh=Mh, bk=bk, qh=qh: s.activation(
                    h1q[:, Mh, :], pb[bk][:], AF.Silu,
                    bias=b1sb[:, qh * 8 + Mh:qh * 8 + Mh + 1]), "s", [hm, "db1"],
                    tpfx + f"he{Mh}")
                h1_tags.append(he)
            ou_tags = []
            for Md in range(8):
                bk = 4 + Md % 2
                om_ = mmstep(bk, lambda t, Md=Md, bk=bk: [t.matmul(
                    pb[bk][:], w2q[:, kh, Md * 128:(Md + 1) * 128], h1q[:, kh, :],
                    start=(kh == 0), stop=(kh == 7)) for kh in range(8)][-1],
                    waits=[lw2] + h1_tags, pfx=tpfx + f"o{Md}")
                if qh == 0:
                    oe_ = evstep(bk, lambda v, Md=Md, bk=bk: v.tensor_scalar_add(
                        outbuf[:, Md, :], pb[bk][:], b2sb[:, Md:Md + 1]),
                        "v", [om_, "db2"], tpfx + f"oe{Md}")
                else:
                    oe_ = evstep(bk, lambda v, Md=Md, bk=bk: v.tensor_copy(
                        outbuf[:, Md, :], pb[bk][:]), "v", [om_], tpfx + f"oe{Md}")
                ou_tags.append(oe_)
            dw = tag(tpfx + "dw")
            if qh == 0:
                prog.on("q", lambda q, tk=tk: [q.dma_start(
                    out=A("outfm_d").rearrange("(md p) s -> p md s", p=128)
                    [:, :, tk * 512:(tk + 1) * 512], in_=outbuf[:])],
                    waits=ou_tags, n_dma=1, tag=dw)
            else:
                prog.on("g", lambda g, tk=tk: [g.dma_start(
                    out=A("outfm_d").rearrange("(md p) s -> p md s", p=128)
                    [:, :, tk * 512:(tk + 1) * 512], in_=outbuf[:],
                    accum_op=ALU.add)],
                    waits=ou_tags + [out_dma[tk]], n_dma=1, tag=dw)
            out_dma[tk] = dw
            ffn_done.append(ou_tags[-1])
        prog.barrier(pfx + "end")
        prog.default_waits = [pfx + "end"]
    fout2 = view(4096, [128, 4, 1024])
    for tk in range(5):
        pfx = f"df{tk}_"
        lx = tag(pfx + "l")
        prog.on("q", lambda q, tk=tk: [q.dma_start(
            out=xgt_t[:], in_=A("outfm_d").rearrange("(md p) s -> p md s", p=128)
            [:, :, tk * 512:(tk + 1) * 512])], waits=[out_dma[tk]], n_dma=1, tag=lx)
        ev_all = []
        for si in range(4):
            for half in range(2):
                bk = 6 + half
                mm1 = mmstep(bk, lambda t, si=si, half=half, bk=bk: [t.transpose(
                    pb[bk][:, kc4 * 128:(kc4 + 1) * 128],
                    xgt_t[:, half * 4 + kc4, si * 128:(si + 1) * 128], ident)
                    for kc4 in range(4)][-1], waits=[lx], pfx=pfx + f"t{si}{half}")
                if half == 0:
                    e1 = evstep(bk, lambda v, si=si, bk=bk: v.tensor_copy(
                        fout2[:, si, 0:512], pb[bk][:]), "v", [mm1], pfx + f"te{si}{half}")
                else:
                    e1 = evstep(bk, lambda s_, si=si, bk=bk: s_.copy(
                        fout2[:, si, 512:1024], pb[bk][:]), "s", [mm1], pfx + f"te{si}{half}")
                ev_all.append(e1)
        fw = tag(pfx + "w")
        prog.on("q", lambda q, tk=tk: [q.dma_start(
            out=A("ffn_in")[tk * 512:(tk + 1) * 512, :].rearrange("(s p) d -> p s d", p=128),
            in_=fout2[:])], waits=ev_all, n_dma=1, tag=fw)
        cf = f"cf{tk}"
        prog.on("g", lambda g, tk=tk: g.collective_compute(
            "AllGather", ALU.bypass, replica_groups=RG,
            ins=[A("ffn_in")[tk * 512:(tk + 1) * 512, :]], outs=[A("ffn_all")[tk]]),
            waits=[fw], tag=cf)
    prog.barrier("phD")
    prog.default_waits = ["phD"]
    if stop_if("phD"):
        return

    o_fmE = view(0, [128, 8, 2048])
    og_h = view(16384, [128, 16, 256])
    wotsb = view(20480, [128, 8, 1024])
    rtsb = view(28672, [128, 16, 4])
    gg0 = view(28736, [128, 16])
    gg1 = view(28752, [128, 16])
    rows0 = view(28800, [128, 1024])
    rows1 = view(29824, [128, 1024])
    gbuf = view(30848, [128, 1024])
    xtile = view(31872, [128, 1024])
    prog.on("q", lambda q: [q.dma_start(
        out=wotsb[:], in_=A("wot").rearrange("(kc p) f -> p kc f", p=128))],
        n_dma=1, tag="ewot")
    rg_tags = []
    for i in range(16):
        t_ = tag(f"erts{i}")
        prog.on("g", lambda g, i=i: [g.indirect_dma_start(
            out=rtsb[:, i, :], out_offset=None,
            in_=A("rts_d"), in_offset=IndirectOffsetOnAxis(myrows_u[:, i:i + 1], 0))],
            waits=[rtw] if i == 0 else [], n_dma=1, tag=t_)
        rg_tags.append(t_)

    def cvt(v):
        v.tensor_copy(gidx0_u[:], bass.AP(rtsb.tensor, rtsb.offset + 0,
                                          [rtsb.ap[0], [4, 16]]))
        v.tensor_copy(gidx1_u[:], bass.AP(rtsb.tensor, rtsb.offset + 1,
                                          [rtsb.ap[0], [4, 16]]))
        v.tensor_copy(gg0[:], bass.AP(rtsb.tensor, rtsb.offset + 2,
                                      [rtsb.ap[0], [4, 16]]))
        return v.tensor_copy(gg1[:], bass.AP(rtsb.tensor, rtsb.offset + 3,
                                             [rtsb.ap[0], [4, 16]]))
    tcvt = vstep(cvt, rg_tags[-1:], "ecvt")
    ofm_tags = []
    for h_ in range(4):
        og_tags = []
        for g16 in range(16):
            t_ = tag(f"eog{h_}_{g16}")
            prog.on("g", lambda g, h_=h_, g16=g16: [g.indirect_dma_start(
                out=og_h[:, g16, :], out_offset=None,
                in_=A("o_all").rearrange("c e t d -> (c e t) d"),
                in_offset=IndirectOffsetOnAxis(orows_u[:, h_ * 16 + g16:h_ * 16 + g16 + 1], 0))],
                waits=([f"co{i}" for i in range(4)] + ofm_tags[-1:]) if g16 == 0 else [],
                n_dma=1, tag=t_)
            og_tags.append(t_)
        for dc in range(2):
            for g4 in range(4):
                bk = 6 + (g4 % 2)
                mm1 = mmstep(bk, lambda t, dc=dc, g4=g4, bk=bk: [t.transpose(
                    pb[bk][:, gi * 128:(gi + 1) * 128],
                    og_h[:, g4 * 4 + gi, dc * 128:(dc + 1) * 128], ident)
                    for gi in range(4)][-1], waits=og_tags, pfx=f"eot{h_}{dc}{g4}")
                ev = evstep(bk, lambda v, h_=h_, dc=dc, g4=g4, bk=bk: v.tensor_copy(
                    o_fmE[:, h_ * 2 + dc, g4 * 512:(g4 + 1) * 512], pb[bk][:]),
                    "v", [mm1], f"eote{h_}{dc}{g4}")
                ofm_tags.append(ev)
    st_tags = []
    for g16 in range(16):
        pfx = f"ef{g16}_"
        lx = tag(pfx + "x")
        prog.on("q", lambda q, g16=g16: [q.dma_start(
            out=xtile[:], in_=A("xmy")[g16 * 128:(g16 + 1) * 128, :])],
            waits=st_tags[-1:], n_dma=1, tag=lx)
        gr0 = tag(pfx + "g0")
        prog.on("g", lambda g, g16=g16: [g.indirect_dma_start(
            out=rows0[:], out_offset=None,
            in_=A("ffn_all").rearrange("c e t d -> (c e t) d"),
            in_offset=IndirectOffsetOnAxis(gidx0_u[:, g16:g16 + 1], 0))],
            waits=([f"cf{i}" for i in range(5)] + [tcvt] if g16 == 0 else []) + st_tags[-1:],
            n_dma=1, tag=gr0)
        gr1 = tag(pfx + "g1")
        prog.on("g", lambda g, g16=g16: [g.indirect_dma_start(
            out=rows1[:], out_offset=None,
            in_=A("ffn_all").rearrange("c e t d -> (c e t) d"),
            in_offset=IndirectOffsetOnAxis(gidx1_u[:, g16:g16 + 1], 0))],
            n_dma=1, tag=gr1)
        ty = vstep(lambda v, g16=g16: v.tensor_scalar_mul(
            gbuf[:], rows1[:], gg1[:, g16:g16 + 1]), [gr1, tcvt], pfx + "y1")
        ty = vstep(lambda v, g16=g16: v.scalar_tensor_tensor(
            gbuf[:], rows0[:], gg0[:, g16:g16 + 1], gbuf[:], ALU.mult, ALU.add),
            [ty, gr0], pfx + "y2")
        tg = tag(pfx + "sil")
        prog.on("s", lambda s: s.activation(gbuf[:], gbuf[:], AF.Silu), waits=[ty], tag=tg)
        sm = []
        for half in range(2):
            bk = half
            mmt = mmstep(bk, lambda t, g16=g16, half=half, bk=bk: [t.matmul(
                pb[bk][:], o_fmE[:, kc, g16 * 128:(g16 + 1) * 128],
                wotsb[:, kc, half * 512:(half + 1) * 512],
                start=(kc == 0), stop=(kc == 7)) for kc in range(8)][-1],
                waits=ofm_tags[-1:] + ["ewot"], pfx=pfx + f"s{half}")
            sm.append(mmt)
        oo = []
        for half in range(2):
            t1 = vstep(lambda v, half=half, bk=half: v.tensor_tensor(
                rows1[:, half * 512:(half + 1) * 512], pb[bk][:],
                gbuf[:, half * 512:(half + 1) * 512], ALU.mult), [sm[half], tg], pfx + f"o{half}")
            t2 = vstep(lambda v, half=half: v.tensor_tensor(
                rows1[:, half * 512:(half + 1) * 512], rows1[:, half * 512:(half + 1) * 512],
                xtile[:, half * 512:(half + 1) * 512], ALU.add), [t1, lx], pfx + f"oa{half}")
            bank_last[half] = t2
            oo.append(t2)
        st = tag(pfx + "st")
        prog.on("q", lambda q, g16=g16: [q.dma_start(
            out=A("out_slice")[g16 * 128:(g16 + 1) * 128, :], in_=rows1[:])],
            waits=oo, n_dma=1, tag=st)
        st_tags.append(st)
    prog.barrier("end")
    prog.on("q", lambda q: None, waits=["end"])



def _c(a, dt=np.float32):
    return np.ascontiguousarray(a, dtype=dt)


def kernel(L0_attn_hidden_states, L0_attn_q_proj_w, L0_attn_k_proj_w, L0_attn_v_proj_w,
           L0_attn_b_proj_w, L0_attn_q_conv_w, L0_attn_k_conv_w, L0_attn_v_conv_w,
           L0_attn_fir_w3, L0_attn_fir_w7, L0_attn_fir_w15, L0_attn_fir_w31,
           L0_attn_gate_w1, L0_attn_gate_b1, L0_attn_gate_w2, L0_attn_gate_b2,
           L0_attn_gate_log_temp, L0_attn_alpha_stat, L0_attn_o_norm_w, L0_attn_o_proj_w,
           L0_ffn_Wr, L0_ffn_br, L0_ffn_W1, L0_ffn_b1, L0_ffn_W2, L0_ffn_b2):
    x = _c(L0_attn_hidden_states)
    p_ = np.arange(128)
    ident = _c(np.eye(128))
    ones128 = np.ones((128, 128), np.float32)
    masklow = _c(-np.tril(ones128, -1))
    maskupd = _c(np.triu(ones128, 0))
    trils = _c(np.triu(ones128, 1))
    iota8 = _c(np.tile(np.arange(8), (128, 1)))
    iota256 = _c(np.tile(np.arange(256), (128, 1)))
    gtok = _c(128 * p_[:, None] + (np.arange(256)[None, :] // 2))
    wbz_base = np.concatenate([_c(L0_ffn_Wr), np.zeros((D, 1), np.float32)], 1)

    in_maps = []
    for c in range(8):
        b, h = c >> 2, c & 3
        q = c & 3
        hs = slice(h * DK, (h + 1) * DK)
        wbz = wbz_base.copy()
        wbz[:, 8] = np.asarray(L0_attn_b_proj_w)[h, :]
        g16 = np.arange(16)
        g64h = np.arange(64)
        hp, gp = g64h // 16, g64h % 16
        myrows = (c * 2048 + g16[None, :] * 128 + p_[:, None]).astype(np.uint32)
        stat_rows = ((4 * b + hp)[None, :] * 8192 + q * 2048 + gp[None, :] * 128
                     + p_[:, None]).astype(np.uint32)
        i64 = np.arange(64)
        probs_rows = (((b * 8192 + i64[None, :] * 128 + p_[:, None]) * 4) + h).astype(np.uint32)
        o_rows = (q * 16384 + (4 * b + hp)[None, :] * 2048 + gp[None, :] * 128
                  + p_[:, None]).astype(np.uint32)
        in_maps.append({
            "x0": x[0], "x1": x[1], "xmine": x[b],
            "xmy": _c(x[b, q * 2048:(q + 1) * 2048]),
            "wq_t": _c(np.asarray(L0_attn_q_proj_w)[hs, :].T),
            "wk_t": _c(np.asarray(L0_attn_k_proj_w)[hs, :].T),
            "wv_t": _c(np.asarray(L0_attn_v_proj_w)[hs, :].T),
            "wbz": _c(wbz),
            "convw": _c(np.concatenate([np.asarray(L0_attn_q_conv_w)[hs],
                                        np.asarray(L0_attn_k_conv_w)[hs],
                                        np.asarray(L0_attn_v_conv_w)[hs]], 0)),
            "firw": _c(np.concatenate([np.asarray(L0_attn_fir_w3)[hs],
                                       np.asarray(L0_attn_fir_w7)[hs],
                                       np.asarray(L0_attn_fir_w15)[hs],
                                       np.asarray(L0_attn_fir_w31)[hs]], 1)),
            "gw1t": _c(np.asarray(L0_attn_gate_w1).T),
            "gb1": _c(np.asarray(L0_attn_gate_b1).reshape(2048, 1)),
            "gw2t": _c(np.asarray(L0_attn_gate_w2).T),
            "gb2": _c(np.asarray(L0_attn_gate_b2).reshape(1, 24)),
            "alpha24": _c(np.tile(np.asarray(L0_attn_alpha_stat), 4).reshape(1, 24)),
            "gltemp": _c(np.asarray(L0_attn_gate_log_temp).reshape(1, 4)),
            "onw": _c(np.asarray(L0_attn_o_norm_w).reshape(1, DK)),
            "wot": _c(np.asarray(L0_attn_o_proj_w).T),
            "brr": _c(np.asarray(L0_ffn_br).reshape(1, 8)),
            "w1e": _c(np.asarray(L0_ffn_W1)[c]),
            "w2e": _c(np.asarray(L0_ffn_W2)[c]),
            "b1e": _c(np.asarray(L0_ffn_b1)[c].reshape(4096, 1)),
            "b2e": _c(np.asarray(L0_ffn_b2)[c].reshape(D, 1)),
            "c_ident": ident, "c_masklow": masklow, "c_maskupd": maskupd,
            "c_trils": trils,
            "c_ones_col": np.ones((128, 1), np.float32),
            "c_ones_row": np.ones((1, 128), np.float32),
            "c_iota8": iota8, "c_iota256": iota256, "c_gtok": gtok,
            "c_myexp": np.full((1, 1), float(c), np.float32),
            "c_dump": _c(CAPP + 128 * np.arange(256)[None, :] + p_[:, None]),
            "c_myrows": myrows, "c_stat_rows": stat_rows,
            "c_probs_rows": probs_rows, "c_o_rows": o_rows,
        })
    nc = build(None)
    res = run_bass_kernel_spmd(nc, in_maps, core_ids=list(range(8)))
    out = np.empty((2, 8192, 1024), np.float32)
    for c in range(8):
        b, q = c >> 2, c & 3
        out[b, q * 2048:(q + 1) * 2048, :] = res.results[c]["out_slice"]
    return out


# version 10
# speedup vs baseline: 1.1829x; 1.1829x over previous
import numpy as np
from contextlib import ExitStack
import concourse.bass as bass
import concourse.mybir as mybir
from concourse.bass_utils import run_bass_kernel_spmd
from concourse.bass_core_rust import IndirectOffsetOnAxis

F32 = mybir.dt.float32
U32 = mybir.dt.uint32
AF = mybir.ActivationFunctionType
ALU = mybir.AluOpType
AX = mybir.AxisListType

B, T, D, H, DK, E = 2, 8192, 1024, 4, 256, 8
N = B * T
CAP = 2458
CAPP = 2560
TT = 512
NTT = T // TT
SUB = TT // 128
CH = 128
NCH = TT // CH
NGRP = N // 128
MYTOK = 2048
BIG = 1.0e30
NEG = -1.0e30

SKIP_SC = False
SCAN_BAR = 0
TRACE_SC = False
TRUNC = None
DUMPS = set()

RG = [[0, 1, 2, 3, 4, 5, 6, 7]]



_NOWRAP = {"wait_ge", "dma_start", "indirect_dma_start", "collective_compute",
           "nop", "engine_nop", "drain", "sem_inc", "register", "alloc_register",
           "reg_mov", "free_register", "wait_op"}


class _EngProxy:

    def __init__(self, eng):
        self._e = eng

    def __getattr__(self, n):
        a = getattr(self._e, n)
        if not callable(a) or n.startswith("_") or n in _NOWRAP:
            return a

        def call(*args, **kw):
            r = a(*args, **kw)
            try:
                out = args[0]
                apl = out.ap
                nfree = 1
                for st_, ct_ in apl[1:]:
                    nfree *= ct_
                if nfree < 128:
                    e2 = self._e
                    for _ in range(2):
                        if hasattr(e2, "engine_nop"):
                            e2.engine_nop()
                        else:
                            e2.nop()
            except Exception:
                pass
            return r
        return call


class Prog:
    NGD = 8
    ENGS = ("t", "v", "s", "g") + tuple(f"gd{k}" for k in range(8)) + ("q",)

    def __init__(self):
        self.steps = []
        self.tagval = {}
        self.cnt = {e: 0 for e in self.ENGS}
        self.default_waits = []
        self._last_gd = [None] * self.NGD
        self._gd_uid = 0

    def on(self, eng, fn, waits=(), tag=None, n_dma=0):
        waits = list(waits) + list(self.default_waits)
        if eng == "g" and n_dma:
            k = self._gd_uid % self.NGD
            self._gd_uid += 1
            eng = f"gd{k}"
            if tag is None:
                tag = f"_gd{self._gd_uid}"
            if self._last_gd[k] is not None:
                waits = waits + [self._last_gd[k]]
            self._last_gd[k] = tag
        self.steps.append([eng, fn, waits, tag, n_dma])

    def gd_frontier(self):
        return [t for t in self._last_gd if t is not None]

    def barrier(self, name):
        self.steps.append(["BAR", None, [], name, 0])

    def resolve(self):
        last_step = {e: None for e in self.ENGS}
        for st in self.steps:
            eng, fn, waits, tag, n_dma = st
            if eng == "BAR":
                snap = {}
                for e in self.ENGS:
                    ls = last_step[e]
                    if ls is not None and ls[3] is None and ls[4] == 0:
                        ls[3] = f"_auto_{id(ls)}"
                        self.cnt[e] += 1
                        self.tagval[ls[3]] = (e, self.cnt[e], False)
                    snap[e] = self.cnt[e]
                self.tagval[tag] = snap
                continue
            if n_dma:
                self.cnt[eng] += 16 * n_dma
                if tag:
                    self.tagval[tag] = (eng, self.cnt[eng], True)
            elif tag:
                self.cnt[eng] += 1
                self.tagval[tag] = (eng, self.cnt[eng], False)
            last_step[eng] = st

    def emit(self, eng_name, eng_obj, sems):
        waited = {e: 0 for e in self.ENGS}
        streams = (("g",) + tuple(f"gd{k}" for k in range(self.NGD))
                   if eng_name == "g" else (eng_name,))
        for st in self.steps:
            eng, fn, waits, tag, n_dma = st
            if eng == "BAR":
                bar = self.tagval[tag]
                continue
            if eng not in streams:
                continue
            for w in waits:
                tv = self.tagval[w]
                if isinstance(tv, dict):
                    for e2 in self.ENGS:
                        val = tv[e2]
                        if val > waited[e2] and val > 0:
                            eng_obj.wait_ge(sems[e2], val)
                            waited[e2] = val
                else:
                    e2, val, isdma = tv
                    if e2 == eng_name and not isdma:
                        continue
                    if val > waited[e2]:
                        eng_obj.wait_ge(sems[e2], val)
                        waited[e2] = val
            insts = fn(_EngProxy(eng_obj) if eng_name in ("v", "s", "g") else eng_obj)
            if n_dma:
                assert isinstance(insts, (list, tuple)) and len(insts) == n_dma, \
                    f"step {tag}: expected {n_dma} dmas, fn returned {insts}"
                for i in insts:
                    i.then_inc(sems[eng], 16)
            elif tag:
                assert insts is not None, f"step {tag}: fn returned None but tagged"
                last = insts[-1] if isinstance(insts, (list, tuple)) else insts
                last.then_inc(sems[eng], 1)


def build(core_consts):
    nc = bass.Bass("TRN2", target_bir_lowering=False, detect_race_conditions=False)

    def inp(name, shape, dtype=F32):
        return nc.dram_tensor(name, shape, dtype, kind="ExternalInput").ap()

    x0 = inp("x0", [T, D])
    x1 = inp("x1", [T, D])
    xmine = inp("xmine", [T, D])
    xmy = inp("xmy", [MYTOK, D])
    wq_t = inp("wq_t", [D, DK])
    wk_t = inp("wk_t", [D, DK])
    wv_t = inp("wv_t", [D, DK])
    wbz = inp("wbz", [D, 9])
    convw = inp("convw", [3 * DK, 4])
    firw = inp("firw", [DK, 56])
    gw1t = inp("gw1t", [1048, 2048])
    gb1 = inp("gb1", [2048, 1])
    gw2t = inp("gw2t", [2048, 24])
    gb2 = inp("gb2", [1, 24])
    alpha24 = inp("alpha24", [1, 24])
    gltemp = inp("gltemp", [1, 4])
    onw = inp("onw", [1, DK])
    wot = inp("wot", [D, D])
    brr = inp("brr", [1, 8])
    w1e = inp("w1e", [D, 4096])
    w2e = inp("w2e", [4096, D])
    b1e = inp("b1e", [4096, 1])
    b2e = inp("b2e", [D, 1])
    c_ident = inp("c_ident", [128, 128])
    c_masklow = inp("c_masklow", [128, 128])
    c_maskupd = inp("c_maskupd", [128, 128])
    c_trils = inp("c_trils", [128, 128])
    c_ones_col = inp("c_ones_col", [128, 1])
    c_ones_row = inp("c_ones_row", [1, 128])
    c_iota8 = inp("c_iota8", [128, 8])
    c_iota256 = inp("c_iota256", [128, 256])
    c_gtok = inp("c_gtok", [128, 256])
    c_myexp = inp("c_myexp", [1, 1])
    c_dump = inp("c_dump", [128, 256])
    c_myrows = inp("c_myrows", [128, 16], U32)
    c_stat_rows = inp("c_stat_rows", [128, 64], U32)
    c_probs_rows = inp("c_probs_rows", [128, 64], U32)
    c_o_rows = inp("c_o_rows", [128, 64], U32)

    out_slice = nc.dram_tensor("out_slice", [MYTOK, D], F32, kind="ExternalOutput").ap()

    def idram(name, shape, dtype=F32):
        kind = "ExternalOutput" if name in DUMPS else "Internal"
        return nc.dram_tensor(name, shape, dtype, kind=kind).ap()

    xn_tm_d = idram("xn_tm_d", [N + 1, D])
    br_d = idram("br_d", [6, T, DK])
    disp_d = idram("disp_d", [CAPP + 32768, 1], U32)
    xg_fm_d = idram("xg_fm_d", [D, CAPP])
    rts_d = idram("rts_d", [N, 4])
    outfm_d = idram("outfm_d", [D, CAPP])
    zdump_d = idram("zdump_d", [128, 128 * 8])
    trace_d = idram("trace_d", [64, 128, 1792])
    routdump_d = idram("routdump_d", [24, 128, 256])

    stats_in = idram("stats_in", [T, 6])
    stats_all = nc.dram_tensor("stats_all", [8, T, 6], F32, addr_space="Shared").ap()
    probs_in = idram("probs_in", [MYTOK, 24])
    probs_all = nc.dram_tensor("probs_all", [8, MYTOK, 24], F32, addr_space="Shared").ap()
    o_in = idram("o_in", [T, DK])
    o_all = nc.dram_tensor("o_all", [4, 8, MYTOK, DK], F32, addr_space="Shared").ap()
    ffn_in = idram("ffn_in", [CAPP, D])
    ffn_all = nc.dram_tensor("ffn_all", [5, 8, 512, D], F32, addr_space="Shared").ap()

    prog = Prog()
    build_phases(nc, prog, locals())

    prog.resolve()

    with ExitStack() as stack:
        semt = stack.enter_context(nc.semaphore("semt"))
        semv = stack.enter_context(nc.semaphore("semv"))
        sems_ = stack.enter_context(nc.semaphore("sems_"))
        semg = stack.enter_context(nc.semaphore("semg"))
        semq = stack.enter_context(nc.semaphore("semq"))
        sems = {"t": semt, "v": semv, "s": sems_, "g": semg, "q": semq}
        for k in range(Prog.NGD):
            sems[f"gd{k}"] = stack.enter_context(nc.semaphore(f"semgd{k}"))
        block = stack.enter_context(nc.Block())

        @block.tensor
        def _(t):
            prog.emit("t", t, sems)

        @block.vector
        def _(v):
            prog.emit("v", v, sems)

        @block.scalar
        def _(s):
            prog.emit("s", s, sems)

        @block.gpsimd
        def _(g):
            prog.emit("g", g, sems)

        @block.sync
        def _(q):
            prog.emit("q", q, sems)

    return nc



def build_phases(nc, prog, io):
    ns = dict(io)

    def A(name):
        return ns[name]

    sb = {}

    def alloc(name, shape, dtype=F32):
        sb[name] = nc.alloc_sbuf_tensor(name, shape, dtype).ap()
        return sb[name]

    ident = alloc("ident", [128, 128])
    masklow = alloc("masklow", [128, 128])
    maskupd = alloc("maskupd", [128, 128])
    trils = alloc("trils", [128, 128])
    ones_col = alloc("ones_col", [128, 1])
    ones_row = alloc("ones_row", [1, 128])
    iota8 = alloc("iota8", [128, 8])
    iota256 = alloc("iota256", [128, 256])
    gtok = alloc("gtok", [128, 256])
    myexp = alloc("myexp", [1, 1])
    wq_sb = alloc("wq_sb", [128, 8, 256])
    wk_sb = alloc("wk_sb", [128, 8, 256])
    wv_sb = alloc("wv_sb", [128, 8, 256])
    wbz_sb = alloc("wbz_sb", [128, 8, 9])
    convw_sb = alloc("convw_sb", [128, 6, 4])
    firw_sb = alloc("firw_sb", [128, 2, 56])
    gw2t_sb = alloc("gw2t_sb", [128, 16, 24])
    gb1_sb = alloc("gb1_sb", [128, 16])
    gb2_row = alloc("gb2_row", [1, 24])
    alpha_row = alloc("alpha_row", [1, 24])
    gltemp_row = alloc("gltemp_row", [1, 4])
    brr_row = alloc("brr_row", [1, 8])
    onw_row = alloc("onw_row", [1, 256])
    onw_b = alloc("onw_b", [128, 256])
    z_sb = alloc("z_sb", [128, 128, 8])
    beta_all = alloc("beta_all", [128, 64])
    stats_sb = alloc("stats_sb", [128, 64, 6])
    S_sb = alloc("S_sb", [128, 2, 256])
    trash = alloc("trash", [128, 1024])
    b2b = alloc("b2b", [128, 24])
    alphab = alloc("alphab", [128, 24])
    itempb = alloc("itempb", [128, 24])
    brrb = alloc("brrb", [128, 8])
    tmp4 = alloc("tmp4", [1, 4])
    cbias = alloc("cbias", [128, 3])
    myexpb = alloc("myexpb", [128, 1])
    th_row = alloc("th_row", [1, 8])
    lo_row = alloc("lo_row", [1, 8])
    hi_row = alloc("hi_row", [1, 8])
    cnt_row = alloc("cnt_row", [1, 8])
    slb_row = alloc("slb_row", [1, 8])
    disp_pre = alloc("disp_pre", [128, 20], U32)
    posu = alloc("posu", [128, 256], U32)
    sgidx = alloc("sgidx", [128, 20], U32)
    myrows_u = alloc("myrows_u", [128, 16], U32)
    statrows_u = alloc("statrows_u", [128, 64], U32)
    probrows_u = alloc("probrows_u", [128, 64], U32)
    orows_u = alloc("orows_u", [128, 64], U32)
    gidx0_u = alloc("gidx0_u", [128, 16], U32)
    gidx1_u = alloc("gidx1_u", [128, 16], U32)
    gtoku = alloc("gtoku", [128, 256], U32)
    dumpt = alloc("dumpt", [128, 256])
    zer256 = alloc("zer256", [128, 256])

    SCRW = 33000
    scr_t = nc.alloc_sbuf_tensor("scratch", [128, SCRW], F32)
    scr = scr_t.ap()

    def view(off, shape):
        n = 1
        for d_ in shape[1:]:
            n *= d_
        assert off + n <= SCRW, f"scratch overflow {off}+{n}"
        a = scr[:, off:off + n]
        if len(shape) > 2:
            lab = "abcdefg"[: len(shape) - 1]
            pat = f"p ({' '.join(lab)}) -> p {' '.join(lab)}"
            kw = {lab[i]: shape[i + 1] for i in range(1, len(shape) - 1)}
            a = a.rearrange(pat, **kw)
        return a

    pb = [nc.alloc_psum_tensor(f"pb{i}", [128, 512], F32).ap() for i in range(8)]

    bank_last = {i: "setup_done" for i in range(8)}

    def mmstep(bk, fn, waits=(), pfx="mm"):
        t_ = tag(pfx)
        prog.on("t", fn, waits=list(waits) + [bank_last[bk]], tag=t_)
        return t_

    def evstep(bk, fn, eng="v", waits=(), pfx="ev"):
        t_ = tag(pfx)
        prog.on(eng, fn, waits=waits, tag=t_)
        bank_last[bk] = t_
        return t_

    def stop_if(name):
        if TRUNC != name:
            return False
        if "zdump_d" in DUMPS:
            prog.on("q", lambda q: [q.dma_start(
                out=A("zdump_d"), in_=z_sb[:].rearrange("p a b -> p (a b)"))],
                waits=[name], n_dma=1, tag="zd")
        if "routdump_d" in DUMPS:
            rl = [("items_g", items_g), ("items_e", items_e), ("kept", keptI),
                  ("slot", slotI), ("thi", thiI), ("gidx", r1), ("gg", ggi0),
                  ("pos", posf)]
            for ri, (nm, ap_) in enumerate(rl):
                prog.on("q", lambda q, ri=ri, ap_=ap_: [q.dma_start(
                    out=A("routdump_d")[ri], in_=ap_)], waits=[name], n_dma=1,
                    tag=f"rd{ri}")
        prog.on("v", lambda v: v.memset(trash[:], 0.0), waits=[name], tag="fz")
        prog.on("q", lambda q: [q.dma_start(
            out=A("out_slice")[0:128, :], in_=trash[:, 0:1024])],
            waits=["fz"], n_dma=1, tag="fzw")
        prog.barrier("end")
        prog.on("q", lambda q: None, waits=["end"])
        return True


    uid = [0]

    def tag(pfx):
        uid[0] += 1
        return f"{pfx}_{uid[0]}"

    def bcast_row(row_ap, width, out_sb, bank, wname, waits=()):
        t1 = mmstep(bank, lambda t, r=row_ap, b=bank, w=width: t.matmul(
            pb[b][:, :w], ones_row, r, start=True, stop=True), waits=waits, pfx="bc")
        return evstep(bank, lambda v, o=out_sb, b=bank, w=width: v.tensor_copy(
            o[:, :w], pb[b][:, :w]), "v", [t1], "bce")

    def ld(dst, src):
        return lambda q: [q.dma_start(out=dst, in_=src)]

    setup_loads = [
        (ident, A("c_ident")), (masklow, A("c_masklow")), (maskupd, A("c_maskupd")),
        (trils, A("c_trils")), (ones_col, A("c_ones_col")), (ones_row, A("c_ones_row")),
        (iota8, A("c_iota8")), (iota256, A("c_iota256")), (gtok, A("c_gtok")),
        (myexp, A("c_myexp")),
        (wq_sb, A("wq_t").rearrange("(kc p) f -> p kc f", p=128)),
        (wk_sb, A("wk_t").rearrange("(kc p) f -> p kc f", p=128)),
        (wv_sb, A("wv_t").rearrange("(kc p) f -> p kc f", p=128)),
        (wbz_sb, A("wbz").rearrange("(kc p) f -> p kc f", p=128)),
        (convw_sb, A("convw").rearrange("(fc p) k -> p fc k", p=128)),
        (firw_sb, A("firw").rearrange("(dc p) k -> p dc k", p=128)),
        (gw2t_sb, A("gw2t").rearrange("(kc p) f -> p kc f", p=128)),

        (gb2_row, A("gb2")), (alpha_row, A("alpha24")), (gltemp_row, A("gltemp")),
        (onw_row, A("onw")),
        (brr_row, A("brr")),
        (dumpt, A("c_dump")),
        (myrows_u, A("c_myrows")), (statrows_u, A("c_stat_rows")),
        (probrows_u, A("c_probs_rows")), (orows_u, A("c_o_rows")),
    ]
    for i, (dst, src) in enumerate(setup_loads):
        prog.on("q", ld(dst[:] if hasattr(dst, "tensor") else dst, src),
                tag=f"sl_{i}", n_dma=1)
    prog.on("q", lambda q: [q.dma_start(
        out=gb1_sb[:], in_=A("gb1").rearrange("(kc p) one -> p (kc one)", p=128),
        allow_slow_non_contiguous=True)], tag="sl_gb1", n_dma=1)
    prog.barrier("setup_done")

    bcast_row(onw_row[0:1, :], 256, onw_b, 6, "onw", waits=["setup_done"])
    bcast_row(gb2_row[0:1, :], 24, b2b, 6, "b2b")
    bcast_row(alpha_row[0:1, :], 24, alphab, 6, "alpha")
    bcast_row(brr_row[0:1, :], 8, brrb, 6, "brr")
    bcast_row(myexp[0:1, 0:1], 1, myexpb, 6, "mye")
    prog.on("v", lambda v: v.memset(trash[:], 0.0), tag="zt")
    prog.on("q", lambda q: [q.dma_start(out=A("xn_tm_d")[N:N + 1, :],
                                        in_=trash[0:1, :])], waits=["zt"], n_dma=1, tag="zdum")
    prog.on("v", lambda v: v.memset(S_sb[:], 0.0), tag="sinit")
    prog.on("v", lambda v: v.memset(zer256[:], 0.0), tag="z256")

    def _cb(v):
        v.memset(cbias[:, 0:1], 1.0e-5)
        v.memset(cbias[:, 1:2], 1.0e-12)
        return v.memset(cbias[:, 2:3], 1.0)
    prog.on("v", _cb, tag="cbias")
    prog.on("s", lambda s: s.activation(tmp4[0:1, :], gltemp_row[0:1, :], AF.Exp),
            waits=["setup_done"], tag="t4a")
    prog.on("s", lambda s: s.activation(tmp4[0:1, :], tmp4[0:1, :], AF.Ln, bias=cbias[0:1, 2:3]),
            waits=["t4a", "cbias"], tag="t4b")
    prog.on("v", lambda v: v.tensor_scalar_add(tmp4[0:1, :], tmp4[0:1, :], 1.0e-4),
            waits=["t4b"], tag="t4c")
    prog.on("v", lambda v: v.reciprocal(tmp4[0:1, :], tmp4[0:1, :]), waits=["t4c"], tag="t4d")
    itemp_view = bass.AP(tmp4.tensor, tmp4.offset, [tmp4.ap[0], [1, 4], [0, 6]])
    bcast_row(itemp_view, 24, itempb, 7, "itemp", waits=["t4d"])
    prog.barrier("p0")
    prog.default_waits = ["p0"]
    if stop_if("p0"):
        return

    x_t = view(0, [128, SUB, 1024])
    xn_fm = view(4096, [128, 8, TT])
    mu_t = view(8192, [128, SUB])
    s2_t = view(8208, [128, SUB])
    sd_t = view(8224, [128, SUB])

    def emit_ln(xin_ap, it, pfx, ld_waits=()):
        tl = tag(pfx + "ld")
        prog.on("q", lambda q, a=xin_ap: [q.dma_start(
            out=x_t[:], in_=a.rearrange("(s p) d -> p s d", p=128))],
            waits=ld_waits, n_dma=1, tag=tl)
        t1 = tag(pfx + "mu")
        prog.on("v", lambda v: v.tensor_reduce(mu_t[:], x_t[:], AX.X, ALU.add),
                waits=[tl], tag=t1)
        sq = []
        for si in range(SUB):
            ts_ = tag(pfx + "sq")
            sq.append(ts_)
            prog.on("s", lambda s, si=si: s.activation(
                trash[:, 0:1024], x_t[:, si, :], AF.Square,
                accum_out=s2_t[:, si:si + 1]), waits=[tl], tag=ts_)
        t2 = tag(pfx + "st")

        def stats(v):
            v.tensor_scalar_mul(mu_t[:], mu_t[:], 1.0 / 1024.0)
            v.tensor_scalar_mul(s2_t[:], s2_t[:], 1.0 / 1024.0)
            v.tensor_tensor(sd_t[:], mu_t[:], mu_t[:], ALU.mult)
            return v.tensor_tensor(s2_t[:], s2_t[:], sd_t[:], ALU.subtract)
        prog.on("v", stats, waits=[t1] + sq, tag=t2)
        t3 = tag(pfx + "sd")
        prog.on("s", lambda s: s.activation(sd_t[:], s2_t[:], AF.Sqrt, bias=cbias[:, 0:1]),
                waits=[t2], tag=t3)
        t4 = tag(pfx + "rs")
        prog.on("v", lambda v: v.reciprocal(sd_t[:], sd_t[:]), waits=[t3], tag=t4)
        t5 = tag(pfx + "xn")

        def xn(v):
            r = None
            for si in range(SUB):
                r = v.tensor_scalar(x_t[:, si, :], x_t[:, si, :],
                                    mu_t[:, si:si + 1], sd_t[:, si:si + 1],
                                    ALU.subtract, ALU.mult)
            return r
        prog.on("v", xn, waits=[t4], tag=t5)
        return t5

    def emit_tr_fm(it, pfx, xn_tag):
        evs = []
        for kc in range(8):
            bk = 6 + (kc % 2)

            def mms(t, kc=kc, bk=bk):
                r = None
                for si in range(SUB):
                    r = t.transpose(pb[bk][:, si * 128:(si + 1) * 128],
                                    x_t[:, si, kc * 128:(kc + 1) * 128], ident)
                return r
            tm = mmstep(bk, mms, waits=[xn_tag], pfx=pfx + "trm")
            eng = "v" if kc % 2 == 0 else "s"
            if eng == "v":
                te = evstep(bk, lambda v, kc=kc, bk=bk: v.tensor_copy(
                    xn_fm[:, kc, :], pb[bk][:, :]), "v", [tm], pfx + "tre")
            else:
                te = evstep(bk, lambda s, kc=kc, bk=bk: s.copy(
                    xn_fm[:, kc, :], pb[bk][:, :]), "s", [tm], pfx + "tre")
            evs.append(te)
        return evs

    def emit_z(gi0, pfx, fm_tags):
        outs = []
        for si in range(SUB):
            def mms(t, si=si):
                r = None
                for kc in range(8):
                    r = t.matmul(pb[5][:, si * 8:si * 8 + 8],
                                 xn_fm[:, kc, si * 128:(si + 1) * 128],
                                 wbz_sb[:, kc, 0:8],
                                 start=(kc == 0), stop=(kc == 7))
                return r
            outs.append(mmstep(5, mms, waits=fm_tags, pfx=pfx + "zmm"))
        tz = evstep(5, lambda v, gi0=gi0: v.tensor_copy(
            z_sb[:, gi0:gi0 + SUB, :],
            pb[5][:, 0:SUB * 8].rearrange("p (a b) -> p a b", b=8)),
            "v", outs, pfx + "zev")
        return [tz]

    for bi in range(2):
        src = A("x0") if bi == 0 else A("x1")
        for it in range(NTT):
            pfx = f"al{bi}_{it}_"
            rows = src[it * TT:(it + 1) * TT, :]
            xn_tag = emit_ln(rows, it, pfx)
            prog.on("q", lambda q, bi=bi, it=it: [q.dma_start(
                out=A("xn_tm_d").rearrange("n d -> n d")[bi * T + it * TT: bi * T + (it + 1) * TT, :]
                    .rearrange("(s p) d -> p s d", p=128),
                in_=x_t[:])], waits=[xn_tag], n_dma=1, tag=tag(pfx + "xnw"))
            fm_tags = emit_tr_fm(it, pfx, xn_tag)
            emit_z(bi * 64 + it * SUB, pfx, fm_tags[-2:])
            bt = f"tb_al{bi}_{it}"
            prog.barrier(bt)
            prog.default_waits = [bt]
    prog.barrier("alite")
    prog.default_waits = ["alite"]
    if stop_if("alite"):
        return

    proj = view(8240, [128, 6, 515])
    vfir = view(11330, [128, 2, 542])
    fir_fm = view(12414, [128, 2, 512])
    brA = view(13438, [128, 4, 256])
    brB = view(14462, [128, 4, 256])
    v_tm = view(15486, [128, 4, 256])
    q_tm = view(16510, [128, 4, 256])
    k_tm = view(17534, [128, 4, 256])
    kn_tm = view(18558, [128, 4, 256])
    rhs_uw = view(19582, [128, 4, 512])
    m_a = view(21630, [128, 4, 128])
    m_b = view(22142, [128, 4, 128])
    n_a = view(22654, [128, 4, 128])
    n_b = view(23166, [128, 4, 128])
    invP = view(23678, [128, 4, 128])
    attnT = view(24190, [128, 4, 128])
    uw_all = view(24702, [128, 4, 512])
    wfm = view(26750, [128, 4, 256])
    gsc = view(27774, [128, 128])
    u_fin = view(27902, [128, 256])
    ssq_q = view(28158, [128, 4])
    ssq_k = view(28162, [128, 4])
    rn_q = view(28166, [128, 4])
    rn_k = view(28170, [128, 4])
    rs_q = view(28174, [128, 4])
    rs_k = view(28178, [128, 4])
    ccA = view(28200, [128, 512])
    ccB = view(28712, [128, 512])

    identb4 = bass.AP(ident.tensor, ident.offset,
                      [ident.ap[0], [0, 4], [1, 128]])


    FIROFF = {3: 0, 7: 3, 15: 10, 31: 25}

    def conv_taps(dst_ap, src_buf, fch, taps, woff, halo, pfx, waits):
        wsb, wslot = (convw_sb, fch) if taps == 4 else (firw_sb, fch)
        last = None
        tprev = tag(pfx + "c0")

        def t0(v):
            return v.tensor_scalar_mul(ccA[:], src_buf[:, fch, 0 + (halo - (taps - 1)):0 + (halo - (taps - 1)) + 512] if False else src_buf[:, fch, (halo - (taps - 1)):(halo - (taps - 1)) + 512],
                                       wsb[:, wslot, woff:woff + 1])
        prog.on("v", t0, waits=waits, tag=tprev)
        cur, nxt = ccA, ccB
        for j in range(1, taps):
            tj = tag(pfx + f"c{j}")
            dst = dst_ap if j == taps - 1 else nxt

            def tap(v, j=j, cur=cur, dst=dst):
                return v.scalar_tensor_tensor(
                    dst, src_buf[:, fch, (halo - (taps - 1) + j):(halo - (taps - 1) + j) + 512],
                    wsb[:, wslot, woff + j:woff + j + 1], cur, ALU.mult, ALU.add)
            prog.on("v", tap, waits=[tprev], tag=tj)
            tprev = tj
            cur, nxt = nxt, cur
        return tprev

    def transpose4(src_fn, dst_ap, bk, pfx, waits, eng="v"):
        tm = tag(pfx + "t4")

        def mms(t):
            r = None
            for si in range(SUB):
                r = t.transpose(pb[bk][:, si * 128:(si + 1) * 128], src_fn(si), ident)
            return r
        mmt = mmstep(bk, mms, waits=waits, pfx=pfx + "t4")
        if eng == "v":
            ev = evstep(bk, lambda v: v.tensor_copy(dst_ap, pb[bk][:]), "v", [mmt], pfx + "t4e")
        else:
            ev = evstep(bk, lambda s: s.copy(dst_ap, pb[bk][:]), "s", [mmt], pfx + "t4e")
        return ev

    prev_S = ("sinit",)
    for it in range(NTT):
        pfx = f"af{it}_"
        rows = A("xmine")[it * TT:(it + 1) * TT, :]
        xn_tag = emit_ln(rows, it, pfx)
        fm_tags = emit_tr_fm(it, pfx, xn_tag)
        fm_done = fm_tags[-2:]
        bmm = mmstep(6, lambda t: [t.matmul(pb[6][:, si:si + 1],
                                            xn_fm[:, kc, si * 128:(si + 1) * 128],
                                            wbz_sb[:, kc, 8:9],
                                            start=(kc == 0), stop=(kc == 7))
                                   for si in range(SUB) for kc in range(8)][-1],
                     waits=fm_done, pfx=pfx + "bmm")
        bev = evstep(6, lambda s, it=it: s.activation(
            beta_all[:, it * SUB:(it + 1) * SUB], pb[6][:, 0:SUB], AF.Sigmoid),
            "s", [bmm], pfx + "bev")
        pe = []
        for f in range(6):
            wsb = (wq_sb, wq_sb, wk_sb, wk_sb, wv_sb, wv_sb)[f]
            dc = f % 2
            bk = f % 6
            pm = mmstep(bk, lambda t, wsb=wsb, dc=dc, bk=bk: [
                t.matmul(pb[bk][:], wsb[:, kc, dc * 128:(dc + 1) * 128],
                         xn_fm[:, kc, :], start=(kc == 0), stop=(kc == 7))
                for kc in range(8)][-1], waits=fm_done, pfx=pfx + f"pj{f}")
            pe0 = []
            if it == 0:
                zi = tag(pfx + f"hz{f}")

                def zf(v, f=f):
                    r = v.memset(proj[:, f, 0:3], 0.0)
                    if f >= 4:
                        r = v.memset(vfir[:, f - 4, 0:30], 0.0)
                    return r
                prog.on("v", zf, tag=zi)
                pe0 = [zi]
            ev = evstep(bk, (lambda v, f=f, bk=bk: v.tensor_copy(proj[:, f, 3:515], pb[bk][:]))
                        if f % 2 == 0 else
                        (lambda s, f=f, bk=bk: s.copy(proj[:, f, 3:515], pb[bk][:])),
                        "v" if f % 2 == 0 else "s", [pm] + pe0, pfx + f"pje{f}")
            pe.append(ev)
        prev_sl = []
        qk_silu = []
        v_silu = []
        for f in range(6):
            cv = conv_taps(ccB[:], proj, f, 4, 0, 3, pfx + f"cv{f}_", [pe[f]] + prev_sl)
            hs = tag(pfx + f"hs{f}")
            prog.on("v", lambda v, f=f: v.tensor_copy(proj[:, f, 0:3], proj[:, f, 512:515]),
                    waits=[cv], tag=hs)
            sl = tag(pfx + f"sl{f}")
            if f < 4:
                prog.on("s", lambda s, f=f: s.activation(proj[:, f, 3:515], ccB[:], AF.Silu),
                        waits=[cv, hs], tag=sl)
                qk_silu.append(sl)
            else:
                prog.on("s", lambda s, f=f: s.activation(vfir[:, f - 4, 30:542], ccB[:], AF.Silu),
                        waits=[cv, hs], tag=sl)
                v_silu.append(sl)
            prev_sl = [sl]
        vt = []
        for dc in range(2):
            vt.append(transpose4(lambda si, dc=dc: vfir[:, dc, 30 + si * 128:30 + (si + 1) * 128],
                                 v_tm[:, :, dc * 128:(dc + 1) * 128], 7, pfx + f"vt{dc}",
                                 v_silu, "v"))
        st_v = []
        for si in range(SUB):
            t_ = tag(pfx + f"stv{si}")
            prog.on("s", lambda s, si=si, it=it: s.activation(
                trash[:, 0:256], v_tm[:, si, :], AF.Abs, scale=1.0 / 256.0,
                accum_out=stats_sb[:, it * SUB + si, 4:5]), waits=vt, tag=t_)
            st_v.append(t_)
        prog.on("q", lambda q, it=it: [q.dma_start(
            out=A("br_d")[4, it * TT:(it + 1) * TT, :].rearrange("(s p) d -> p s d", p=128),
            in_=v_tm[:])], waits=vt, n_dma=1, tag=tag(pfx + "vspill"))
        qt = []
        for dc in range(2):
            qt.append(transpose4(lambda si, dc=dc: proj[:, dc, 3 + si * 128:3 + (si + 1) * 128],
                                 q_tm[:, :, dc * 128:(dc + 1) * 128], 7, pfx + f"qt{dc}",
                                 qk_silu, "v"))
            qt.append(transpose4(lambda si, dc=dc: proj[:, 2 + dc, 3 + si * 128:3 + (si + 1) * 128],
                                 k_tm[:, :, dc * 128:(dc + 1) * 128], 7, pfx + f"kt{dc}",
                                 qk_silu, "s"))
        sq_tags = []
        for si in range(SUB):
            t1 = tag(pfx + f"sqq{si}")
            prog.on("s", lambda s, si=si: s.activation(
                trash[:, 0:256], q_tm[:, si, :], AF.Square,
                accum_out=ssq_q[:, si:si + 1]), waits=qt, tag=t1)
            t2 = tag(pfx + f"sqk{si}")
            prog.on("s", lambda s, si=si: s.activation(
                trash[:, 256:512], k_tm[:, si, :], AF.Square,
                accum_out=ssq_k[:, si:si + 1]), waits=qt, tag=t2)
            sq_tags += [t1, t2]
        trn = tag(pfx + "rn")

        def rnf(s):
            s.activation(rs_q[:], ssq_q[:], AF.Sqrt, bias=cbias[:, 1:2])
            return s.activation(rs_k[:], ssq_k[:], AF.Sqrt, bias=cbias[:, 1:2])
        prog.on("s", rnf, waits=sq_tags, tag=trn)
        trn1 = tag(pfx + "rn1")

        def rnr(v):
            v.reciprocal(rs_q[:], rs_q[:])
            return v.reciprocal(rs_k[:], rs_k[:])
        prog.on("v", rnr, waits=[trn], tag=trn1)
        trn2 = tag(pfx + "rn2")

        def rnc(s):
            s.activation(rn_q[:], rs_q[:], AF.Copy)
            return s.activation(rn_k[:], rs_k[:], AF.Copy)
        prog.on("s", rnc, waits=[trn1], tag=trn2)
        prep_rhs = []
        for si in range(SUB):
            t1 = tag(pfx + f"kn{si}")
            prog.on("v", lambda v, si=si: v.tensor_scalar_mul(
                kn_tm[:, si, :], k_tm[:, si, :], rn_k[:, si:si + 1]), waits=[trn2], tag=t1)
            t2 = tag(pfx + f"rh{si}")

            def rh(v, si=si, it=it):
                v.tensor_scalar_mul(rhs_uw[:, si, 0:256], v_tm[:, si, :],
                                    beta_all[:, it * SUB + si:it * SUB + si + 1])
                return v.tensor_scalar_mul(rhs_uw[:, si, 256:512], kn_tm[:, si, :],
                                           beta_all[:, it * SUB + si:it * SUB + si + 1])
            prog.on("v", rh, waits=[t1, bev] + vt, tag=t2)
            prep_rhs.append(t2)
        br_bufs = [brA, brB]
        fir_sp = []
        for bi_, kk in enumerate((3, 7, 15, 31)):
            bb = br_bufs[bi_ % 2]
            buf_free = [fir_sp[bi_ - 2]] if bi_ >= 2 else []
            fe = []
            for dc in range(2):
                cv = conv_taps(fir_fm[:, dc, :], vfir, dc, kk, FIROFF[kk], 30,
                               pfx + f"f{kk}_{dc}_", v_silu + fir_sp[-1:])
                fe.append(transpose4(lambda si, dc=dc: fir_fm[:, dc, si * 128:(si + 1) * 128],
                                     bb[:, :, dc * 128:(dc + 1) * 128], 7,
                                     pfx + f"ft{kk}{dc}", [cv] + buf_free, "v"))
            for si in range(SUB):
                prog.on("s", lambda s, si=si, it=it, bi_=bi_, bb=bb: s.activation(
                    trash[:, 0:256], bb[:, si, :], AF.Abs, scale=1.0 / 256.0,
                    accum_out=stats_sb[:, it * SUB + si, bi_:bi_ + 1]), waits=fe,
                    tag=tag(pfx + f"stf{kk}{si}"))
            sp = tag(pfx + f"fsp{kk}")
            prog.on("q", lambda q, it=it, bi_=bi_, bb=bb: [q.dma_start(
                out=A("br_d")[bi_, it * TT:(it + 1) * TT, :].rearrange("(s p) d -> p s d", p=128),
                in_=bb[:])], waits=fe, n_dma=1, tag=sp)
            fir_sp.append(sp)
        prog.on("v", lambda v: v.tensor_copy(vfir[:, :, 0:30], vfir[:, :, 512:542]),
                waits=fir_sp[-1:], tag=tag(pfx + "vhalo"))
        uw_tags = []
        wfm_tags = []
        at_tags = []
        n_tags = []
        m_tags = []
        for ci in range(NCH):
            cpfx = pfx + f"c{ci}_"
            if SCAN_BAR >= 2:
                prog.barrier(cpfx + "PBAR")
                prog.default_waits = [cpfx + "PBAR"]
            gm = mmstep(0, lambda t, ci=ci: [t.matmul(
                pb[0][:, 0:128], proj[:, 2 + dc, 3 + ci * 128:3 + (ci + 1) * 128],
                proj[:, 2 + dc, 3 + ci * 128:3 + (ci + 1) * 128],
                start=(dc == 0), stop=(dc == 1)) for dc in range(2)][-1],
                waits=qk_silu, pfx=cpfx + "G")
            g1 = evstep(0, lambda v, ci=ci: v.tensor_scalar_mul(
                gsc[:], pb[0][:, 0:128], rn_k[:, ci:ci + 1]), "v", [gm, trn2], cpfx + "g1")
            gt = mmstep(1, lambda t: t.transpose(pb[1][:, 0:128], gsc[:], ident),
                        waits=[g1], pfx=cpfx + "gt")
            gn = evstep(1, lambda v, ci=ci: v.tensor_scalar_mul(
                gsc[:], pb[1][:, 0:128], rn_k[:, ci:ci + 1]), "v", [gt], cpfx + "gn")
            tm_ = tag(cpfx + "M")
            prog.on("v", lambda v, ci=ci, it=it: v.scalar_tensor_tensor(
                m_a[:, ci, :], gsc[:], beta_all[:, it * SUB + ci:it * SUB + ci + 1],
                masklow[:], ALU.mult, ALU.mult), waits=[gn, bev], tag=tm_)
            m_tags.append(tm_)
            nt = mmstep(1, lambda t, ci=ci: t.transpose(pb[1][:, 0:128], m_a[:, ci, :], ident),
                        waits=[tm_], pfx=cpfx + "Nt")
            nev = evstep(1, lambda v, ci=ci: v.tensor_copy(n_a[:, ci, :], pb[1][:, 0:128]),
                         "v", [nt], cpfx + "Ne")
            n_tags.append(nev)
            am = mmstep(2, lambda t, ci=ci: [t.matmul(
                pb[2][:, 0:128], proj[:, 2 + dc, 3 + ci * 128:3 + (ci + 1) * 128],
                proj[:, 0 + dc, 3 + ci * 128:3 + (ci + 1) * 128],
                start=(dc == 0), stop=(dc == 1)) for dc in range(2)][-1],
                waits=qk_silu, pfx=cpfx + "at")
            ae = evstep(2, lambda v, ci=ci: v.scalar_tensor_tensor(
                attnT[:, ci, :], pb[2][:, 0:128], rn_k[:, ci:ci + 1], maskupd[:],
                ALU.mult, ALU.mult), "v", [am, trn2], cpfx + "ae")
            at_tags.append(ae)
        pi0 = tag(pfx + "pi0")
        prog.on("v", lambda v: v.tensor_tensor(
            invP[:], n_a[:], identb4, ALU.add), waits=n_tags, tag=pi0)
        mcur, ncur = m_a, n_a
        mnext, nnext = m_b, n_b
        pprev = pi0
        mprev, nprev = m_tags[-1], n_tags[-1]
        for lvl in range(1, 7):
            if SCAN_BAR >= 2:
                prog.barrier(pfx + f"LBAR{lvl}")
                prog.default_waits = [pfx + f"LBAR{lvl}"]
            mm_t = mmstep(3, lambda t, mcur=mcur, ncur=ncur: [t.matmul(
                pb[3][:, ci * 128:(ci + 1) * 128], ncur[:, ci, :], mcur[:, ci, :],
                start=True, stop=True) for ci in range(NCH)][-1],
                waits=[mprev, nprev], pfx=pfx + f"pw{lvl}m")
            me = evstep(3, lambda v, mnext=mnext: v.tensor_copy(mnext[:], pb[3][:]),
                        "v", [mm_t], pfx + f"pw{lvl}me")
            if lvl < 6:
                nn_t = mmstep(4, lambda t, mcur=mcur, ncur=ncur: [t.matmul(
                    pb[4][:, ci * 128:(ci + 1) * 128], mcur[:, ci, :], ncur[:, ci, :],
                    start=True, stop=True) for ci in range(NCH)][-1],
                    waits=[mprev, nprev], pfx=pfx + f"pw{lvl}n")
                ne = evstep(4, lambda s, nnext=nnext: s.copy(nnext[:], pb[4][:]),
                            "s", [nn_t], pfx + f"pw{lvl}ne")
            else:
                ne = me
            pu = mmstep(5, lambda t, mnext=mnext: [t.matmul(
                pb[5][:, ci * 128:(ci + 1) * 128], mnext[:, ci, :], invP[:, ci, :],
                start=True, stop=True) for ci in range(NCH)][-1],
                waits=[me, pprev], pfx=pfx + f"pu{lvl}")
            pa = evstep(5, lambda v: v.tensor_tensor(invP[:], invP[:], pb[5][:], ALU.add),
                        "v", [pu], pfx + f"pa{lvl}")
            pprev = pa
            mprev, nprev = me, ne
            mcur, ncur = mnext, nnext
            mnext, nnext = (m_a, n_a) if lvl % 2 == 1 else (m_b, n_b)
        for ci in range(NCH):
            cpfx = pfx + f"s{ci}_"
            if SCAN_BAR >= 2:
                prog.barrier(cpfx + "SBAR")
                prog.default_waits = [cpfx + "SBAR"]
            um = mmstep(0, lambda t, ci=ci: t.matmul(
                pb[0][:], invP[:, ci, :], rhs_uw[:, ci, :], start=True, stop=True),
                waits=[pprev, prep_rhs[ci]], pfx=cpfx + "uw")
            ue = evstep(0, lambda v, ci=ci: v.tensor_copy(uw_all[:, ci, :], pb[0][:]),
                        "v", [um], cpfx + "uwe")
            uw_tags.append(ue)
            wm = mmstep(1, lambda t, ci=ci: [t.matmul(
                pb[1][:, dc * 128:(dc + 1) * 128],
                rhs_uw[:, ci, 256 + dc * 128:256 + (dc + 1) * 128],
                invP[:, ci, :], start=True, stop=True) for dc in range(2)][-1],
                waits=[pprev, prep_rhs[ci]], pfx=cpfx + "wf")
            we = evstep(1, lambda s, ci=ci: s.copy(wfm[:, ci, :], pb[1][:, 0:256]),
                        "s", [wm], cpfx + "wfe")
            wfm_tags.append(we)
        for ci in range(NCH):
            cpfx = pfx + f"sc{ci}_"
            if SCAN_BAR >= 1:
                prog.barrier(cpfx + "BAR")
                prog.default_waits = [cpfx + "BAR"]
            uc = mmstep(2, lambda t, ci=ci: [t.matmul(
                pb[2][:, 0:256], wfm[:, ci, dc * 128:(dc + 1) * 128],
                S_sb[:, dc, :], start=(dc == 0), stop=(dc == 1)) for dc in range(2)][-1],
                waits=[wfm_tags[ci]] + list(prev_S), pfx=cpfx + "uc")
            uf = evstep(2, lambda v, ci=ci: v.tensor_tensor(
                u_fin[:], uw_all[:, ci, 0:256], pb[2][:, 0:256], ALU.subtract),
                "v", [uc, uw_tags[ci]], cpfx + "uf")
            om = mmstep(3, lambda t, ci=ci: [t.matmul(
                pb[3][:, 0:256], proj[:, 0 + dc, 3 + ci * 128:3 + (ci + 1) * 128],
                S_sb[:, dc, :], start=(dc == 0), stop=False) for dc in range(2)][-1] and
                t.matmul(pb[3][:, 0:256], attnT[:, ci, :], u_fin[:],
                         start=False, stop=True),
                waits=list(prev_S) + [at_tags[ci], uf], pfx=cpfx + "om")
            oe = evstep(3, lambda v, ci=ci: v.tensor_scalar_mul(
                brA[:, ci, :], pb[3][:, 0:256], rn_q[:, ci:ci + 1]), "v",
                [om, fir_sp[2]], cpfx + "oe")
            sm0 = mmstep(4, lambda t, ci=ci: t.matmul(
                pb[4][:, 0:256], kn_tm[:, ci, 0:128], u_fin[:], start=True, stop=True),
                waits=[uf], pfx=cpfx + "sm0")
            sm1 = mmstep(5, lambda t, ci=ci: t.matmul(
                pb[5][:, 0:256], kn_tm[:, ci, 128:256], u_fin[:], start=True, stop=True),
                waits=[uf], pfx=cpfx + "sm1")
            sa0 = evstep(4, lambda v: v.tensor_tensor(
                S_sb[:, 0, :], S_sb[:, 0, :], pb[4][:, 0:256], ALU.add), "v", [sm0], cpfx + "sa0")
            sa1 = evstep(5, lambda v: v.tensor_tensor(
                S_sb[:, 1, :], S_sb[:, 1, :], pb[5][:, 0:256], ALU.add), "v", [sm1], cpfx + "sa1")
            prev_S = (sa0, sa1)
            if TRACE_SC:
                gci = it * NCH + ci
                def trc(q, ci=ci, gci=gci):
                    return [q.dma_start(out=A("trace_d")[gci, :, 0:512],
                                        in_=S_sb[:].rearrange("p a b -> p (a b)")),
                            q.dma_start(out=A("trace_d")[gci, :, 512:768], in_=u_fin[:]),
                            q.dma_start(out=A("trace_d")[gci, :, 768:1280],
                                        in_=uw_all[:, ci, :]),
                            q.dma_start(out=A("trace_d")[gci, :, 1280:1536],
                                        in_=wfm[:, ci, :]),
                            q.dma_start(out=A("trace_d")[gci, :, 1536:1664],
                                        in_=invP[:, ci, :]),
                            q.dma_start(out=A("trace_d")[gci, :, 1664:1792],
                                        in_=attnT[:, ci, :])]
                prog.on("q", trc, waits=[sa1, oe], n_dma=6, tag=tag(cpfx + "trc"))
                prog.barrier(cpfx + "TBAR")
                prog.default_waits = [cpfx + "TBAR"]
            prog.on("s", lambda s, ci=ci, it=it: s.activation(
                trash[:, 0:256], brA[:, ci, :], AF.Abs, scale=1.0 / 256.0,
                accum_out=stats_sb[:, it * SUB + ci, 5:6]), waits=[oe],
                tag=tag(cpfx + "std"))
        prog.on("q", lambda q, it=it: [q.dma_start(
            out=A("br_d")[5, it * TT:(it + 1) * TT, :].rearrange("(s p) d -> p s d", p=128),
            in_=brA[:])], waits=list(prev_S), n_dma=1, tag=tag(pfx + "dsp"))
        bt = f"tb_af{it}"
        prog.barrier(bt)
        prog.default_waits = [bt]
    prog.barrier("afull")
    prog.default_waits = ["afull"]
    if stop_if("afull"):
        return
    prog.on("q", lambda q: [q.dma_start(
        out=A("stats_in").rearrange("(g p) f -> p g f", p=128),
        in_=stats_sb[:])], n_dma=1, tag="statw")

    items_g = view(0, [128, 256])
    items_e = view(256, [128, 256])
    eqs = view(512, [128, 8, 256])
    gme = view(2560, [128, 8, 256])
    work = view(4608, [128, 8, 256])
    work2 = view(6656, [128, 8, 256])
    scn = view(8704, [128, 8, 256])
    g1t = view(10752, [128, 128])
    id1t = view(10880, [128, 128])
    g2t = view(11008, [128, 128])
    id2t = view(11136, [128, 128])
    ebuf = view(11264, [128, 128, 8])
    ebuf2 = view(12288, [128, 128, 8])
    zred = view(13312, [128, 128])
    zred2 = view(13440, [128, 128])
    keptI = view(13568, [128, 256])
    slotI = view(13824, [128, 256])
    thiI = view(14080, [128, 256])
    gtI = view(14336, [128, 256])
    eqthI = view(14592, [128, 256])
    r1 = view(14848, [128, 256])
    r2 = view(15104, [128, 256])
    r3 = view(15360, [128, 256])
    posf = view(15616, [128, 256])
    ggi0 = view(15872, [128, 256])
    thb = view(16128, [128, 8])
    hib = view(16136, [128, 8])
    cpe = view(16144, [128, 8])
    offs = view(16152, [128, 8])
    tots = view(16160, [128, 8])
    slbb = view(16168, [128, 8])
    thsb = view(16176, [128, 8])
    rts_sb = view(16192, [128, 4, 128])
    posD = view(16704, [128, 1])

    def b8(a):
        return bass.AP(a.tensor, a.offset, [a.ap[0], a.ap[1], [0, 256]])

    def be(a):
        return bass.AP(a.tensor, a.offset, [a.ap[0], [0, 8], a.ap[1]])

    def swp(a3):
        return bass.AP(a3.tensor, a3.offset, [a3.ap[0], [1, 256], [256, 8]])

    def bg(a, n):
        return bass.AP(a.tensor, a.offset, [a.ap[0], [0, n], a.ap[1]])

    def vstep(fn, waits=(), pfx="r"):
        t_ = tag(pfx)
        prog.on("v", fn, waits=waits, tag=t_)
        return t_

    iota8b = bass.AP(iota8.tensor, iota8.offset, [iota8.ap[0], [0, 128], [1, 8]])
    t_ = vstep(lambda v: v.tensor_tensor(
        z_sb[:], z_sb[:], bg(brrb, 128), ALU.add), ["alite"], "rz")
    t_ = vstep(lambda v: v.tensor_reduce(zred[:], z_sb[:], AX.X, ALU.max), [t_], "rmax")
    t_ = vstep(lambda v: v.tensor_tensor(
        ebuf[:], z_sb[:], bass.AP(zred.tensor, zred.offset,
                                  [zred.ap[0], [1, 128], [0, 8]]), ALU.subtract), [t_], "rsub")
    t_ex = tag("rexp")
    prog.on("s", lambda s: s.activation(ebuf[:], ebuf[:], AF.Exp), waits=[t_], tag=t_ex)
    t_ = vstep(lambda v: v.tensor_reduce(zred2[:], ebuf[:], AX.X, ALU.add), [t_ex], "rsum")
    t_ = vstep(lambda v: v.reciprocal(zred2[:], zred2[:]), [t_], "rrec")
    tprob = vstep(lambda v: v.tensor_tensor(
        ebuf[:], ebuf[:], bass.AP(zred2.tensor, zred2.offset,
                                  [zred2.ap[0], [1, 128], [0, 8]]), ALU.mult), [t_], "rdiv")
    t_ = vstep(lambda v: v.tensor_reduce(g1t[:], ebuf[:], AX.X, ALU.max), [tprob], "tg1")
    teq = vstep(lambda v: v.tensor_tensor(
        ebuf2[:], ebuf[:], bass.AP(g1t.tensor, g1t.offset,
                                   [g1t.ap[0], [1, 128], [0, 8]]), ALU.is_equal), [t_], "teq")
    t_ = vstep(lambda v: v.tensor_scalar(ebuf2[:], ebuf2[:], -BIG, BIG, ALU.mult, ALU.add),
               [teq], "tc1")
    t_ = vstep(lambda v: v.tensor_tensor(ebuf2[:], ebuf2[:], iota8b, ALU.add), [t_], "tc2")
    tid1 = vstep(lambda v: v.tensor_reduce(id1t[:], ebuf2[:], AX.X, ALU.min), [t_], "tid1")
    tm1 = vstep(lambda v: v.tensor_tensor(
        ebuf2[:], iota8b, bass.AP(id1t.tensor, id1t.offset,
                                  [id1t.ap[0], [1, 128], [0, 8]]), ALU.is_equal), [tid1], "tm1")
    tp2 = vstep(lambda v: v.scalar_tensor_tensor(
        ebuf2[:], ebuf2[:], -BIG, ebuf[:], ALU.mult, ALU.add), [tm1], "tp2")
    t_ = vstep(lambda v: v.tensor_reduce(g2t[:], ebuf2[:], AX.X, ALU.max), [tp2], "tg2")
    teq2 = vstep(lambda v: v.tensor_tensor(
        ebuf2[:], ebuf2[:], bass.AP(g2t.tensor, g2t.offset,
                                    [g2t.ap[0], [1, 128], [0, 8]]), ALU.is_equal), [t_], "teq2")
    t_ = vstep(lambda v: v.tensor_scalar(ebuf2[:], ebuf2[:], -BIG, BIG, ALU.mult, ALU.add),
               [teq2], "tc3")
    t_ = vstep(lambda v: v.tensor_tensor(ebuf2[:], ebuf2[:], iota8b, ALU.add), [t_], "tc4")
    tid2 = vstep(lambda v: v.tensor_reduce(id2t[:], ebuf2[:], AX.X, ALU.min), [t_], "tid2")
    itev = []
    for ix, (src, ch) in enumerate(((g1t, 0), (g2t, 1), (id1t, 0), (id2t, 1))):
        mmt = mmstep(0, lambda t, src=src: t.transpose(pb[0][:, 0:128], src[:], ident),
                     waits=[tid1, tid2, tprob], pfx=f"itr{ix}")
        dst = items_g if ix < 2 else items_e
        ev = evstep(0, lambda v, dst=dst, ch=ch: v.tensor_copy(
            bass.AP(dst.tensor, dst.offset + ch, [dst.ap[0], [2, 128]]),
            pb[0][:, 0:128]), "v", [mmt], f"itre{ix}")
        itev.append(ev)
    teqs = vstep(lambda v: v.tensor_tensor(
        eqs[:], be(items_e), bass.AP(iota8.tensor, iota8.offset,
                                     [iota8.ap[0], [1, 8], [0, 256]]), ALU.is_equal),
        itev, "eqs")
    t_a = vstep(lambda v: v.tensor_tensor(gme[:], eqs[:], be(items_g), ALU.mult),
                [teqs], "gmea")
    t_b = vstep(lambda v: v.tensor_scalar(work[:], eqs[:], BIG, -BIG, ALU.mult, ALU.add),
                [teqs], "gmeb")
    tgme = vstep(lambda v: v.tensor_tensor(gme[:], gme[:], work[:], ALU.add),
                 [t_a, t_b], "gme")
    tbi = vstep(lambda v: (v.memset(lo_row[0:1, :], 0.0), v.memset(hi_row[0:1, :], 1.0),
                           v.memset(th_row[0:1, :], 0.5))[-1], ["alite"], "binit")
    prev = [tgme, tbi]
    for bi_ in range(45):
        bpfx = f"bis{bi_}_"
        bm = mmstep(6, lambda t: t.matmul(pb[6][:, 0:8], ones_row, th_row[0:1, :],
                                          start=True, stop=True), waits=prev, pfx=bpfx + "b")
        be_ = evstep(6, lambda v: v.tensor_copy(thb[:], pb[6][:, 0:8]), "v", [bm], bpfx + "be")
        ti = vstep(lambda v: v.tensor_tensor(work[:], gme[:], b8(thb), ALU.is_gt),
                   [be_], bpfx + "i")
        tr_ = vstep(lambda v: v.tensor_reduce(cpe[:], work[:], AX.X, ALU.add), [ti], bpfx + "r")
        cm = mmstep(7, lambda t: t.matmul(pb[7][0:1, 0:8], ones_col, cpe[:],
                                          start=True, stop=True), waits=[tr_], pfx=bpfx + "c")
        ce = evstep(7, lambda v: v.tensor_copy(cnt_row[0:1, :], pb[7][0:1, 0:8]),
                    "v", [cm], bpfx + "ce")

        def upd(v):
            v.tensor_scalar(cnt_row[0:1, :], cnt_row[0:1, :], float(CAP), None, ALU.is_ge)
            v.tensor_tensor(thb[0:1, 0:8], th_row[0:1, :], lo_row[0:1, :], ALU.subtract)
            v.tensor_tensor(thb[0:1, 0:8], thb[0:1, 0:8], cnt_row[0:1, :], ALU.mult)
            v.tensor_tensor(lo_row[0:1, :], lo_row[0:1, :], thb[0:1, 0:8], ALU.add)
            v.tensor_scalar(cnt_row[0:1, :], cnt_row[0:1, :], -1.0, 1.0, ALU.mult, ALU.add)
            v.tensor_tensor(thb[0:1, 0:8], th_row[0:1, :], hi_row[0:1, :], ALU.subtract)
            v.tensor_tensor(thb[0:1, 0:8], thb[0:1, 0:8], cnt_row[0:1, :], ALU.mult)
            v.tensor_tensor(hi_row[0:1, :], hi_row[0:1, :], thb[0:1, 0:8], ALU.add)
            v.tensor_tensor(th_row[0:1, :], lo_row[0:1, :], hi_row[0:1, :], ALU.add)
            return v.tensor_scalar_mul(th_row[0:1, :], th_row[0:1, :], 0.5)
        prev = [vstep(upd, [ce], bpfx + "u")]
    bm = mmstep(6, lambda t: t.matmul(pb[6][:, 0:8], ones_row, hi_row[0:1, :],
                                      start=True, stop=True), waits=prev, pfx="exb")
    bhi = evstep(6, lambda v: v.tensor_copy(hib[:], pb[6][:, 0:8]), "v", [bm], "exbe")
    t_ = vstep(lambda v: v.tensor_tensor(work[:], gme[:], b8(hib), ALU.is_le), [bhi], "exle")
    t_a = vstep(lambda v: v.tensor_tensor(work2[:], gme[:], work[:], ALU.mult), [t_], "exm")
    t_b = vstep(lambda v: v.tensor_scalar(work[:], work[:], BIG, -BIG, ALU.mult, ALU.add),
                [t_a], "exo")
    t_ = vstep(lambda v: v.tensor_tensor(work2[:], work2[:], work[:], ALU.add), [t_b], "exc")
    t_ = vstep(lambda v: v.tensor_reduce(cpe[:], work2[:], AX.X, ALU.max), [t_], "exr")
    xm = mmstep(7, lambda t: t.transpose(pb[7][0:8, 0:128], cpe[:], ident), waits=[t_], pfx="ext")
    xe = evstep(7, lambda v: v.tensor_copy(zred[0:8, 0:128], pb[7][0:8, 0:128]), "v", [xm], "exte")
    t_ = vstep(lambda v: v.tensor_reduce(zred2[0:8, 0:1], zred[0:8, 0:128], AX.X, ALU.max),
               [xe], "exr2")
    xm2 = mmstep(6, lambda t: t.transpose(pb[6][0:1, 0:8], zred2[0:8, 0:1], ident[0:8, 0:8]),
                 waits=[t_], pfx="ext2")
    tth = evstep(6, lambda v: v.tensor_copy(th_row[0:1, :], pb[6][0:1, 0:8]), "v", [xm2], "exte2")
    bm = mmstep(6, lambda t: t.matmul(pb[6][:, 0:8], ones_row, th_row[0:1, :],
                                      start=True, stop=True), waits=[tth], pfx="thb")
    tthb = evstep(6, lambda v: v.tensor_copy(thsb[:], pb[6][:, 0:8]), "v", [bm], "thbe")
    t_ = vstep(lambda v: v.tensor_tensor(work[:], eqs[:], b8(thsb), ALU.mult), [tthb], "thm")
    tthi = vstep(lambda v: v.tensor_reduce(thiI[:], swp(work), AX.X, ALU.add), [t_], "thi")
    tgt = vstep(lambda v: v.tensor_tensor(gtI[:], items_g[:], thiI[:], ALU.is_gt), [tthi], "gt")
    teqt = vstep(lambda v: v.tensor_tensor(eqthI[:], items_g[:], thiI[:], ALU.is_equal),
                 [tthi], "eqt")
    t_ = vstep(lambda v: v.tensor_tensor(work[:], eqs[:], be(gtI), ALU.mult), [tgt], "slm")
    t_ = vstep(lambda v: v.tensor_reduce(cpe[:], work[:], AX.X, ALU.add), [t_], "slr")
    cm = mmstep(7, lambda t: t.matmul(pb[7][0:1, 0:8], ones_col, cpe[:], start=True, stop=True),
                waits=[t_], pfx="slc")
    ce = evstep(7, lambda v: v.tensor_copy(cnt_row[0:1, :], pb[7][0:1, 0:8]), "v", [cm], "slce")
    tslb = vstep(lambda v: v.tensor_scalar(slb_row[0:1, :], cnt_row[0:1, :], -1.0, float(CAP),
                                           ALU.mult, ALU.add), [ce], "slb")

    def prefix_pass(src_tag, indI, dst, pfxn):
        sc = []
        for e_ in range(8):
            sc.append(vstep(lambda v, e_=e_: v.tensor_tensor_scan(
                scn[:, e_, :], work[:, e_, :], zer256[:], 0.0, ALU.add, ALU.add),
                [src_tag], pfxn + f"s{e_}"))
        t1 = vstep(lambda v: v.tensor_reduce(tots[:], indI[:], AX.X, ALU.add),
                   [src_tag], pfxn + "tot")
        om_ = mmstep(6, lambda t: t.matmul(pb[6][:, 0:8], trils[:], tots[:],
                                           start=True, stop=True), waits=[t1], pfx=pfxn + "off")
        oe_ = evstep(6, lambda s: s.copy(offs[:], pb[6][:, 0:8]), "s", [om_], pfxn + "offe")
        t2 = vstep(lambda v: v.tensor_tensor(scn[:], scn[:], b8(offs), ALU.add),
                   [oe_], pfxn + "full")
        t3 = vstep(lambda v: v.tensor_tensor(work2[:], scn[:], eqs[:], ALU.mult), [t2], pfxn + "sel")
        return vstep(lambda v: v.tensor_reduce(dst[:], swp(work2), AX.X, ALU.add),
                     [t3], pfxn + "red")

    t_ = vstep(lambda v: v.tensor_tensor(work[:], eqs[:], be(eqthI), ALU.mult), [teqt], "tiem")
    trk = prefix_pass(t_, work, r1, "trk")
    bm = mmstep(6, lambda t: t.matmul(pb[6][:, 0:8], ones_row, slb_row[0:1, :],
                                      start=True, stop=True), waits=[tslb], pfx="slbb")
    tslbb = evstep(6, lambda v: v.tensor_copy(slbb[:], pb[6][:, 0:8]), "v", [bm], "slbbe")
    t_ = vstep(lambda v: v.tensor_tensor(work[:], eqs[:], b8(slbb), ALU.mult), [tslbb], "slim")
    tsli = vstep(lambda v: v.tensor_reduce(r2[:], swp(work), AX.X, ALU.add), [t_], "sli")
    t_ = vstep(lambda v: v.tensor_tensor(r3[:], r1[:], r2[:], ALU.is_le), [trk, tsli], "tkle")
    t_ = vstep(lambda v: v.tensor_tensor(r3[:], r3[:], eqthI[:], ALU.mult), [t_], "tk2")
    tkept = vstep(lambda v: v.tensor_tensor(keptI[:], gtI[:], r3[:], ALU.max), [t_], "kept")
    t_ = vstep(lambda v: v.tensor_tensor(work[:], eqs[:], be(keptI), ALU.mult), [tkept], "klm")
    tslot = prefix_pass(t_, work, slotI, "slt")
    tslot = vstep(lambda v: v.tensor_tensor(slotI[:], slotI[:], keptI[:], ALU.subtract),
                  [tslot], "slx")
    def s512f(v):
        v.tensor_scalar(r1[:], slotI[:], 512.0, None, ALU.is_ge)
        for thr in (1024.0, 1536.0, 2048.0):
            v.tensor_scalar(r2[:], slotI[:], thr, None, ALU.is_ge)
            v.tensor_tensor(r1[:], r1[:], r2[:], ALU.add)
        return v.tensor_scalar_mul(r1[:], r1[:], 3584.0)
    t_ = vstep(s512f, [tslot], "fm1")
    t_ = vstep(lambda v: v.tensor_tensor(r1[:], r1[:], slotI[:], ALU.add), [t_], "fm4")
    t_ = vstep(lambda v: v.scalar_tensor_tensor(r1[:], items_e[:], 512.0, r1[:],
                                                ALU.mult, ALU.add), [t_], "fm5")
    tgidx = vstep(lambda v: v.tensor_tensor(r1[:], r1[:], keptI[:], ALU.mult), [t_], "fm6")
    tgg = vstep(lambda v: v.tensor_tensor(ggi0[:], items_g[:], keptI[:], ALU.mult),
                [tkept], "fgg")
    rts_ev = []
    for ix, (src, ch) in enumerate(((r1, 0), (r1, 1), (ggi0, 0), (ggi0, 1))):
        sv = bass.AP(src.tensor, src.offset + ch, [src.ap[0], [2, 128]])
        mmt = mmstep(1, lambda t, sv=sv, ix=ix: t.transpose(
            pb[1][:, ix * 128:(ix + 1) * 128], sv, ident), waits=[tgidx, tgg], pfx=f"rts{ix}")
        rts_ev.append(mmt)
    rte = evstep(1, lambda v: v.tensor_copy(
        bass.AP(rts_sb.tensor, rts_sb.offset, [rts_sb.ap[0], [1, 4], [4, 128]]),
        pb[1][:].rearrange("p (a b) -> p a b", b=128)), "v", rts_ev, "rtse")
    rtw = tag("rtsw")
    prog.on("q", lambda q: [q.dma_start(
        out=A("rts_d").rearrange("(g p) f -> p g f", p=128),
        in_=bass.AP(rts_sb.tensor, rts_sb.offset, [rts_sb.ap[0], [4, 128], [1, 4]]))],
        waits=[rte], n_dma=1, tag=rtw)
    t_ = vstep(lambda v: v.tensor_scalar(r2[:], items_e[:], myexpb[:, 0:1], None, ALU.is_equal),
               itev + ["p0"], "dse")
    tsel = vstep(lambda v: v.tensor_tensor(r2[:], r2[:], keptI[:], ALU.mult), [t_, tkept], "dsel")
    t_a = vstep(lambda v: v.tensor_tensor(r3[:], r2[:], slotI[:], ALU.mult), [tsel, tslot], "dp1")
    t_b = vstep(lambda v: v.tensor_scalar(posf[:], r2[:], -1.0, 1.0, ALU.mult, ALU.add),
                [tsel], "dp2")
    t_c = vstep(lambda v: v.tensor_tensor(posf[:], posf[:], dumpt[:], ALU.mult), [t_b], "dp3")
    tpos = vstep(lambda v: v.tensor_tensor(posf[:], posf[:], r3[:], ALU.add), [t_a, t_c], "dp4")
    tposu = vstep(lambda v: v.tensor_copy(posu[:], posf[:]), [tpos], "dpu")
    tgtoku = vstep(lambda v: v.tensor_copy(gtoku[:], gtok[:]), ["p0"], "dgt")
    t_ = vstep(lambda v: v.memset(zred[:, 0:20], float(N)), ["alite"], "dpf1")
    t_ = vstep(lambda v: v.tensor_copy(disp_pre[:], zred[:, 0:20]), [t_], "dpf2")
    dpf = tag("dpf3")
    prog.on("q", lambda q: [q.dma_start(
        out=A("disp_d")[0:CAPP, :].rearrange("(g p) one -> p (g one)", p=128),
        in_=disp_pre[:], allow_slow_non_contiguous=True)],
        waits=[t_], n_dma=1, tag=dpf)
    prog.barrier("routing")
    prog.default_waits = ["routing"]
    if stop_if("routing"):
        return

    cstat = tag("cstat")
    prog.on("g", lambda g: g.collective_compute(
        "AllGather", ALU.bypass, replica_groups=RG,
        ins=[A("stats_in")], outs=[A("stats_all")]), waits=["statw"], tag=cstat)
    gw1sb = view(8240, [128, 9, 1024])
    h1B = view(17456, [128, 8, 512])
    sstat = view(21552, [128, 64, 6])
    stats_fm = view(21936, [128, 512])
    ltile = view(22448, [128, 4, 24])
    lred = view(22544, [128, 16])
    lred2 = view(22560, [128, 16])
    sft = view(22576, [128, 24])
    sg_tags = []
    for i in range(64):
        t_ = tag(f"sg{i}")
        prog.on("g", lambda g, i=i: [g.indirect_dma_start(
            out=sstat[:, i, :], out_offset=None,
            in_=A("stats_all").rearrange("c t f -> (c t) f"),
            in_offset=IndirectOffsetOnAxis(statrows_u[:, i:i + 1], 0))],
            waits=[cstat], n_dma=1, tag=t_)
        sg_tags.append(t_)
    sc_tags = []
    for j in range(256 if not SKIP_SC else 0):
        st_ = tag(f"dsc{j}")
        prog.on("g", lambda g, j=j: [g.indirect_dma_start(
            out=A("disp_d"), out_offset=IndirectOffsetOnAxis(posu[:, j:j + 1], 0),
            in_=gtoku[:, j:j + 1], in_offset=None)],
            waits=[tposu, tgtoku, dpf], n_dma=1, tag=st_)
        sc_tags.append(st_)
    sc_tags = sc_tags[-Prog.NGD:]
    pq_tags = []
    for qt in range(4):
        pfx = f"gb{qt}_"
        rows = A("xmy")[qt * TT:(qt + 1) * TT, :]
        xn_tag = emit_ln(rows, qt, pfx)
        fm_tags = emit_tr_fm(qt, pfx, xn_tag)
        fm_done = fm_tags[-2:]
        sf_tags = []
        for g4 in range(4):
            gg = qt * 4 + g4
            sv = bass.AP(sstat.tensor, sstat.offset + gg * 6,
                         [sstat.ap[0], [96, 4], [1, 6]])
            tc_ = vstep(lambda v, sv=sv: v.tensor_copy(
                sft[:].rearrange("p (h w) -> p h w", w=6), sv),
                sg_tags + sf_tags[-1:], pfx + f"sfc{g4}")
            mmt = mmstep(5, lambda t: t.transpose(pb[5][0:24, 0:128], sft[:], ident),
                         waits=[tc_], pfx=pfx + f"sf{g4}")
            sf_tags.append(evstep(5, lambda v, g4=g4: v.tensor_copy(
                stats_fm[0:24, g4 * 128:(g4 + 1) * 128], pb[5][0:24, 0:128]),
                "v", [mmt], pfx + f"sfe{g4}"))
        h1_done = []
        for Mh in range(2):
            l1 = tag(pfx + f"gw{Mh}a")
            prog.on("q", lambda q, Mh=Mh: [q.dma_start(
                out=gw1sb[:, 0:8, :],
                in_=A("gw1t")[0:1024, Mh * 1024:(Mh + 1) * 1024]
                .rearrange("(kc p) f -> p kc f", p=128))], waits=h1_done[-1:], n_dma=1, tag=l1)
            l2 = tag(pfx + f"gw{Mh}b")
            prog.on("q", lambda q, Mh=Mh: [q.dma_start(
                out=gw1sb[0:24, 8, :],
                in_=A("gw1t")[1024:1048, Mh * 1024:(Mh + 1) * 1024])], n_dma=1, tag=l2)
            for Mc in range(8):
                bk = Mc % 4
                col = Mh * 8 + Mc

                def h1mm(t, Mc=Mc, bk=bk):
                    for kc in range(8):
                        t.matmul(pb[bk][:], gw1sb[:, kc, Mc * 128:(Mc + 1) * 128],
                                 xn_fm[:, kc, :], start=(kc == 0), stop=False)
                    return t.matmul(pb[bk][:], gw1sb[0:24, 8, Mc * 128:(Mc + 1) * 128],
                                    stats_fm[0:24, :], start=False, stop=True)
                hm = mmstep(bk, h1mm, waits=[l1, l2] + fm_done + sf_tags, pfx=pfx + f"h1m{col}")
                he = evstep(bk, lambda s, Mc=Mc, bk=bk, col=col: s.activation(
                    h1B[:, Mc, :], pb[bk][:], AF.Gelu, bias=gb1_sb[:, col:col + 1]),
                    "s", [hm], pfx + f"h1e{col}")
                lt_ = mmstep(6, lambda t, Mc=Mc, Mh=Mh, col=col: [t.matmul(
                    pb[6][:, g4 * 24:g4 * 24 + 24], h1B[:, Mc, g4 * 128:(g4 + 1) * 128],
                    gw2t_sb[:, col, :], start=(col == 0), stop=(col == 15),
                    skip_group_check=True)
                    for g4 in range(4)][-1], waits=[he], pfx=pfx + f"lg{col}")
                h1_done.append(lt_)
        lev = evstep(6, lambda v: v.tensor_copy(
            ltile[:].rearrange("p a b -> p (a b)"), pb[6][:, 0:96]), "v",
            h1_done[-1:], pfx + "lev")
        def epi(v, qt=qt):
            v.tensor_tensor(ltile[:], ltile[:], bg(b2b, 4), ALU.add)
            for g4 in range(4):
                gg = qt * 4 + g4
                sv = bass.AP(sstat.tensor, sstat.offset + gg * 6,
                             [sstat.ap[0], [96, 4], [1, 6]])
                v.tensor_tensor(trash[:, 0:24].rearrange("p (h w) -> p h w", w=6), sv,
                                alphab[:].rearrange("p (h w) -> p h w", w=6), ALU.mult)
                v.tensor_tensor(ltile[:, g4, :], ltile[:, g4, :], trash[:, 0:24], ALU.add)
            return v.tensor_tensor(ltile[:], ltile[:], bg(itempb, 4), ALU.mult)
        te = vstep(epi, [lev] + sg_tags, pfx + "epi")
        lt4 = ltile[:].rearrange("p a (h w) -> p (a h) w", w=6)
        t_ = vstep(lambda v: v.tensor_reduce(lred[:], lt4, AX.X, ALU.max), [te], pfx + "smx")
        t_ = vstep(lambda v: v.tensor_tensor(
            lt4, lt4, bass.AP(lred.tensor, lred.offset, [lred.ap[0], [1, 16], [0, 6]]),
            ALU.subtract), [t_], pfx + "sms")
        t_x = tag(pfx + "sme")
        prog.on("s", lambda s: s.activation(ltile[:], ltile[:], AF.Exp), waits=[t_], tag=t_x)
        t_ = vstep(lambda v: v.tensor_reduce(lred2[:], lt4, AX.X, ALU.add), [t_x], pfx + "smr")
        t_ = vstep(lambda v: v.reciprocal(lred2[:], lred2[:]), [t_], pfx + "smc")
        t_ = vstep(lambda v: v.tensor_tensor(
            lt4, lt4, bass.AP(lred2.tensor, lred2.offset, [lred2.ap[0], [1, 16], [0, 6]]),
            ALU.mult), [t_], pfx + "smd")
        t_ = vstep(lambda v: v.tensor_scalar_mul(ltile[:], ltile[:], 0.95), [t_], pfx + "smf")
        t_ = vstep(lambda v: v.tensor_scalar_add(
            bass.AP(ltile.tensor, ltile.offset + 5, [ltile.ap[0], [24, 4], [6, 4]])
            .rearrange("p a b -> p (a b)"), bass.AP(
                ltile.tensor, ltile.offset + 5, [ltile.ap[0], [24, 4], [6, 4]])
            .rearrange("p a b -> p (a b)"), 0.05), [t_], pfx + "smg")
        pw = tag(pfx + "pw")
        prog.on("q", lambda q, qt=qt: [q.dma_start(
            out=A("probs_in")[qt * TT:(qt + 1) * TT, :].rearrange("(s p) f -> p s f", p=128),
            in_=ltile[:])], waits=[t_], n_dma=1, tag=pw)
        pq_tags.append(pw)
        prog.barrier(f"qb{qt}")
        prog.default_waits = [f"qb{qt}"]
    cprob = tag("cprob")
    prog.on("g", lambda g: g.collective_compute(
        "AllGather", ALU.bypass, replica_groups=RG,
        ins=[A("probs_in")], outs=[A("probs_all")]), waits=pq_tags, tag=cprob)
    prog.barrier("phB")
    prog.default_waits = ["phB"]
    if stop_if("phB"):
        return

    sprob = view(0, [128, 64, 6])
    brl = [view(512 + b_ * 2048, [128, 8, 256]) for b_ in range(6)]
    o_acc = view(12800, [128, 8, 256])
    cssq = view(14848, [128, 8])
    pg_tags = []
    for i in range(64):
        t_ = tag(f"pg{i}")
        prog.on("g", lambda g, i=i: [g.indirect_dma_start(
            out=sprob[:, i, :], out_offset=None,
            in_=A("probs_all").rearrange("c t (h w) -> (c t h) w", w=6),
            in_offset=IndirectOffsetOnAxis(probrows_u[:, i:i + 1], 0))],
            waits=[cprob], n_dma=1, tag=t_)
        pg_tags.append(t_)
    for cc in range(8):
        pfx = f"cc{cc}_"
        bl_tags = []
        for b_ in range(6):
            t_ = tag(pfx + f"bl{b_}")
            prog.on("q", lambda q, b_=b_, cc=cc: [q.dma_start(
                out=brl[b_][:],
                in_=A("br_d")[b_, cc * 1024:(cc + 1) * 1024, :]
                .rearrange("(s p) d -> p s d", p=128))], n_dma=1, tag=t_)
            bl_tags.append(t_)
        rms_tags = []
        for g8 in range(8):
            gg = cc * 8 + g8

            def comb(v, g8=g8, gg=gg):
                r = v.tensor_scalar_mul(o_acc[:, g8, :], brl[0][:, g8, :],
                                        sprob[:, gg, 0:1])
                for b_ in range(1, 6):
                    r = v.scalar_tensor_tensor(o_acc[:, g8, :], brl[b_][:, g8, :],
                                               sprob[:, gg, b_:b_ + 1], o_acc[:, g8, :],
                                               ALU.mult, ALU.add)
                return r
            t_ = vstep(comb, bl_tags + pg_tags, pfx + f"cb{g8}")
            t2 = tag(pfx + f"rq{g8}")
            prog.on("s", lambda s, g8=g8: s.activation(
                trash[:, 0:256], o_acc[:, g8, :], AF.Square, scale=1.0 / 16.0,
                accum_out=cssq[:, g8:g8 + 1]), waits=[t_], tag=t2)
            rms_tags.append(t2)
        t_ = tag(pfx + "rsd")
        prog.on("s", lambda s: s.activation(cssq[:], cssq[:], AF.Sqrt, bias=cbias[:, 0:1]),
                waits=rms_tags, tag=t_)
        t_ = vstep(lambda v: v.reciprocal(cssq[:], cssq[:]), [t_], pfx + "rrc")
        onorm = []
        for g8 in range(8):
            onorm.append(vstep(lambda v, g8=g8: v.scalar_tensor_tensor(
                o_acc[:, g8, :], o_acc[:, g8, :], cssq[:, g8:g8 + 1], onw_b[:],
                ALU.mult, ALU.mult), [t_], pfx + f"on{g8}"))
        ow = tag(pfx + "ow")
        prog.on("q", lambda q, cc=cc: [q.dma_start(
            out=A("o_in")[cc * 1024:(cc + 1) * 1024, :].rearrange("(s p) d -> p s d", p=128),
            in_=o_acc[:])], waits=onorm, n_dma=1, tag=ow)
        if cc % 2 == 1:
            ct = f"co{cc // 2}"
            prog.on("g", lambda g, cc=cc: g.collective_compute(
                "AllGather", ALU.bypass, replica_groups=RG,
                ins=[A("o_in")[(cc - 1) * 1024:(cc + 1) * 1024, :]],
                outs=[A("o_all")[cc // 2]]), waits=[ow], tag=ct)
        prog.barrier(pfx + "end")
        prog.default_waits = [pfx + "end"]
    prog.barrier("phC")
    prog.default_waits = ["phC"]
    if stop_if("phC"):
        return

    xg_g = view(0, [128, 5, 1024])
    xgT = view(5120, [128, 8, 640])
    w1q = view(10240, [128, 8, 1024])
    w2q = view(18432, [128, 8, 1024])
    h1q = view(26624, [128, 8, 512])
    xgt_t = view(0, [128, 8, 512])
    outbuf = view(4096, [128, 8, 512])
    fout = view(30720, [128, 4, 512])
    b1sb = view(32768, [128, 32])
    b2sb = view(32800, [128, 8])
    t_ = tag("dld")
    prog.on("q", lambda q: [q.dma_start(
        out=sgidx[:], in_=A("disp_d")[0:CAPP, :].rearrange("(g p) one -> p (g one)", p=128),
        allow_slow_non_contiguous=True)],
        waits=(sc_tags or [dpf]), n_dma=1, tag=t_)
    dld = t_
    prog.on("q", lambda q: [q.dma_start(
        out=b1sb[:], in_=A("b1e").rearrange("(c p) one -> p (c one)", p=128),
        allow_slow_non_contiguous=True)], n_dma=1, tag="db1")
    prog.on("q", lambda q: [q.dma_start(
        out=b2sb[:], in_=A("b2e").rearrange("(c p) one -> p (c one)", p=128),
        allow_slow_non_contiguous=True)], n_dma=1, tag="db2")
    stg_tags = []
    for r_ in range(4):
        pfx = f"dg{r_}_"
        ga = []
        for i5 in range(5):
            t_ = tag(pfx + f"g{i5}")
            prog.on("g", lambda g, r_=r_, i5=i5: [g.indirect_dma_start(
                out=xg_g[:, i5, :], out_offset=None,
                in_=A("xn_tm_d"),
                in_offset=IndirectOffsetOnAxis(sgidx[:, r_ * 5 + i5:r_ * 5 + i5 + 1], 0))],
                waits=[dld, "zdum"] + stg_tags[-1:], n_dma=1, tag=t_)
            ga.append(t_)
        ev_all = []
        for kc in range(8):
            bk = 6 + kc % 2
            mm1 = mmstep(bk, lambda t, kc=kc, bk=bk: [t.transpose(
                pb[bk][:, i5 * 128:(i5 + 1) * 128],
                xg_g[:, i5, kc * 128:(kc + 1) * 128], ident) for i5 in range(4)][-1],
                waits=ga, pfx=pfx + f"t{kc}a")
            e1 = evstep(bk, lambda v, kc=kc, bk=bk: v.tensor_copy(
                xgT[:, kc, 0:512], pb[bk][:]), "v", [mm1], pfx + f"te{kc}a")
            mm2 = mmstep(bk, lambda t, kc=kc, bk=bk: t.transpose(
                pb[bk][:, 0:128], xg_g[:, 4, kc * 128:(kc + 1) * 128], ident),
                waits=ga, pfx=pfx + f"t{kc}b")
            e2 = evstep(bk, lambda s, kc=kc, bk=bk: s.copy(
                xgT[:, kc, 512:640], pb[bk][:, 0:128]), "s", [mm2], pfx + f"te{kc}b")
            ev_all += [e1, e2]
        t_ = tag(pfx + "w")
        prog.on("q", lambda q, r_=r_: [q.dma_start(
            out=A("xg_fm_d").rearrange("(kc p) s -> p kc s", p=128)[:, :, r_ * 640:(r_ + 1) * 640],
            in_=xgT[:])], waits=ev_all, n_dma=1, tag=t_)
        stg_tags.append(t_)
    prog.barrier("dstage")
    prog.default_waits = ["dstage"]
    if stop_if("dstage"):
        return
    out_dma = [None] * 5
    ffn_done = []
    for qh in range(4):
        pfx = f"dq{qh}_"
        lw1 = tag(pfx + "w1")
        prog.on("q", lambda q, qh=qh: [q.dma_start(
            out=w1q[:], in_=A("w1e").rearrange("(kc p) f -> p kc f", p=128)
            [:, :, qh * 1024:(qh + 1) * 1024])], waits=ffn_done[-1:], n_dma=1, tag=lw1)
        lw2 = tag(pfx + "w2")
        prog.on("q", lambda q, qh=qh: [q.dma_start(
            out=w2q[:], in_=A("w2e")[qh * 1024:(qh + 1) * 1024, :]
            .rearrange("(kh p) f -> p kh f", p=128))], n_dma=1, tag=lw2)
        for tk in range(5):
            tpfx = pfx + f"t{tk}_"
            lx = tag(tpfx + "x")
            prog.on("q", lambda q, tk=tk: [q.dma_start(
                out=xgt_t[:], in_=A("xg_fm_d").rearrange("(kc p) s -> p kc s", p=128)
                [:, :, tk * 512:(tk + 1) * 512])], waits=ffn_done[-1:], n_dma=1, tag=lx)
            h1_tags = []
            for Mh in range(8):
                bk = Mh % 2
                hm = mmstep(bk, lambda t, Mh=Mh, bk=bk: [t.matmul(
                    pb[bk][:], w1q[:, kc, Mh * 128:(Mh + 1) * 128], xgt_t[:, kc, :],
                    start=(kc == 0), stop=(kc == 7)) for kc in range(8)][-1],
                    waits=[lw1, lx], pfx=tpfx + f"h{Mh}")
                he = evstep(bk, lambda s, Mh=Mh, bk=bk, qh=qh: s.activation(
                    h1q[:, Mh, :], pb[bk][:], AF.Silu,
                    bias=b1sb[:, qh * 8 + Mh:qh * 8 + Mh + 1]), "s", [hm, "db1"],
                    tpfx + f"he{Mh}")
                h1_tags.append(he)
            ou_tags = []
            for Md in range(8):
                bk = 4 + Md % 2
                om_ = mmstep(bk, lambda t, Md=Md, bk=bk: [t.matmul(
                    pb[bk][:], w2q[:, kh, Md * 128:(Md + 1) * 128], h1q[:, kh, :],
                    start=(kh == 0), stop=(kh == 7)) for kh in range(8)][-1],
                    waits=[lw2] + h1_tags, pfx=tpfx + f"o{Md}")
                if qh == 0:
                    oe_ = evstep(bk, lambda v, Md=Md, bk=bk: v.tensor_scalar_add(
                        outbuf[:, Md, :], pb[bk][:], b2sb[:, Md:Md + 1]),
                        "v", [om_, "db2"], tpfx + f"oe{Md}")
                else:
                    oe_ = evstep(bk, lambda v, Md=Md, bk=bk: v.tensor_copy(
                        outbuf[:, Md, :], pb[bk][:]), "v", [om_], tpfx + f"oe{Md}")
                ou_tags.append(oe_)
            dw = tag(tpfx + "dw")
            if qh == 0:
                prog.on("q", lambda q, tk=tk: [q.dma_start(
                    out=A("outfm_d").rearrange("(md p) s -> p md s", p=128)
                    [:, :, tk * 512:(tk + 1) * 512], in_=outbuf[:])],
                    waits=ou_tags, n_dma=1, tag=dw)
            else:
                prog.on("g", lambda g, tk=tk: [g.dma_start(
                    out=A("outfm_d").rearrange("(md p) s -> p md s", p=128)
                    [:, :, tk * 512:(tk + 1) * 512], in_=outbuf[:],
                    accum_op=ALU.add)],
                    waits=ou_tags + [out_dma[tk]], n_dma=1, tag=dw)
            out_dma[tk] = dw
            ffn_done.append(ou_tags[-1])
        prog.barrier(pfx + "end")
        prog.default_waits = [pfx + "end"]
    fout2 = view(4096, [128, 4, 1024])
    for tk in range(5):
        pfx = f"df{tk}_"
        lx = tag(pfx + "l")
        prog.on("q", lambda q, tk=tk: [q.dma_start(
            out=xgt_t[:], in_=A("outfm_d").rearrange("(md p) s -> p md s", p=128)
            [:, :, tk * 512:(tk + 1) * 512])], waits=[out_dma[tk]], n_dma=1, tag=lx)
        ev_all = []
        for si in range(4):
            for half in range(2):
                bk = 6 + half
                mm1 = mmstep(bk, lambda t, si=si, half=half, bk=bk: [t.transpose(
                    pb[bk][:, kc4 * 128:(kc4 + 1) * 128],
                    xgt_t[:, half * 4 + kc4, si * 128:(si + 1) * 128], ident)
                    for kc4 in range(4)][-1], waits=[lx], pfx=pfx + f"t{si}{half}")
                if half == 0:
                    e1 = evstep(bk, lambda v, si=si, bk=bk: v.tensor_copy(
                        fout2[:, si, 0:512], pb[bk][:]), "v", [mm1], pfx + f"te{si}{half}")
                else:
                    e1 = evstep(bk, lambda s_, si=si, bk=bk: s_.copy(
                        fout2[:, si, 512:1024], pb[bk][:]), "s", [mm1], pfx + f"te{si}{half}")
                ev_all.append(e1)
        fw = tag(pfx + "w")
        prog.on("q", lambda q, tk=tk: [q.dma_start(
            out=A("ffn_in")[tk * 512:(tk + 1) * 512, :].rearrange("(s p) d -> p s d", p=128),
            in_=fout2[:])], waits=ev_all, n_dma=1, tag=fw)
        cf = f"cf{tk}"
        prog.on("g", lambda g, tk=tk: g.collective_compute(
            "AllGather", ALU.bypass, replica_groups=RG,
            ins=[A("ffn_in")[tk * 512:(tk + 1) * 512, :]], outs=[A("ffn_all")[tk]]),
            waits=[fw], tag=cf)
    prog.barrier("phD")
    prog.default_waits = ["phD"]
    if stop_if("phD"):
        return

    o_fmE = view(0, [128, 8, 2048])
    og_h = view(16384, [128, 16, 256])
    wotsb = view(20480, [128, 8, 1024])
    rtsb = view(28672, [128, 16, 4])
    gg0 = view(28736, [128, 16])
    gg1 = view(28752, [128, 16])
    rows0 = view(28800, [128, 1024])
    rows1 = view(29824, [128, 1024])
    gbuf = view(30848, [128, 1024])
    xtile = view(31872, [128, 1024])
    prog.on("q", lambda q: [q.dma_start(
        out=wotsb[:], in_=A("wot").rearrange("(kc p) f -> p kc f", p=128))],
        n_dma=1, tag="ewot")
    rg_tags = []
    for i in range(16):
        t_ = tag(f"erts{i}")
        prog.on("g", lambda g, i=i: [g.indirect_dma_start(
            out=rtsb[:, i, :], out_offset=None,
            in_=A("rts_d"), in_offset=IndirectOffsetOnAxis(myrows_u[:, i:i + 1], 0))],
            waits=[rtw], n_dma=1, tag=t_)
        rg_tags.append(t_)

    def cvt(v):
        v.tensor_copy(gidx0_u[:], bass.AP(rtsb.tensor, rtsb.offset + 0,
                                          [rtsb.ap[0], [4, 16]]))
        v.tensor_copy(gidx1_u[:], bass.AP(rtsb.tensor, rtsb.offset + 1,
                                          [rtsb.ap[0], [4, 16]]))
        v.tensor_copy(gg0[:], bass.AP(rtsb.tensor, rtsb.offset + 2,
                                      [rtsb.ap[0], [4, 16]]))
        return v.tensor_copy(gg1[:], bass.AP(rtsb.tensor, rtsb.offset + 3,
                                             [rtsb.ap[0], [4, 16]]))
    tcvt = vstep(cvt, rg_tags, "ecvt")
    ofm_tags = []
    for h_ in range(4):
        og_tags = []
        for g16 in range(16):
            t_ = tag(f"eog{h_}_{g16}")
            prog.on("g", lambda g, h_=h_, g16=g16: [g.indirect_dma_start(
                out=og_h[:, g16, :], out_offset=None,
                in_=A("o_all").rearrange("c e t d -> (c e t) d"),
                in_offset=IndirectOffsetOnAxis(orows_u[:, h_ * 16 + g16:h_ * 16 + g16 + 1], 0))],
                waits=[f"co{i}" for i in range(4)] + ofm_tags[-1:],
                n_dma=1, tag=t_)
            og_tags.append(t_)
        for dc in range(2):
            for g4 in range(4):
                bk = 6 + (g4 % 2)
                mm1 = mmstep(bk, lambda t, dc=dc, g4=g4, bk=bk: [t.transpose(
                    pb[bk][:, gi * 128:(gi + 1) * 128],
                    og_h[:, g4 * 4 + gi, dc * 128:(dc + 1) * 128], ident)
                    for gi in range(4)][-1], waits=og_tags, pfx=f"eot{h_}{dc}{g4}")
                ev = evstep(bk, lambda v, h_=h_, dc=dc, g4=g4, bk=bk: v.tensor_copy(
                    o_fmE[:, h_ * 2 + dc, g4 * 512:(g4 + 1) * 512], pb[bk][:]),
                    "v", [mm1], f"eote{h_}{dc}{g4}")
                ofm_tags.append(ev)
    st_tags = []
    for g16 in range(16):
        pfx = f"ef{g16}_"
        lx = tag(pfx + "x")
        prog.on("q", lambda q, g16=g16: [q.dma_start(
            out=xtile[:], in_=A("xmy")[g16 * 128:(g16 + 1) * 128, :])],
            waits=st_tags[-1:], n_dma=1, tag=lx)
        gr0 = tag(pfx + "g0")
        prog.on("g", lambda g, g16=g16: [g.indirect_dma_start(
            out=rows0[:], out_offset=None,
            in_=A("ffn_all").rearrange("c e t d -> (c e t) d"),
            in_offset=IndirectOffsetOnAxis(gidx0_u[:, g16:g16 + 1], 0))],
            waits=[f"cf{i}" for i in range(5)] + [tcvt] + st_tags[-1:],
            n_dma=1, tag=gr0)
        gr1 = tag(pfx + "g1")
        prog.on("g", lambda g, g16=g16: [g.indirect_dma_start(
            out=rows1[:], out_offset=None,
            in_=A("ffn_all").rearrange("c e t d -> (c e t) d"),
            in_offset=IndirectOffsetOnAxis(gidx1_u[:, g16:g16 + 1], 0))],
            waits=[f"cf{i}" for i in range(5)] + [tcvt] + st_tags[-1:],
            n_dma=1, tag=gr1)
        ty = vstep(lambda v, g16=g16: v.tensor_scalar_mul(
            gbuf[:], rows1[:], gg1[:, g16:g16 + 1]), [gr1, tcvt], pfx + "y1")
        ty = vstep(lambda v, g16=g16: v.scalar_tensor_tensor(
            gbuf[:], rows0[:], gg0[:, g16:g16 + 1], gbuf[:], ALU.mult, ALU.add),
            [ty, gr0], pfx + "y2")
        tg = tag(pfx + "sil")
        prog.on("s", lambda s: s.activation(gbuf[:], gbuf[:], AF.Silu), waits=[ty], tag=tg)
        sm = []
        for half in range(2):
            bk = half
            mmt = mmstep(bk, lambda t, g16=g16, half=half, bk=bk: [t.matmul(
                pb[bk][:], o_fmE[:, kc, g16 * 128:(g16 + 1) * 128],
                wotsb[:, kc, half * 512:(half + 1) * 512],
                start=(kc == 0), stop=(kc == 7)) for kc in range(8)][-1],
                waits=ofm_tags[-1:] + ["ewot"], pfx=pfx + f"s{half}")
            sm.append(mmt)
        oo = []
        for half in range(2):
            t1 = vstep(lambda v, half=half, bk=half: v.tensor_tensor(
                rows1[:, half * 512:(half + 1) * 512], pb[bk][:],
                gbuf[:, half * 512:(half + 1) * 512], ALU.mult), [sm[half], tg], pfx + f"o{half}")
            t2 = vstep(lambda v, half=half: v.tensor_tensor(
                rows1[:, half * 512:(half + 1) * 512], rows1[:, half * 512:(half + 1) * 512],
                xtile[:, half * 512:(half + 1) * 512], ALU.add), [t1, lx], pfx + f"oa{half}")
            bank_last[half] = t2
            oo.append(t2)
        st = tag(pfx + "st")
        prog.on("q", lambda q, g16=g16: [q.dma_start(
            out=A("out_slice")[g16 * 128:(g16 + 1) * 128, :], in_=rows1[:])],
            waits=oo, n_dma=1, tag=st)
        st_tags.append(st)
    prog.barrier("end")
    prog.on("q", lambda q: None, waits=["end"])



def _c(a, dt=np.float32):
    return np.ascontiguousarray(a, dtype=dt)


def kernel(L0_attn_hidden_states, L0_attn_q_proj_w, L0_attn_k_proj_w, L0_attn_v_proj_w,
           L0_attn_b_proj_w, L0_attn_q_conv_w, L0_attn_k_conv_w, L0_attn_v_conv_w,
           L0_attn_fir_w3, L0_attn_fir_w7, L0_attn_fir_w15, L0_attn_fir_w31,
           L0_attn_gate_w1, L0_attn_gate_b1, L0_attn_gate_w2, L0_attn_gate_b2,
           L0_attn_gate_log_temp, L0_attn_alpha_stat, L0_attn_o_norm_w, L0_attn_o_proj_w,
           L0_ffn_Wr, L0_ffn_br, L0_ffn_W1, L0_ffn_b1, L0_ffn_W2, L0_ffn_b2):
    x = _c(L0_attn_hidden_states)
    p_ = np.arange(128)
    ident = _c(np.eye(128))
    ones128 = np.ones((128, 128), np.float32)
    masklow = _c(-np.tril(ones128, -1))
    maskupd = _c(np.triu(ones128, 0))
    trils = _c(np.triu(ones128, 1))
    iota8 = _c(np.tile(np.arange(8), (128, 1)))
    iota256 = _c(np.tile(np.arange(256), (128, 1)))
    gtok = _c(128 * p_[:, None] + (np.arange(256)[None, :] // 2))
    wbz_base = np.concatenate([_c(L0_ffn_Wr), np.zeros((D, 1), np.float32)], 1)

    in_maps = []
    for c in range(8):
        b, h = c >> 2, c & 3
        q = c & 3
        hs = slice(h * DK, (h + 1) * DK)
        wbz = wbz_base.copy()
        wbz[:, 8] = np.asarray(L0_attn_b_proj_w)[h, :]
        g16 = np.arange(16)
        g64h = np.arange(64)
        hp, gp = g64h // 16, g64h % 16
        myrows = (c * 2048 + g16[None, :] * 128 + p_[:, None]).astype(np.uint32)
        stat_rows = ((4 * b + hp)[None, :] * 8192 + q * 2048 + gp[None, :] * 128
                     + p_[:, None]).astype(np.uint32)
        i64 = np.arange(64)
        probs_rows = (((b * 8192 + i64[None, :] * 128 + p_[:, None]) * 4) + h).astype(np.uint32)
        o_rows = (q * 16384 + (4 * b + hp)[None, :] * 2048 + gp[None, :] * 128
                  + p_[:, None]).astype(np.uint32)
        in_maps.append({
            "x0": x[0], "x1": x[1], "xmine": x[b],
            "xmy": _c(x[b, q * 2048:(q + 1) * 2048]),
            "wq_t": _c(np.asarray(L0_attn_q_proj_w)[hs, :].T),
            "wk_t": _c(np.asarray(L0_attn_k_proj_w)[hs, :].T),
            "wv_t": _c(np.asarray(L0_attn_v_proj_w)[hs, :].T),
            "wbz": _c(wbz),
            "convw": _c(np.concatenate([np.asarray(L0_attn_q_conv_w)[hs],
                                        np.asarray(L0_attn_k_conv_w)[hs],
                                        np.asarray(L0_attn_v_conv_w)[hs]], 0)),
            "firw": _c(np.concatenate([np.asarray(L0_attn_fir_w3)[hs],
                                       np.asarray(L0_attn_fir_w7)[hs],
                                       np.asarray(L0_attn_fir_w15)[hs],
                                       np.asarray(L0_attn_fir_w31)[hs]], 1)),
            "gw1t": _c(np.asarray(L0_attn_gate_w1).T),
            "gb1": _c(np.asarray(L0_attn_gate_b1).reshape(2048, 1)),
            "gw2t": _c(np.asarray(L0_attn_gate_w2).T),
            "gb2": _c(np.asarray(L0_attn_gate_b2).reshape(1, 24)),
            "alpha24": _c(np.tile(np.asarray(L0_attn_alpha_stat), 4).reshape(1, 24)),
            "gltemp": _c(np.asarray(L0_attn_gate_log_temp).reshape(1, 4)),
            "onw": _c(np.asarray(L0_attn_o_norm_w).reshape(1, DK)),
            "wot": _c(np.asarray(L0_attn_o_proj_w).T),
            "brr": _c(np.asarray(L0_ffn_br).reshape(1, 8)),
            "w1e": _c(np.asarray(L0_ffn_W1)[c]),
            "w2e": _c(np.asarray(L0_ffn_W2)[c]),
            "b1e": _c(np.asarray(L0_ffn_b1)[c].reshape(4096, 1)),
            "b2e": _c(np.asarray(L0_ffn_b2)[c].reshape(D, 1)),
            "c_ident": ident, "c_masklow": masklow, "c_maskupd": maskupd,
            "c_trils": trils,
            "c_ones_col": np.ones((128, 1), np.float32),
            "c_ones_row": np.ones((1, 128), np.float32),
            "c_iota8": iota8, "c_iota256": iota256, "c_gtok": gtok,
            "c_myexp": np.full((1, 1), float(c), np.float32),
            "c_dump": _c(CAPP + 128 * np.arange(256)[None, :] + p_[:, None]),
            "c_myrows": myrows, "c_stat_rows": stat_rows,
            "c_probs_rows": probs_rows, "c_o_rows": o_rows,
        })
    nc = build(None)
    res = run_bass_kernel_spmd(nc, in_maps, core_ids=list(range(8)))
    out = np.empty((2, 8192, 1024), np.float32)
    for c in range(8):
        b, q = c >> 2, c & 3
        out[b, q * 2048:(q + 1) * 2048, :] = res.results[c]["out_slice"]
    return out
